# version 02
# baseline (speedup 1.0000x reference)
import math
from contextlib import ExitStack

import numpy as np
import ml_dtypes

import concourse.bass as bass
import concourse.mybir as mybir
from concourse.bass_utils import run_bass_kernel_spmd

F32 = mybir.dt.float32
BF16 = mybir.dt.bfloat16
I32 = mybir.dt.int32
AF = mybir.ActivationFunctionType
ALU = mybir.AluOpType
AX = mybir.AxisListType
BF = ml_dtypes.bfloat16

NCORES = 8
B_PER_CORE = 2
B_FULL = 16
T_SEQ = 2048
NT = B_PER_CORE * T_SEQ
D = 1024
DT = D // 128
TC = 512
NEG = -1.0e30

ENGS = ("sync", "scalar", "vector", "tensor", "gpsimd")


_CONSTS = None


def make_consts():
    global _CONSTS
    if _CONSTS is not None:
        return _CONSTS
    c = {}
    c["ident"] = np.eye(128, dtype=np.float32)
    c["ident_bf"] = np.eye(128, dtype=BF)
    c["ones_bf"] = np.ones((128, 128), dtype=BF)
    c["ones_f32"] = np.ones((128, 128), dtype=np.float32)
    eq4 = np.zeros((128, 16), dtype=np.float32)
    for j in range(4):
        eq4[:, 4 * j:4 * j + 4] = (j < np.arange(4)[None, :])
    c["eqmask4"] = eq4
    c["ones32"] = np.full((32, 128), 1.0 / 32.0, dtype=np.float32)
    c["ones32_bf"] = np.full((32, 128), 1.0 / 32.0, dtype=BF)
    b64 = np.zeros((128, 128), dtype=np.float32)
    b64[:64, :64] = 1.0
    b64[64:, 64:] = 1.0
    c["bones64_bf"] = b64.astype(BF)
    p = np.arange(128)[:, None]
    f = np.arange(128)[None, :]
    c["madd_f_le_p"] = np.where(f <= p, 0.0, NEG).astype(np.float32)
    c["madd_f_lt_p"] = np.where(f < p, 0.0, NEG).astype(np.float32)
    c["madd_f_ge_p"] = np.where(f >= p, 0.0, NEG).astype(np.float32)
    c["madd_f_gt_p"] = np.where(f > p, 0.0, NEG).astype(np.float32)
    c["mul_f_le_p_bf"] = (f <= p).astype(BF)
    c["mul_f_ge_p_bf"] = (f >= p).astype(BF)
    c["mul_f_ge_p"] = (f >= p).astype(np.float32)
    eq = np.zeros((128, 256), dtype=np.float32)
    for j in range(16):
        eq[:, 16 * j:16 * j + 16] = (j < np.arange(16)[None, :])
    c["eqmask16"] = eq
    c["zeros64"] = np.zeros((64, 512), dtype=np.float32)
    c["zeros128b"] = np.zeros((128, 8), dtype=BF)
    hd = 64
    inv = (1.0 / 10000.0 ** (np.arange(0, hd, 2).astype(np.float64) / hd))
    t = np.arange(T_SEQ, dtype=np.float64)
    ang = np.outer(inv, t)
    cosv, sinv = np.cos(ang), np.sin(ang)
    c64 = np.concatenate([cosv, cosv], 0)
    s64 = np.concatenate([-sinv, sinv], 0)
    c["rope0_cos"] = np.tile(c64, (2, 1)).astype(np.float32)
    c["rope0_sin"] = np.tile(s64, (2, 1)).astype(np.float32)
    c["rope0_cos_bf"] = np.tile(c64, (2, 1)).astype(BF)
    c["rope0_sin_bf"] = np.tile(s64, (2, 1)).astype(BF)
    perm = np.zeros((128, 128), dtype=np.float32)
    for m in range(128):
        base, i = (m // 64) * 64, m % 64
        perm[base + (i + 32) % 64, m] = 1.0
    c["rope0_perm_bf"] = perm.astype(BF)
    d3 = 32
    inv3 = (10000.0 ** (-np.arange(0, d3, 2).astype(np.float64) / d3))
    ang3 = np.outer(inv3, t)
    cos3, sin3 = np.cos(ang3), np.sin(ang3)
    c32 = np.zeros((32, T_SEQ))
    s32 = np.zeros((32, T_SEQ))
    c32[0::2] = cos3
    c32[1::2] = cos3
    s32[0::2] = -sin3
    s32[1::2] = sin3
    c["rope3_cos"] = np.tile(c32, (4, 1)).astype(np.float32)
    c["rope3_sin"] = np.tile(s32, (4, 1)).astype(np.float32)
    c["rope3_cos_bf"] = np.tile(c32, (4, 1)).astype(BF)
    c["rope3_sin_bf"] = np.tile(s32, (4, 1)).astype(BF)
    perm3 = np.zeros((128, 128), dtype=np.float32)
    for m in range(128):
        perm3[m ^ 1, m] = 1.0
    c["rope3_perm_bf"] = perm3.astype(BF)
    _CONSTS = c
    return c


class Prog:
    def __init__(self, name="prog"):
        self.nc = bass.Bass("TRN2", target_bir_lowering=False)
        self.stack = ExitStack()
        self.q = {e: [] for e in ENGS}
        self.sems = {}
        self.cnt = {}
        self.n_ops = 0
        self._uid = 0

    def uid(self):
        self._uid += 1
        return self._uid

    def sem(self, name):
        if name not in self.sems:
            self.sems[name] = self.stack.enter_context(self.nc.semaphore(name))
            self.cnt[name] = 0
        return self.sems[name]

    def din(self, name, shape, dtype):
        return self.nc.dram_tensor(name, list(shape), dtype, kind="ExternalInput").ap()

    def dout(self, name, shape, dtype):
        return self.nc.dram_tensor(name, list(shape), dtype, kind="ExternalOutput").ap()

    def dtmp(self, name, shape, dtype):
        return self.nc.dram_tensor(name, list(shape), dtype, kind="Internal").ap()

    def sb(self, name, shape, dtype, stack=None):
        return (stack or self.stack).enter_context(
            self.nc.sbuf_tensor(f"{name}_{self.uid()}", list(shape), dtype))

    def ps(self, name, shape, dtype=F32, stack=None):
        shape = list(shape)
        elem = 4 if dtype == F32 else 2
        per_part = shape[1] * elem
        if per_part < 2048:
            shape = [shape[0], 2048 // elem]
        return (stack or self.stack).enter_context(
            self.nc.psum_tensor(f"{name}_{self.uid()}", list(shape), dtype))

    GUARD_THRESH = 512
    GUARD_ENGS = ("vector", "gpsimd", "scalar")

    def op(self, eng, meth, *args, waits=(), inc=None, incn=1, **kw):
        sems = self.sems
        waits = list(waits)
        if eng in self.GUARD_ENGS and meth != "dma_start":
            g = getattr(self, "_guard", None)
            if g is None:
                g = self._guard = {}
            prev = g.get(eng)
            if prev is not None:
                waits.append(prev)
            small = False
            if args and isinstance(args[0], bass.AP):
                try:
                    small = args[0].free_size() < self.GUARD_THRESH
                except Exception:
                    small = False
            if small:
                if inc is None:
                    inc = {"vector": "V", "scalar": "S", "gpsimd": "G"}[eng]
        for s, _ in waits:
            self.sem(s)
        if inc:
            self.sem(inc)
        wl = tuple((s, c) for (s, c) in waits if c > 0)

        def f(e, meth=meth, args=args, kw=kw, wl=wl, inc=inc, incn=incn):
            for s, c in wl:
                e.wait_ge(sems[s], c)
            r = getattr(e, meth)(*args, **kw)
            if inc:
                r.then_inc(sems[inc], incn)
        self.q[eng].append(f)
        self.n_ops += 1
        if inc:
            self.cnt[inc] += incn
            if eng in self.GUARD_ENGS and meth != "dma_start" and args                     and isinstance(args[0], bass.AP):
                try:
                    if args[0].free_size() < self.GUARD_THRESH:
                        self._guard[eng] = (inc, self.cnt[inc])
                except Exception:
                    pass
            return self.cnt[inc]
        return None

    def wait(self, eng, *waits):
        sems = self.sems
        for s, _ in waits:
            self.sem(s)
        wl = tuple((s, c) for (s, c) in waits if c > 0)

        def f(e, wl=wl):
            for s, c in wl:
                e.wait_ge(sems[s], c)
        self.q[eng].append(f)

    def dma(self, queue, out, in_, sem=None, waits=(), **kw):
        if sem is None:
            sem = {"sync": "LD", "scalar": "LD2", "gpsimd": "LDG"}[queue]
        return self.op(queue, "dma_start", waits=waits, inc=sem, incn=16,
                       out=out, in_=in_, **kw)

    def snap(self, sem):
        self.sem(sem)
        return self.cnt[sem]

    def finish(self):
        nc = self.nc
        with nc.Block() as block:
            for eng in ENGS:
                ops = self.q[eng]
                if not ops:
                    continue
                getattr(block, eng)(lambda e, ops=tuple(ops): [f(e) for f in ops])
        return nc


TAPS = []
_WTD3_ECHO = []
DEBUG_ECHO = []


def tap(p, name, ap, waits, block_eng="vector"):
    if name not in TAPS:
        return
    out = p.dout("tap_" + name, list(ap.shape), ap.dtype)
    stc = p.dma("gpsimd", out[:], ap, sem="STG", waits=list(waits))
    p.wait("sync", ("STG", p.snap("STG")))
    if block_eng:
        g = getattr(p, "_guard", None)
        if g is None:
            g = p._guard = {}
        g[block_eng] = ("STG", stc)


def merge_counts(*dicts):
    out = {}
    for d in dicts:
        for k, v in d.items():
            out[k] = max(out.get(k, 0), v)
    return out


def as_waits(d):
    return [(k, v) for k, v in d.items() if v > 0]


NFSCAL = 48


class Core:
    def __init__(self, p: Prog,
                 const_names=("ident_bf", "ones_bf", "ones32_bf")):
        self.p = p
        self.res = [p.sb(f"res{i}", [128, NT], BF16) for i in range(DT)]
        self.const_aps = {}
        self.const_sb = {}
        cs = make_consts()
        for n in const_names:
            a = cs[n]
            dt = BF16 if a.dtype == BF else F32
            self.const_aps[n] = p.din("c_" + n, list(a.shape), dt)
            self.const_sb[n] = p.sb("cs_" + n, list(a.shape), dt)
        self._cdata = {("c_" + n): np.asarray(cs[n]) for n in const_names}
        self.fs_vals = []
        self.fs_sb = p.sb("fscal", [128, NFSCAL], F32)
        self.const_aps["fscal"] = p.din("c_fscal", [128, NFSCAL], F32)
        self.const_sb["fscal"] = self.fs_sb
        self.res_ready = {}

    def fscal(self, v):
        v = float(v)
        if v not in self.fs_vals:
            assert len(self.fs_vals) < NFSCAL
            self.fs_vals.append(v)
        return self.fs_sb[:, self.fs_vals.index(v):self.fs_vals.index(v) + 1]

    def get_const_data(self):
        d = dict(self._cdata)
        a = np.zeros((128, NFSCAL), np.float32)
        a[:, :len(self.fs_vals)] = np.asarray(self.fs_vals, np.float32)[None, :]
        d["c_fscal"] = a
        return d

    def load_consts(self):
        p = self.p
        for n, ap in self.const_aps.items():
            p.dma("sync", self.const_sb[n][:], ap[:])
        return {"LD": p.snap("LD")}


def emit_load_res(p, core, x_ap, fp32_src):
    if fp32_src:
        for i in range(DT):
            p.dma("gpsimd", core.res[i][:], x_ap[128 * i:128 * (i + 1), :])
        core.res_ready = {"LDG": p.snap("LDG")}
    else:
        for i in range(DT):
            p.dma("sync", core.res[i][:], x_ap[128 * i:128 * (i + 1), :])
        core.res_ready = {"LD": p.snap("LD")}
    return core.res_ready


def emit_store_res(p, core, out_ap, fp32_dst):
    q, sem = ("gpsimd", "STG") if fp32_dst else ("sync", "ST")
    w = as_waits(core.res_ready)
    for i in range(DT):
        p.dma(q, out_ap[128 * i:128 * (i + 1), :], core.res[i][:], sem=sem, waits=w)
    return {sem: p.snap(sem)}


class LNPool:

    def __init__(self, p, stack, tag, tc=TC, src_f32=False):
        self.tc = tc
        self.src_f32 = src_f32
        sdt = F32 if src_f32 else BF16
        self.psA = p.ps(f"lnA{tag}", [128, tc], F32, stack=stack)
        self.psB = p.ps(f"lnB{tag}", [128, tc], F32, stack=stack)
        self.sq = [p.sb(f"lnsq{tag}{i}", [128, tc], sdt, stack=stack) for i in range(DT)]
        self.mf = p.sb(f"lnmf{tag}", [128, tc], F32, stack=stack)
        self.qf = p.sb(f"lnqf{tag}", [128, tc], F32, stack=stack)
        self.vf = p.sb(f"lnvf{tag}", [128, tc], F32, stack=stack)
        self.m_bf = p.sb(f"lnmb{tag}", [128, tc], BF16, stack=stack)
        self.r_bf = p.sb(f"lnrb{tag}", [128, tc], BF16, stack=stack)
        self.t_bf = self.sq
        self.prev = {}


def emit_ln(p, core, pool, src_tiles, t0, tn, dst_tiles=None, dst_t0=None,
            res_add=False, res_t0=None, src_ready=(), dst_war=(), eps=1e-5,
            tap_tag=None):
    ones = core.const_sb["ones_f32" if pool.src_f32 else "ones_bf"]
    sl = slice(t0, t0 + tn)
    d0 = dst_t0 if dst_t0 is not None else t0
    dsl = slice(d0, d0 + tn)
    rsl = slice(res_t0 if res_t0 is not None else t0,
                (res_t0 if res_t0 is not None else t0) + tn)
    pr = pool.prev
    src_ready = list(src_ready)
    sq_w = src_ready + as_waits({k: v for k, v in pr.items()
                                 if k in ("T", "V")})
    g1 = None
    for i in range(DT):
        g1 = p.op("gpsimd", "tensor_tensor", pool.sq[i][:, 0:tn],
                  src_tiles[i][:, sl], src_tiles[i][:, sl], ALU.mult,
                  waits=sq_w if i == 0 else (), inc="G" if i == DT - 1 else None)
    mm_w = src_ready + as_waits({k: v for k, v in pr.items() if k == "Vst"})
    for i in range(DT):
        p.op("tensor", "matmul", pool.psA[:, 0:tn], ones[:], src_tiles[i][:, sl],
             start=(i == 0), stop=(i == DT - 1), waits=mm_w if i == 0 else ())
    for i in range(DT):
        last = i == DT - 1
        p.op("tensor", "matmul", pool.psB[:, 0:tn], ones[:], pool.sq[i][:, 0:tn],
             start=(i == 0), stop=last,
             waits=[("G", g1)] if i == 0 else (), inc="T" if last else None)
    tdone = p.cnt["T"]
    if tap_tag:
        tap(p, tap_tag + "_sq0", pool.sq[0][:, 0:tn], [("G", g1)],
            block_eng="gpsimd")
    p.sem("Vst")
    war_mr = as_waits({k: v for k, v in pr.items() if k in ("S", "Gap")})
    p.op("vector", "tensor_scalar_mul", pool.mf[:, 0:tn], pool.psA[:, 0:tn],
         1.0 / D, waits=[("T", tdone)] + war_mr)
    p.op("vector", "scalar_tensor_tensor", pool.vf[:, 0:tn], pool.mf[:, 0:tn],
         -1.0, pool.mf[:, 0:tn], ALU.mult, ALU.mult)
    p.op("vector", "tensor_scalar", pool.qf[:, 0:tn], pool.psB[:, 0:tn],
         1.0 / D, None, ALU.mult, inc="Vst")
    vst = p.cnt["Vst"]
    if tap_tag:
        tap(p, tap_tag + "_mf", pool.mf[:, 0:tn], [("Vst", vst)])
        tap(p, tap_tag + "_qf", pool.qf[:, 0:tn], [("Vst", vst)])
    if pool.src_f32:
        p.op("vector", "tensor_add", pool.vf[:, 0:tn], pool.vf[:, 0:tn],
             pool.qf[:, 0:tn], inc="V")
    else:
        p.op("vector", "tensor_add", pool.vf[:, 0:tn], pool.vf[:, 0:tn],
             pool.qf[:, 0:tn])
        p.op("vector", "tensor_copy", pool.m_bf[:, 0:tn], pool.mf[:, 0:tn], inc="V")
    vm = p.cnt["V"]
    if tap_tag:
        tap(p, tap_tag + "_vf", pool.vf[:, 0:tn], [("V", vm)])
    p.op("scalar", "activation", pool.vf[:, 0:tn], pool.vf[:, 0:tn], AF.Sqrt,
         bias=core.fscal(eps), waits=[("V", vm)], inc="S")
    sdone = p.cnt["S"]
    if pool.src_f32:
        p.op("vector", "reciprocal", pool.qf[:, 0:tn], pool.vf[:, 0:tn],
             waits=[("S", sdone)], inc="V")
    else:
        p.op("vector", "reciprocal", pool.qf[:, 0:tn], pool.vf[:, 0:tn],
             waits=[("S", sdone)])
        p.op("vector", "tensor_copy", pool.r_bf[:, 0:tn], pool.qf[:, 0:tn], inc="V")
    vr = p.cnt["V"]
    if tap_tag:
        tap(p, tap_tag + "_rbf",
            (pool.qf if pool.src_f32 else pool.r_bf)[:, 0:tn], [("V", vr)])
        tap(p, tap_tag + "_mbf",
            (pool.mf if pool.src_f32 else pool.m_bf)[:, 0:tn], [("V", vr)])
    dst_war = list(dst_war)
    m_op = pool.mf if pool.src_f32 else pool.m_bf
    r_op = pool.qf if pool.src_f32 else pool.r_bf
    if pool.src_f32:
        pass
    vlast = glast = None
    for i in range(DT):
        eng = "vector" if i % 2 == 0 else "gpsimd"
        w = []
        if i < 2:
            w += [("V", vr), ("T", tdone)]
            w += dst_war
        p.op(eng, "tensor_sub", pool.t_bf[i][:, 0:tn], src_tiles[i][:, sl],
             m_op[:, 0:tn], waits=w)
        if res_add:
            if pool.src_f32:
                cast_t = pool.m_bf if eng == "vector" else pool.r_bf
                p.op(eng, "tensor_tensor", cast_t[:, 0:tn],
                     pool.t_bf[i][:, 0:tn], r_op[:, 0:tn], ALU.mult)
                r = p.op(eng, "tensor_add", core.res[i][:, rsl], core.res[i][:, rsl],
                         cast_t[:, 0:tn],
                         inc="V" if eng == "vector" else "Gap")
            else:
                p.op(eng, "tensor_mul", pool.t_bf[i][:, 0:tn], pool.t_bf[i][:, 0:tn],
                     r_op[:, 0:tn])
                r = p.op(eng, "tensor_add", core.res[i][:, rsl], core.res[i][:, rsl],
                         pool.t_bf[i][:, 0:tn],
                         inc="V" if eng == "vector" else "Gap")
        else:
            r = p.op(eng, "tensor_mul", dst_tiles[i][:, dsl], pool.t_bf[i][:, 0:tn],
                     r_op[:, 0:tn],
                     inc="V" if eng == "vector" else "Gap")
        if eng == "vector":
            vlast = r
        else:
            glast = r
    done = {"V": vlast, "Gap": glast}
    pool.prev = {"T": tdone, "Vst": vst, "S": sdone, "Gap": glast, "V": vlast}
    return done


class WScr:

    def __init__(self, p, name, w_ap, O, I):
        self.scr = p.dtmp(f"scr_{name}_{p.uid()}", [O, I], BF16)
        p.dma("gpsimd", self.scr[:], w_ap[:])
        self.done = p.snap("LDG")
        self.O, self.I = O, I


def load_wT(p, wscr, stack, o0=0, osz=None, sem="LD", tag="w", bufs=None,
            extra_waits=()):
    osz = osz or wscr.O
    I = wscr.I
    tiles = bufs or [p.sb(f"wT_{tag}{i}", [128, osz], BF16, stack=stack)
                     for i in range(I // 128)]
    w = [("LDG", wscr.done)] + list(extra_waits)
    for i in range(I // 128):
        p.dma("sync", tiles[i][:, 0:osz],
              wscr.scr[o0:o0 + osz, 128 * i:128 * (i + 1)],
              transpose=True, waits=w if i == 0 else list(extra_waits), sem=sem)
    return tiles, {sem: p.snap(sem)}


def load_w_nat(p, name, w_ap, I, O, stack, sem="LDG", bufs=None, extra_waits=()):
    tiles = bufs or [p.sb(f"wN_{name}{i}", [128, O], BF16, stack=stack)
                     for i in range(I // 128)]
    for i in range(I // 128):
        p.dma("gpsimd", tiles[i][:, 0:O], w_ap[128 * i:128 * (i + 1), :],
              sem=sem, waits=list(extra_waits))
    return tiles, {sem: p.snap(sem)}


def emit_l3f(p, core, w_ap):
    with ExitStack() as st:
        wscr = WScr(p, "l3f", w_ap, D, D)
        wT, wld = load_wT(p, wscr, st, tag="l3f")
        wsT = [p.sb(f"l3f_ws{i}", [128, D], BF16, stack=st) for i in range(DT)]
        for i in range(DT):
            p.op("scalar", "activation", wsT[i][:], wT[i][:], AF.Sign,
                 waits=as_waits(wld) if i == 0 else (), inc="S")
        sdone = {"S": p.cnt["S"]}

        lnp = LNPool(p, st, "A")
        lnp2 = LNPool(p, st, "B")
        xh = [p.sb(f"l3f_xh{i}", [128, TC], BF16, stack=st) for i in range(DT)]
        yb = [p.sb(f"l3f_y{i}", [128, TC], BF16, stack=st) for i in range(DT)]
        psy = [p.ps(f"l3f_ps{i}", [128, TC], F32, stack=st) for i in range(4)]
        bank_free = {i: {} for i in range(4)}
        xh_war = {}
        yb_war = {}
        res_r = as_waits(core.res_ready)
        done = {}
        for tc in range(NT // TC):
            t0 = tc * TC
            d1 = emit_ln(p, core, lnp, core.res, t0, TC, dst_tiles=xh, dst_t0=0,
                         src_ready=res_r if tc == 0 else (),
                         dst_war=as_waits(xh_war))
            xw = as_waits(merge_counts(d1, sdone))
            last_mm = 0
            evs = {}
            for o in range(DT):
                bank = psy[o % 4]
                for i in range(DT):
                    w = []
                    if i == 0:
                        w = xw + as_waits(bank_free[o % 4])
                    last = i == DT - 1
                    p.op("tensor", "matmul", bank[:],
                         wsT[i][:, 128 * o:128 * (o + 1)], xh[i][:, 0:TC],
                         start=(i == 0), stop=last, waits=w,
                         inc="T" if last else None)
                tmm = p.cnt["T"]
                last_mm = tmm
                vv = p.op("scalar", "activation", yb[o][:, 0:TC], bank[:],
                          AF.Copy, waits=[("T", tmm)] + as_waits(yb_war), inc="S")
                bank_free[o % 4] = {"S": vv}
                evs[o] = vv
            xh_war = {"T": last_mm}
            d2 = emit_ln(p, core, lnp2, yb, 0, TC, res_add=True, res_t0=t0,
                         src_ready=[("S", evs[DT - 1])])
            yb_war = {"T": lnp2.prev["T"], "Gap": d2["Gap"], "V": d2["V"]}
            done = d2
        core.res_ready = merge_counts(core.res_ready, done)



def load_scale_cols(p, core, vec_ap, n, stack, tag, ps_shared=None, extra_waits=()):
    nt = n // 128
    nr = max(nt, 32)
    rows = p.sb(f"scv_{tag}", [nr, 128], F32, stack=stack)
    ps = ps_shared if ps_shared is not None else p.ps(
        f"scp_{tag}", [128, max(nt, 1)], F32, stack=stack)
    cols = p.sb(f"scc_{tag}", [128, max(nt, 1)], F32, stack=stack)
    kk = min(nr, 32)
    vz = 0
    if nt < kk:
        p.op("vector", "tensor_copy", rows[0:kk, :],
             core.const_sb["zeros64"][0:kk, 0:128],
             waits=list(extra_waits) + as_waits(core.res_ready))
        vz = p.cnt["V"]
    ld = p.dma("sync", rows[0:nt, :], vec_ap.rearrange("(a b) -> a b", b=128),
               waits=list(extra_waits) + ([("V", vz)] if vz else []))
    p.op("tensor", "matmul", ps[:, 0:nt], rows[0:kk, :] if nt <= 32 else rows[:],
         core.const_sb["ident"][0:(kk if nt <= 32 else nr), 0:nt],
         start=True, stop=True,
         waits=[("LD", ld)] + list(extra_waits), inc="T")
    p.op("vector", "tensor_copy", cols[:], ps[:, 0:nt],
         waits=[("T", p.cnt["T"])], inc="V")
    return cols, {"V": p.cnt["V"]}


def mm_group(p, psum_ap, pairs, waits_first=(), inc_last="T"):
    n = len(pairs)
    for i, (l, r) in enumerate(pairs):
        p.op("tensor", "matmul", psum_ap, l, r,
             start=(i == 0), stop=(i == n - 1) and inc_last is not None,
             waits=waits_first if i == 0 else (),
             inc=inc_last if (i == n - 1 and inc_last) else None)
    return p.cnt[inc_last] if inc_last else None


class WTd:

    def __init__(self, p, name, w_ap, O, I, staging, staging_free):
        self.O, self.I = O, I
        scr = p.dtmp(f"scr_{name}_{p.uid()}", [O, I], BF16)
        p.dma("gpsimd", scr[:], w_ap[:])
        cast_done = p.snap("LDG")
        self.wtd = p.dtmp(f"wtd_{name}_{p.uid()}", [I, O], BF16)
        ns = len(staging)
        ocap = staging[0].shape[1]
        k = 0
        for i in range(I // 128):
            for o0 in range(0, O, ocap):
                osz = min(ocap, O - o0)
                sbuf = staging[k % ns]
                w = [("LDG", cast_done)] + as_waits(staging_free[k % ns])
                ld = p.dma("sync", sbuf[:, 0:osz],
                           scr[o0:o0 + osz, 128 * i:128 * (i + 1)],
                           transpose=True, waits=w)
                stc = p.dma("scalar",
                            self.wtd[128 * i:128 * (i + 1), o0:o0 + osz],
                            sbuf[:, 0:osz], sem="LD2", waits=[("LD", ld)])
                staging_free[k % ns] = {"LD2": stc}
                k += 1
        self.ready = {"LD2": p.snap("LD2")}


def load_wslice(p, wtd: WTd, o0, osz, buf, col0=0, sem="LD", extra_waits=(),
                i0=0, nI=None):
    if nI is None:
        nI = wtd.I // 128
    w = as_waits(wtd.ready) + list(extra_waits)
    G = 8
    for g in range(0, nI, G):
        n = min(G, nI - g)
        out_ap = buf[:, col0 + g * osz: col0 + (g + n) * osz].rearrange(
            "p (i o) -> p i o", o=osz)
        in_ap = wtd.wtd[128 * (i0 + g):128 * (i0 + g + n), o0:o0 + osz].rearrange(
            "(i p) o -> p i o", p=128)
        p.dma("sync", out_ap, in_ap, sem=sem, waits=w if g == 0 else ())
    return {sem: p.snap(sem)}


def emit_l1f(p, core, aps):
    HID = 3072
    with ExitStack() as st:
        stg_stack = ExitStack()
        staging = [p.sb(f"l1f_stg{i}", [128, HID], BF16, stack=stg_stack)
                   for i in range(2)]
        stg_free = {0: {}, 1: {}}
        wtd_g = WTd(p, "l1fg", aps["L1_ffn_w_gate"], HID, D, staging, stg_free)
        wtd_u = WTd(p, "l1fu", aps["L1_ffn_w_up"], HID, D, staging, stg_free)
        stg_rel = [("LD2", p.snap("LD2"))]
        stg_stack.close()
        scr_d = WScr(p, "l1fd", aps["L1_ffn_w_down"], D, HID)
        wdT, wd_ld = load_wT(p, scr_d, st, tag="l1fd", extra_waits=stg_rel)
        lnp = LNPool(p, st, "A")
        xh = [p.sb(f"l1f_xh{i}", [128, TC], BF16, stack=st) for i in range(DT)]
        yb = [p.sb(f"l1f_y{i}", [128, TC], BF16, stack=st) for i in range(DT)]
        h_all = [p.sb(f"l1f_h{i}", [128, TC], BF16, stack=st) for i in range(HID // 128)]
        wgb = p.sb("l1f_wgb", [128, DT * 1024], BF16, stack=st)
        wub = p.sb("l1f_wub", [128, DT * 1024], BF16, stack=st)
        gps = [p.ps(f"l1f_g{i}", [128, TC], F32, stack=st) for i in range(2)]
        ups = [p.ps(f"l1f_u{i}", [128, TC], F32, stack=st) for i in range(2)]
        yps = [p.ps(f"l1f_yp{i}", [128, TC], F32, stack=st) for i in range(2)]
        gbuf = [p.sb(f"l1f_gb{i}", [128, TC], BF16, stack=st) for i in range(2)]
        s127 = core.fscal(1.0 / 127.0)
        res_r = as_waits(core.res_ready)
        wg_free = {}
        bank_free = {0: {}, 1: {}}
        ybank_free = {0: {}, 1: {}}
        yb_war = {}
        xh_war = {}
        done = {}

        def wg_slice(i, ot):
            return wgb[:, 1024 * i + 128 * ot:1024 * i + 128 * (ot + 1)]

        def wu_slice(i, ot):
            return wub[:, 1024 * i + 128 * ot:1024 * i + 128 * (ot + 1)]

        for tc in range(NT // TC):
            t0 = tc * TC
            d1 = emit_ln(p, core, lnp, core.res, t0, TC, dst_tiles=xh, dst_t0=0,
                         src_ready=(res_r + stg_rel) if tc == 0 else (),
                         dst_war=as_waits(xh_war) + (stg_rel if tc == 0 else []))
            xw = as_waits(d1)
            last_mm = 0
            hv = None
            for sl in range(3):
                g_ld = load_wslice(p, wtd_g, 1024 * sl, 1024, wgb,
                                   extra_waits=as_waits(wg_free) + stg_rel)
                u_ld = load_wslice(p, wtd_u, 1024 * sl, 1024, wub,
                                   extra_waits=as_waits(wg_free) + stg_rel)
                wld = as_waits(merge_counts(g_ld, u_ld))
                for ot in range(8):
                    og = sl * 8 + ot
                    bk = ot % 2
                    tg = mm_group(p, gps[bk][:],
                                  [(wg_slice(i, ot), xh[i][:, 0:TC]) for i in range(DT)],
                                  waits_first=xw + wld + as_waits(bank_free[bk]))
                    sg = p.op("scalar", "activation", gbuf[bk][:], gps[bk][:],
                              AF.Silu, scale=s127, waits=[("T", tg)], inc="S")
                    tu = mm_group(p, ups[bk][:],
                                  [(wu_slice(i, ot), xh[i][:, 0:TC]) for i in range(DT)],
                                  waits_first=as_waits(bank_free[bk]))
                    su = p.op("scalar", "activation", h_all[og][:], ups[bk][:],
                              AF.Copy, scale=s127, waits=[("T", tu)], inc="S")
                    hv = p.op("vector", "tensor_mul", h_all[og][:], h_all[og][:],
                              gbuf[bk][:], waits=[("S", su)], inc="V")
                    bank_free[bk] = {"V": hv, "S": su}
                    last_mm = tu
                wg_free = {"T": last_mm}
            ydone = {}
            for ot in range(DT):
                bk = ot % 2
                ty = mm_group(p, yps[bk][:],
                              [(wdT[f][:, 128 * ot:128 * (ot + 1)], h_all[f][:])
                               for f in range(HID // 128)],
                              waits_first=[("V", hv)] + as_waits(wd_ld)
                                          + as_waits(ybank_free[bk]))
                sy = p.op("scalar", "activation", yb[ot][:], yps[bk][:],
                          AF.Copy, scale=s127,
                          waits=[("T", ty)] + as_waits(yb_war), inc="S")
                ybank_free[bk] = {"S": sy}
                ydone = {"S": sy, "T": ty}
            xh_war = {"T": ydone["T"]}
            d2 = emit_ln(p, core, lnp, yb, 0, TC, res_add=True, res_t0=t0,
                         src_ready=as_waits(ydone))
            yb_war = {"T": lnp.prev["T"], "Gap": d2["Gap"], "V": d2["V"]}
            done = d2
        core.res_ready = merge_counts(core.res_ready, done)


def emit_l2f(p, core, aps):
    HID = 4096
    HT = HID // 128
    with ExitStack() as st:
        stg_stack = ExitStack()
        staging = [p.sb(f"l2f_stg{i}", [128, HID], BF16, stack=stg_stack)
                   for i in range(2)]
        stg_free = {0: {}, 1: {}}
        wtd1 = WTd(p, "l2f1", aps["L2_ffn_w1_w"], HID, D, staging, stg_free)
        wtd2 = WTd(p, "l2f2", aps["L2_ffn_w2_w"], HID, D, staging, stg_free)
        wtdh1 = WTd(p, "l2fh1", aps["L2_ffn_h1_w"], HID, HID, staging, stg_free)
        wtdh2 = WTd(p, "l2fh2", aps["L2_ffn_h2_w"], HID, HID, staging, stg_free)
        wtd3 = WTd(p, "l2f3", aps["L2_ffn_w3_w"], D, HID, staging, stg_free)
        stg_rel = [("LD2", p.snap("LD2"))]
        if "l2f_wtd3end" in TAPS:
            _WTD3_ECHO.append(wtd3)
        stg_stack.close()
        scps = p.ps("l2f_scl", [128, 32], F32, stack=st)
        s1c, s1d = load_scale_cols(p, core, aps["L2_ffn_w1_s"], HID, st, "s1",
                                   scps, extra_waits=stg_rel)
        s2c, s2d = load_scale_cols(p, core, aps["L2_ffn_w2_s"], HID, st, "s2", scps,
                                   extra_waits=as_waits(s1d))
        h1c, h1d = load_scale_cols(p, core, aps["L2_ffn_h1_s"], HID, st, "h1", scps,
                                   extra_waits=as_waits(s2d))
        h2c, h2d = load_scale_cols(p, core, aps["L2_ffn_h2_s"], HID, st, "h2", scps,
                                   extra_waits=as_waits(h1d))
        s3c, s3d = load_scale_cols(p, core, aps["L2_ffn_w3_s"], D, st, "s3", scps,
                                   extra_waits=as_waits(h2d))
        scl_d = as_waits(merge_counts(s1d, s2d, h1d, h2d, s3d))
        tap(p, "l2f_s1c", s1c[:], scl_d)
        tap(p, "l2f_s3c", s3c[:], scl_d)

        lnp = LNPool(p, st, "A")
        xh = [p.sb(f"l2f_xh{i}", [128, TC], BF16, stack=st) for i in range(DT)]
        ha = [p.sb(f"l2f_ha{i}", [128, TC], BF16, stack=st) for i in range(HT)]
        hb = [p.sb(f"l2f_hb{i}", [128, TC], BF16, stack=st) for i in range(HT)]
        yb = hb[0:DT]
        w12 = p.sb("l2f_w12", [128, DT * 512 * 2], BF16, stack=st)
        whb = p.sb("l2f_whb", [128, HT * 512], BF16, stack=st)
        ps = [p.ps(f"l2f_ps{i}", [128, TC], F32, stack=st) for i in range(4)]
        bank_free = {i: {} for i in range(4)}
        whb_free = {}
        w12_free = {}
        xh_war = {}
        yb_war = {}
        res_r = as_waits(core.res_ready)
        done = {}
        for tc in range(NT // TC):
            t0 = tc * TC
            d1 = emit_ln(p, core, lnp, core.res, t0, TC, dst_tiles=xh, dst_t0=0,
                         src_ready=(res_r + stg_rel) if tc == 0 else (),
                         dst_war=as_waits(xh_war) + (stg_rel if tc == 0 else []))
            xw = as_waits(d1)
            if tc == 0 and "l2f_xh0" in TAPS:
                xo = p.dout("tap_l2f_xh0", [128, TC], BF16)
                p.dma("gpsimd", xo[:], xh[0][:, 0:TC], sem="STG", waits=xw)
            hv = None
            last_mm = 0
            for sl in range(DT):
                a_ld = load_wslice(p, wtd1, TC * sl, TC, w12, col0=0,
                                   extra_waits=as_waits(w12_free) + stg_rel)
                b_ld = load_wslice(p, wtd2, TC * sl, TC, w12, col0=DT * 512,
                                   extra_waits=as_waits(w12_free) + stg_rel)
                wld = as_waits(merge_counts(a_ld, b_ld))
                for ot in range(4):
                    og = sl * 4 + ot
                    bka, bkb = ot % 2, 2 + ot % 2
                    tg = mm_group(p, ps[bka][:],
                                  [(w12[:, 512 * i + 128 * ot:512 * i + 128 * (ot + 1)],
                                    xh[i][:, 0:TC]) for i in range(DT)],
                                  waits_first=xw + wld + scl_d + as_waits(bank_free[bka]))
                    sg = p.op("scalar", "activation", ha[og][:], ps[bka][:], AF.Silu,
                              scale=s1c[:, og:og + 1], waits=[("T", tg)], inc="S")
                    tu = mm_group(p, ps[bkb][:],
                                  [(w12[:, DT * 512 + 512 * i + 128 * ot:
                                        DT * 512 + 512 * i + 128 * (ot + 1)],
                                    xh[i][:, 0:TC]) for i in range(DT)],
                                  waits_first=as_waits(bank_free[bkb]))
                    su = p.op("scalar", "activation", hb[og][:], ps[bkb][:], AF.Copy,
                              scale=s2c[:, og:og + 1],
                              waits=[("T", tu)] +
                                    (as_waits(yb_war) if og == 0 else []),
                              inc="S")
                    hv = p.op("vector", "tensor_mul", ha[og][:], ha[og][:], hb[og][:],
                              waits=[("S", sg), ("S", su)], inc="V")
                    bank_free[bka] = {"V": hv}
                    bank_free[bkb] = {"V": hv}
                    last_mm = tu
                w12_free = {"T": last_mm}
            hw = [("V", hv)]
            if tc == 0:
                tap(p, "l2f_ha1", ha[0][:], hw)
                tap(p, "l2f_ha1b", ha[31][:], hw)
            for (wtdH, srcb, dstb, scol) in ((wtdh1, ha, hb, h1c), (wtdh2, hb, ha, h2c)):
                sv = None
                lm = 0
                for sl in range(DT):
                    w_ld = load_wslice(p, wtdH, TC * sl, TC, whb,
                                       extra_waits=as_waits(whb_free) + stg_rel)
                    wld2 = as_waits(w_ld)
                    for ot in range(4):
                        og = sl * 4 + ot
                        bk = ot % 2
                        tg = mm_group(p, ps[bk][:],
                                      [(whb[:, 512 * i + 128 * ot:512 * i + 128 * (ot + 1)],
                                        srcb[i][:, 0:TC]) for i in range(HT)],
                                      waits_first=hw + wld2 + as_waits(bank_free[bk]))
                        sv = p.op("scalar", "activation", dstb[og][:], ps[bk][:], AF.Gelu,
                                  scale=scol[:, og:og + 1], waits=[("T", tg)], inc="S")
                        bank_free[bk] = {"S": sv}
                        lm = tg
                    whb_free = {"T": lm}
                hw = [("S", sv)]
            if tc == 0:
                tap(p, "l2f_g2" if dstb is hb else "l2f_g3", dstb[17][:], hw,
                    block_eng="scalar")
            ydone = {}
            lm = 0
            for half in range(2):
                w_ld = load_wslice(p, wtd3, TC * half, TC, whb,
                                   extra_waits=as_waits(whb_free) + stg_rel)
                wld3 = as_waits(w_ld)
                for ot in range(4):
                    og = half * 4 + ot
                    bk = ot % 2
                    ty = mm_group(p, ps[bk][:],
                                  [(whb[:, 512 * i + 128 * ot:512 * i + 128 * (ot + 1)],
                                    ha[i][:, 0:TC]) for i in range(HT)],
                                  waits_first=hw + wld3 + as_waits(bank_free[bk]))
                    sy = p.op("scalar", "activation", yb[og][:], ps[bk][:], AF.Copy,
                              scale=s3c[:, og:og + 1],
                              waits=[("T", ty)] + as_waits(yb_war), inc="S")
                    bank_free[bk] = {"S": sy}
                    lm = ty
                    ydone = {"S": sy, "T": ty}
                whb_free = {"T": lm}
            xh_war = {"T": last_mm}
            if tc == 0:
                tap(p, "l2f_yb", yb[3][:], as_waits(ydone), block_eng="scalar")
            if "l2f_pertc" in TAPS:
                yo = p.dout(f"tap_l2f_y{tc}", [128, TC], BF16)
                p.dma("gpsimd", yo[:], yb[3][:, 0:TC], sem="STG",
                      waits=as_waits(ydone))
                yo0 = p.dout(f"tap_l2f_ya{tc}", [128, TC], BF16)
                p.dma("gpsimd", yo0[:], yb[0][:, 0:TC], sem="STG",
                      waits=as_waits(ydone))
                yo1 = p.dout(f"tap_l2f_yc{tc}", [128, TC], BF16)
                p.dma("gpsimd", yo1[:], yb[1][:, 0:TC], sem="STG",
                      waits=as_waits(ydone))
            d2 = emit_ln(p, core, lnp, yb, 0, TC, res_add=True, res_t0=t0,
                         src_ready=as_waits(ydone),
                         tap_tag="l2d2" if tc == 0 else None)
            if "l2f_pertc" in TAPS:
                ro = p.dout(f"tap_l2f_r{tc}", [128, TC], BF16)
                p.dma("gpsimd", ro[:], core.res[0][:, t0:t0 + TC], sem="STG",
                      waits=as_waits(d2))
            yb_war = {"T": lnp.prev["T"], "Gap": d2["Gap"], "V": d2["V"]}
            done = d2
        core.res_ready = merge_counts(core.res_ready, done)



def emit_l0f(p, core, aps):
    E, Fd = 16, 512
    QT = 1024
    with ExitStack() as st:
        stg_stack = ExitStack()
        staging = [p.sb(f"l0f_stg{i}", [128, 8192], BF16, stack=stg_stack)
                   for i in range(2)]
        stg_free = {0: {}, 1: {}}
        w1_flat = aps["L0_ffn_w1_e"].rearrange("e f h -> (e f) h")
        w3_flat = aps["L0_ffn_w3_e"].rearrange("e f h -> (e f) h")
        w2_flat = aps["L0_ffn_w2_e"].rearrange("e h f -> (e h) f")
        wtd1 = WTd(p, "l0f1", w1_flat, E * Fd, D, staging, stg_free)
        wtd3 = WTd(p, "l0f3", w3_flat, E * Fd, D, staging, stg_free)
        wtd2 = WTd(p, "l0f2", w2_flat, E * D, Fd, staging, stg_free)
        wtdS1 = WTd(p, "l0fs1", aps["L0_ffn_w1_s"], D, D, staging, stg_free)
        wtdS3 = WTd(p, "l0fs3", aps["L0_ffn_w3_s"], D, D, staging, stg_free)
        wtdS2 = WTd(p, "l0fs2", aps["L0_ffn_w2_s"], D, D, staging, stg_free)
        stg_rel = [("LD2", p.snap("LD2"))]
        stg_stack.close()

        xhat_d = p.dtmp("l0f_xhat", [D, NT], BF16)
        comb_d = p.dtmp("l0f_combd", [NT, 128], BF16)
        comb = p.sb("l0f_comb", [128, 32 * 16], F32, stack=st)
        comb_fm = p.sb("l0f_combfm", [128, NT], BF16, stack=st)
        wg_nat = p.sb("l0f_wgn", [16, D], BF16, stack=st)
        wgT = [p.sb(f"l0f_wgt{i}", [128, 16], BF16, stack=st) for i in range(DT)]
        ldg = p.dma("gpsimd", wg_nat[:], aps["L0_ffn_w_gate"][:],
                    waits=stg_rel)
        with ExitStack() as stA:
            lnp = LNPool(p, stA, "A")
            xh = [p.sb(f"l0f_xh{i}", [128, TC], BF16, stack=stA) for i in range(DT)]
            ps_sc = p.ps("l0f_psc", [128, 128], F32, stack=stA)
            ps_wgt = p.ps("l0f_pwg", [128, 16], BF16, stack=stA)
            sc = p.sb("l0f_sc", [128, 16], F32, stack=stA)
            g4 = p.sb("l0f_g4", [128, 16], F32, stack=stA)
            t4a = p.sb("l0f_t4a", [128, 4], F32, stack=stA)
            t4b = p.sb("l0f_t4b", [128, 4], F32, stack=stA)
            gs = p.sb("l0f_gs", [128, 4], F32, stack=stA)
            rk4 = p.sb("l0f_rk4", [128, 4], F32, stack=stA)
            eqv = p.sb("l0f_eqv", [128, 16], F32, stack=stA)
            rk = p.sb("l0f_rk", [128, 16], F32, stack=stA)
            msk = p.sb("l0f_msk", [128, 16], F32, stack=stA)
            dcol = p.sb("l0f_dcol", [128, 1], F32, stack=stA)
            eqm16 = core.const_sb["eqmask16"]
            eqm4 = core.const_sb["eqmask4"]
            for i in range(DT):
                p.op("tensor", "transpose", ps_wgt[:, 0:16],
                     wg_nat[0:16, 128 * i:128 * (i + 1)],
                     core.const_sb["ident_bf"][0:16, 0:16],
                     waits=[("LDG", ldg)] if i == 0
                           else [("S", p.cnt["S"])], inc="T")
                p.op("scalar", "activation", wgT[i][:], ps_wgt[:, 0:16],
                     AF.Copy, waits=[("T", p.cnt["T"])], inc="S")
            wgT_done = p.cnt["S"]
            res_r = as_waits(core.res_ready)
            xh_war = {}
            sc_prev = {}
            for tc in range(NT // TC):
                t0 = tc * TC
                d1 = emit_ln(p, core, lnp, core.res, t0, TC, dst_tiles=xh, dst_t0=0,
                             src_ready=(res_r + stg_rel) if tc == 0 else (),
                             dst_war=as_waits(xh_war) + (stg_rel if tc == 0 else []))
                xw = as_waits(d1)
                xst = None
                for i in range(DT):
                    xst = p.dma("scalar", xhat_d[128 * i:128 * (i + 1), t0:t0 + TC],
                                xh[i][:, 0:TC], sem="LD2", waits=xw if i == 0 else ())
                for tt in range(4):
                    gt = tc * 4 + tt
                    ccols = slice(16 * gt, 16 * gt + 16)
                    tmm = mm_group(p, ps_sc[0:128, 0:16],
                                   [(xh[i][:, 128 * tt:128 * (tt + 1)], wgT[i][:])
                                    for i in range(DT)],
                                   waits_first=xw + [("S", wgT_done)]
                                               + as_waits(sc_prev))
                    p.op("scalar", "activation", sc[:], ps_sc[0:128, 0:16],
                         AF.Sigmoid, waits=[("T", tmm)], inc="S")
                    sw = [("S", p.cnt["S"])]
                    v = lambda *a, **k: p.op("vector", *a, **k)
                    _sb = sc[:, 0:16]
                    def _sv(j):
                        return bass.AP(_sb.tensor, _sb.offset + j,
                                       [list(_sb.ap[0]), [4, 4]])
                    v("tensor_copy", g4[:, 0:4], _sv(0), waits=sw)
                    v("tensor_copy", g4[:, 4:8], _sv(1))
                    v("tensor_copy", g4[:, 8:12], _sv(2))
                    v("tensor_copy", g4[:, 12:16], _sv(3))
                    ga, gb, gc, gd = (g4[:, 0:4], g4[:, 4:8], g4[:, 8:12], g4[:, 12:16])
                    v("tensor_max", t4a[:], ga, gb)
                    v("tensor_max", t4b[:], gc, gd)
                    v("tensor_add", t4a[:], t4a[:], t4b[:])
                    v("tensor_add", gs[:], ga, gb)
                    v("tensor_max", gs[:], gs[:], t4a[:])
                    v("tensor_add", t4b[:], gc, gd)
                    v("tensor_max", gs[:], gs[:], t4b[:])
                    v("memset", rk4[:], 0.0)
                    for j in range(4):
                        v("scalar_tensor_tensor", rk4[:], gs[:], gs[:, j:j + 1],
                          rk4[:], ALU.is_lt, ALU.add)
                        v("tensor_single_scalar", t4a[:], gs[:], gs[:, j:j + 1],
                          ALU.is_equal)
                        v("tensor_mul", t4a[:], t4a[:], eqm4[:, 4 * j:4 * j + 4])
                        v("tensor_add", rk4[:], rk4[:], t4a[:])
                    v("tensor_single_scalar", rk4[:], rk4[:], 2.0, ALU.is_lt)
                    _b = rk4[:, 0:4]
                    allow_rep = bass.AP(_b.tensor, _b.offset,
                                        [list(_b.ap[0]), [1, 4], [0, 4]])
                    v("tensor_copy", eqv[:], allow_rep)
                    v("tensor_single_scalar", rk[:], eqv[:], 1.0, ALU.subtract)
                    v("tensor_single_scalar", rk[:], rk[:], 1e30, ALU.mult)
                    v("tensor_mul", msk[:], sc[:], eqv[:])
                    v("tensor_add", msk[:], msk[:], rk[:])
                    v("memset", rk[:], 0.0)
                    for j in range(16):
                        v("scalar_tensor_tensor", rk[:], msk[:], msk[:, j:j + 1],
                          rk[:], ALU.is_lt, ALU.add)
                        v("tensor_single_scalar", eqv[:], msk[:], msk[:, j:j + 1],
                          ALU.is_equal)
                        v("tensor_mul", eqv[:], eqv[:], eqm16[:, 16 * j:16 * j + 16])
                        v("tensor_add", rk[:], rk[:], eqv[:])
                    if gt == 0:
                        tap(p, "r_gs", gs[:], [("GDv", p.cnt.get("GDv", 0))])
                        tap(p, "r_rk4", rk4[:], [("GDv", p.cnt.get("GDv", 0))])
                        tap(p, "r_msk", msk[:], [("GDv", p.cnt.get("GDv", 0))])
                        tap(p, "r_rk", rk[:], [("GDv", p.cnt.get("GDv", 0))])
                        tap(p, "r_sc", sc[:], [("GDv", p.cnt.get("GDv", 0))])
                    v("tensor_single_scalar", rk[:], rk[:], 4.0, ALU.is_lt)
                    v("tensor_mul", sc[:], sc[:], rk[:])
                    v("reduce_sum", dcol[:], sc[:], AX.X)
                    v("tensor_single_scalar", dcol[:], dcol[:], 1e-6, ALU.add)
                    v("reciprocal", dcol[:], dcol[:])
                    v("tensor_scalar_mul", comb[:, ccols], sc[:], dcol[:, 0:1],
                      inc="V")
                    vc = p.cnt["V"]
                    p.dma("gpsimd", comb_d[128 * gt:128 * (gt + 1), 0:16],
                          comb[:, ccols], sem="LDG", waits=[("V", vc)])
                    sc_prev = {"V": vc}
                xh_war = {"T": p.cnt["T"], "LD2": xst}
            p.dma("sync", comb_fm[:], comb_d[:],
                  transpose=True, waits=[("LDG", p.snap("LDG"))])
            tap(p, "combfm", comb_fm[0:16, :], [("LD", p.snap("LD"))])
            phaseA_done = merge_counts(
                {"LD": p.snap("LD"), "LD2": p.snap("LD2"), "S": p.cnt["S"],
                 "V": p.cnt["V"], "T": p.cnt["T"], "Gap": p.cnt.get("Gap", 0)})
        pw = as_waits(phaseA_done)
        y_acc = [p.sb(f"l0f_ya{i}", [128, QT], F32, stack=st) for i in range(DT)]
        xhs = p.sb("l0f_xhs", [128, DT * TC], BF16, stack=st)
        w1b = p.sb("l0f_w1b", [128, DT * TC], BF16, stack=st)
        w3b = p.sb("l0f_w3b", [128, DT * TC], BF16, stack=st)
        w2b = p.sb("l0f_w2b", [128, 4 * 1024], BF16, stack=st)
        g_sb = [p.sb(f"l0f_g{i}", [128, TC], BF16, stack=st) for i in range(4)]
        u_sb = [p.sb(f"l0f_u{i}", [128, TC], BF16, stack=st) for i in range(4)]
        cb = p.sb("l0f_cb", [128, TC], BF16, stack=st)
        shuf = p.sb("l0f_shuf", [32, TC], BF16, stack=st)
        yb_war = {}
        done = {}
        passes = [("r", e) for e in range(E)] + [("s", 0), ("s", 1)]
        with ExitStack() as stB:
            lnp2 = LNPool(p, stB, "B", src_f32=True)
            psg = [p.ps(f"l0f_pg{i}", [128, TC], F32, stack=stB) for i in range(2)]
            psu = [p.ps(f"l0f_pu{i}", [128, TC], F32, stack=stB) for i in range(2)]
            psy = [p.ps(f"l0f_py{i}", [128, TC], F32, stack=stB) for i in range(2)]
            psc = psu[0]
            gW = {}
            xhs_war = {}
            cb_war = {}
            gb_free = {0: {}, 1: {}}
            yb_free = {0: {}, 1: {}}
            ya_ready = {}
            for q in range(NT // QT):
                qw = pw if q == 0 else []
                zv = None
                for i in range(DT):
                    zv = p.op("vector", "memset", y_acc[i][:], 0.0,
                              waits=(as_waits(ya_ready) + qw) if i == 0 else (),
                              inc="V" if i == DT - 1 else None)
                ya_w = [("V", zv)]
                for (kind, e) in passes:
                    if kind == "r":
                        l1 = load_wslice(p, wtd1, Fd * e, Fd, w1b,
                                         extra_waits=as_waits(gW) + qw)
                        l3 = load_wslice(p, wtd3, Fd * e, Fd, w3b,
                                         extra_waits=as_waits(gW))
                        l2 = load_wslice(p, wtd2, D * e, 1024, w2b,
                                         extra_waits=as_waits(gW))
                    else:
                        wt1, wt3, wt2 = wtdS1, wtdS3, wtdS2
                        l1 = load_wslice(p, wt1, Fd * e, Fd, w1b,
                                         extra_waits=as_waits(gW) + qw)
                        l3 = load_wslice(p, wt3, Fd * e, Fd, w3b,
                                         extra_waits=as_waits(gW))
                        l2 = load_wslice(p, wt2, 0, 1024, w2b,
                                         extra_waits=as_waits(gW),
                                         i0=4 * e, nI=4)
                    wld = as_waits(merge_counts(l1, l3, l2))
                    for sp in range(QT // TC):
                        t0 = q * QT + sp * TC
                        xld = p.dma("gpsimd",
                                    xhs.rearrange("p (i t) -> p i t", t=TC),
                                    xhat_d[:, t0:t0 + TC].rearrange(
                                        "(i p) t -> p i t", p=128),
                                    sem="LDG",
                                    waits=as_waits(xhs_war) + (pw if q == 0 else []))
                        xw2 = [("LDG", xld)]
                        if kind == "r":
                            sv = p.op("vector", "stream_shuffle", shuf[:],
                                      comb_fm[0:32, t0:t0 + TC], [e] * 32,
                                      waits=as_waits(cb_war) + (pw if q == 0 else []),
                                      inc="V")
                            tcb = p.op("tensor", "matmul", psc[:],
                                       core.const_sb["ones32_bf"][:], shuf[:],
                                       start=True, stop=True,
                                       waits=[("V", sv)] + as_waits(cb_war)
                                             + as_waits(gb_free[0]), inc="T")
                            scb = p.op("scalar", "activation", cb[:], psc[:],
                                       AF.Copy, waits=[("T", tcb)], inc="S")
                            cbw = [("S", scb)]
                        else:
                            cbw = []
                        if q == 0 and sp == 0:
                            tap(p, f"xhs_{kind}{e}", xhs[:], [("LDG", xld)])
                        hv = None
                        for ft in range(4):
                            bk = ft % 2
                            tg = mm_group(p, psg[bk][:],
                                          [(w1b[:, TC * i + 128 * ft:TC * i + 128 * (ft + 1)],
                                            xhs[:, TC * i:TC * i + TC])
                                           for i in range(DT)],
                                          waits_first=xw2 + wld + as_waits(gb_free[bk]))
                            sg = p.op("scalar", "activation", g_sb[ft][:], psg[bk][:],
                                      AF.Silu, waits=[("T", tg)], inc="S")
                            tu = mm_group(p, psu[bk][:],
                                          [(w3b[:, TC * i + 128 * ft:TC * i + 128 * (ft + 1)],
                                            xhs[:, TC * i:TC * i + TC])
                                           for i in range(DT)],
                                          waits_first=cbw if (ft == 0 and kind == "r") else ())
                            su = p.op("scalar", "activation", u_sb[ft][:], psu[bk][:],
                                      AF.Copy, waits=[("T", tu)], inc="S")
                            hv = p.op("vector", "tensor_mul", g_sb[ft][:], g_sb[ft][:],
                                      u_sb[ft][:], waits=[("S", su), ("S", sg)])
                            if kind == "r":
                                hv = p.op("vector", "tensor_mul", g_sb[ft][:],
                                          g_sb[ft][:], cb[:],
                                          waits=cbw if ft == 0 else (), inc="V")
                            else:
                                hv = p.op("vector", "tensor_mul", g_sb[ft][:],
                                          g_sb[ft][:], g_sb[ft][:], ALU_BYPASS_NOOP=None)                                     if False else hv
                                hv = p.op("vector", "engine_nop", inc="V")
                            gb_free[bk] = {"V": hv}
                        hw2 = [("V", hv)]
                        if q == 0 and sp == 0:
                            tap(p, f"h_{kind}{e}", g_sb[0][:], [("V", p.cnt["V"])])
                            if kind == "r":
                                tap(p, f"cb_{kind}{e}", cb[:], [("S", p.cnt["S"])])
                        for ot in range(DT):
                            bk = ot % 2
                            ty = mm_group(p, psy[bk][:],
                                          [(w2b[:, 1024 * f + 128 * ot:1024 * f + 128 * (ot + 1)],
                                            g_sb[f][:]) for f in range(4)],
                                          waits_first=hw2 + wld + as_waits(yb_free[bk]))
                            av = p.op("vector", "tensor_add",
                                      y_acc[ot][:, TC * sp:TC * sp + TC],
                                      y_acc[ot][:, TC * sp:TC * sp + TC],
                                      psy[bk][:],
                                      waits=[("T", ty)] + (ya_w if ot < 2 else []),
                                      inc="V")
                            yb_free[bk] = {"V": av}
                        xhs_war = {"T": p.cnt["T"]}
                        if kind == "r":
                            cb_war = {"T": tcb, "S": scb, "V": p.cnt["V"]}
                    gW = {"T": p.cnt["T"]}
                ydone = {"V": p.cnt["V"]}
                if q == 0:
                    for i in range(DT):
                        tap(p, f"yacc{i}", y_acc[i][:], as_waits(ydone))
                for sp in range(QT // TC):
                    t0 = q * QT + sp * TC
                    d2 = emit_ln(p, core, lnp2,
                                 [y_acc[i] for i in range(DT)],
                                 sp * TC, TC, res_add=True, res_t0=t0,
                                 src_ready=as_waits(ydone) + as_waits(yb_war))
                    yb_war = {"T": lnp2.prev["T"], "Gap": d2["Gap"], "V": d2["V"]}
                    done = d2
                ya_ready = merge_counts({"T": lnp2.prev["T"]},
                                        {k: v for k, v in done.items()})
        core.res_ready = merge_counts(core.res_ready, done)


def emit_l3a(p, core, aps):
    DL, DH, DHR = 256, 64, 32
    NH = 16
    SQ = 1.0 / math.sqrt(96.0)
    with ExitStack() as st:
        wDQ, d1 = load_w_nat(p, "wdq", aps["L3_attn_W_DQ"], D, DL, st)
        wDKV, d2 = load_w_nat(p, "wdkv", aps["L3_attn_W_DKV"], D, DL, st)
        wUQ, d3 = load_w_nat(p, "wuq", aps["L3_attn_W_UQ"], DL, D, st)
        wQR, d4 = load_w_nat(p, "wqr", aps["L3_attn_W_QR"], DL, 512, st)
        wUK, d5 = load_w_nat(p, "wuk", aps["L3_attn_W_UK"], DL, D, st)
        wUV, d6 = load_w_nat(p, "wuv", aps["L3_attn_W_UV"], DL, D, st)
        wKR, d7 = load_w_nat(p, "wkr", aps["L3_attn_W_KR"], D, 32, st)
        wO, d8 = load_w_nat(p, "wo", aps["L3_attn_W_O"], D, D, st)
        wld = as_waits(merge_counts(d1, d2, d3, d4, d5, d6, d7, d8))
        cosb = core.const_sb["rope3_cos_bf"]
        sinb = core.const_sb["rope3_sin_bf"]
        perm = core.const_sb["rope3_perm_bf"]
        mge = core.const_sb["madd_f_ge_p"]
        onesb = core.const_sb["ones_bf"]
        identb = core.const_sb["ident_bf"]

        lnp = LNPool(p, st, "A", tc=256)
        lnp2 = lnp
        xh = [p.sb(f"l3a_xh{i}", [128, TC], BF16, stack=st) for i in range(DT)]
        cq = [p.sb(f"l3a_cq{i}", [128, T_SEQ], BF16, stack=st) for i in range(2)]
        ckv = [p.sb(f"l3a_ckv{i}", [128, T_SEQ], BF16, stack=st) for i in range(2)]
        kr = p.sb("l3a_kr", [32, T_SEQ], BF16, stack=st)
        krt = p.sb("l3a_krt", [32, TC], BF16, stack=st)
        yfm = [p.sb(f"l3a_y{i}", [128, T_SEQ], BF16, stack=st) for i in range(DT)]
        yb = xh
        qc = p.sb("l3a_qc", [64, T_SEQ], BF16, stack=st)
        qr = p.sb("l3a_qr", [32, T_SEQ], BF16, stack=st)
        qrt = krt
        kc = p.sb("l3a_kc", [64, T_SEQ], BF16, stack=st)
        vtm = p.sb("l3a_vtm", [128, NH * DH], BF16, stack=st)
        ssb = [p.sb("l3a_s0", [128, TC], F32, stack=st)] * 2
        ptb = [p.sb(f"l3a_pt{i}", [128, TC], BF16, stack=st) for i in range(2)]
        dsb = ssb[0]

        psP = [p.ps(f"l3a_pp{i}", [128, TC], F32, stack=st) for i in range(2)]
        psS = [p.ps(f"l3a_ps{i}", [128, TC], F32, stack=st) for i in range(2)]
        psO = psP[0]
        psD = psP[1]

        res_r = as_waits(core.res_ready)
        done = {}
        yb_war = {}
        xh_war = {}
        krt_war = {}
        qrt_war = krt_war
        ssb_war = {}
        pf = {0: {}, 1: {}}
        vt_f = {0: {}, 1: {}}
        sA_f = {0: {}, 1: {}}
        o_f = {}
        d_f = {}
        NC_B = T_SEQ // TC
        for b in range(B_PER_CORE):
            cq_done = {}
            for tc in range(NC_B):
                t0 = b * T_SEQ + tc * TC
                lc = tc * TC
                for _h in range(2):
                    d1_ = emit_ln(p, core, lnp, core.res, t0 + 256 * _h, 256,
                                  dst_tiles=xh, dst_t0=256 * _h,
                                  src_ready=res_r if (b == 0 and tc == 0 and _h == 0) else (),
                                  dst_war=as_waits(xh_war) if _h == 0 else ())
                xw = as_waits(d1_) + wld
                for (dst, wt, not_) in ((cq, wDQ, 2), (ckv, wDKV, 2)):
                    for ot in range(not_):
                        bk = ot % 2
                        tmm = mm_group(p, psP[bk][:],
                                       [(wt[i][:, 128 * ot:128 * (ot + 1)],
                                         xh[i][:, 0:TC]) for i in range(DT)],
                                       waits_first=xw + as_waits(pf[bk]))
                        sev = p.op("scalar", "activation", dst[ot][:, lc:lc + TC],
                                   psP[bk][:], AF.Copy, waits=[("T", tmm)], inc="S")
                        pf[bk] = {"S": sev}
                tmm = mm_group(p, psP[0][0:32, 0:TC],
                               [(wKR[i][:, 0:32], xh[i][:, 0:TC]) for i in range(DT)],
                               waits_first=xw + as_waits(pf[0]))
                sev = p.op("scalar", "activation", krt[0:32, 0:TC],
                           psP[0][0:32, 0:TC], AF.Copy,
                           waits=[("T", tmm)] + as_waits(krt_war), inc="S")
                pf[0] = {"S": sev}
                tpm = p.op("tensor", "matmul", psP[1][0:32, 0:TC],
                           perm[0:32, 0:32], krt[0:32, 0:TC], start=True, stop=True,
                           waits=[("S", sev)] + as_waits(pf[1]), inc="T")
                spm = p.op("scalar", "activation", kr[0:32, lc:lc + TC],
                           psP[1][0:32, 0:TC], AF.Copy, waits=[("T", tpm)], inc="S")
                pf[1] = {"S": spm}
                v1 = p.op("vector", "tensor_mul", krt[0:32, 0:TC], krt[0:32, 0:TC],
                          cosb[0:32, lc:lc + TC], waits=[("S", spm)])
                v1 = p.op("vector", "tensor_mul", kr[0:32, lc:lc + TC],
                          kr[0:32, lc:lc + TC], sinb[0:32, lc:lc + TC])
                v1 = p.op("vector", "tensor_add", kr[0:32, lc:lc + TC],
                          kr[0:32, lc:lc + TC], krt[0:32, 0:TC], inc="V")
                krt_war = merge_counts(krt_war, {"V": v1})
                cq_done = {"V": v1, "S": p.cnt["S"]}
            xh_war = {"T": p.cnt["T"]}
            cqw = as_waits(cq_done)
            for h in range(NH):
                hw = cqw
                qcd = {}
                for tc in range(NC_B):
                    lc = tc * TC
                    t1 = mm_group(p, psP[0][0:64, 0:TC],
                                  [(wUQ[i][:, 64 * h:64 * h + 64], cq[i][:, lc:lc + TC])
                                   for i in range(2)],
                                  waits_first=hw + as_waits(pf[0]))
                    s1 = p.op("scalar", "activation", qc[0:64, lc:lc + TC],
                              psP[0][0:64, 0:TC], AF.Copy, waits=[("T", t1)], inc="S")
                    pf[0] = {"S": s1}
                    t2 = mm_group(p, psP[1][0:64, 0:TC],
                                  [(wUK[i][:, 64 * h:64 * h + 64], ckv[i][:, lc:lc + TC])
                                   for i in range(2)],
                                  waits_first=as_waits(pf[1]))
                    s2 = p.op("scalar", "activation", kc[0:64, lc:lc + TC],
                              psP[1][0:64, 0:TC], AF.Copy, waits=[("T", t2)], inc="S")
                    pf[1] = {"S": s2}
                    t4 = mm_group(p, psP[1][0:32, 0:TC],
                                  [(wQR[i][:, 32 * h:32 * h + 32], cq[i][:, lc:lc + TC])
                                   for i in range(2)],
                                  waits_first=[("S", s2)])
                    s4 = p.op("scalar", "activation", qrt[0:32, 0:TC],
                              psP[1][0:32, 0:TC], AF.Copy,
                              waits=[("T", t4)] + as_waits(krt_war), inc="S")
                    t5 = p.op("tensor", "matmul", psP[1][32:64, 0:TC],
                              perm[0:32, 0:32], qrt[0:32, 0:TC],
                              start=True, stop=True, waits=[("S", s4)], inc="T")
                    s5 = p.op("scalar", "activation", qr[0:32, lc:lc + TC],
                              psP[1][32:64, 0:TC], AF.Copy, waits=[("T", t5)], inc="S")
                    pf[1] = {"S": s5}
                    v1 = p.op("vector", "tensor_mul", qrt[0:32, 0:TC],
                              qrt[0:32, 0:TC], cosb[0:32, lc:lc + TC],
                              waits=[("S", s5)])
                    v1 = p.op("vector", "tensor_mul", qr[0:32, lc:lc + TC],
                              qr[0:32, lc:lc + TC], sinb[0:32, lc:lc + TC])
                    v1 = p.op("vector", "tensor_add", qr[0:32, lc:lc + TC],
                              qr[0:32, lc:lc + TC], qrt[0:32, 0:TC], inc="V")
                    qrt_war = krt_war = merge_counts(krt_war, {"V": v1})
                    qcd = {"V": v1, "S": p.cnt["S"]}
                hpw = as_waits(qcd)
                vt_done = {}
                for s in range(NH):
                    bk = s % 2
                    tt = mm_group(p, psP[bk][:, 0:64],
                                  [(ckv[i][:, 128 * s:128 * (s + 1)],
                                    wUV[i][:, 64 * h:64 * h + 64]) for i in range(2)],
                                  waits_first=hpw + as_waits(pf[bk]))
                    sv = p.op("scalar", "activation", vtm[:, 64 * s:64 * s + 64],
                              psP[bk][:, 0:64], AF.Copy, waits=[("T", tt)], inc="S")
                    pf[bk] = {"S": sv}
                    vt_done = {"S": sv}
                vtw = as_waits(vt_done)
                o_f = merge_counts(o_f, pf[0])
                d_f = merge_counts(d_f, pf[1])
                for tc in range(NC_B):
                    lc = tc * TC
                    nst = 4 * (tc + 1)
                    ofirst = True
                    for s in range(nst):
                        bk = s % 2
                        tS = p.op("tensor", "matmul", psS[bk][:],
                                  kc[0:64, 128 * s:128 * (s + 1)], qc[0:64, lc:lc + TC],
                                  start=True, stop=False,
                                  waits=hpw + as_waits(sA_f[bk]))
                        tS = p.op("tensor", "matmul", psS[bk][:],
                                  kr[0:32, 128 * s:128 * (s + 1)], qr[0:32, lc:lc + TC],
                                  start=False, stop=True, inc="T")
                        vc = p.op("vector", "tensor_scalar", ssb[bk][:], psS[bk][:],
                                  -80.0, 80.0, ALU.max, ALU.min,
                                  waits=[("T", tS)] + as_waits(ssb_war), inc="V")
                        rel = s - 4 * tc
                        if rel >= 0:
                            vc = p.op("vector", "tensor_add",
                                      ssb[bk][:, 128 * rel:128 * (rel + 1)],
                                      ssb[bk][:, 128 * rel:128 * (rel + 1)],
                                      mge[:], inc="V")
                        se = p.op("scalar", "activation", ptb[bk][:], ssb[bk][:],
                                  AF.Exp, scale=core.fscal(SQ),
                                  waits=[("V", vc)], inc="S")
                        ssb_war = {"S": se}
                        pw = [("S", se)]
                        if rel > 0:
                            mz = p.op("vector", "memset", ptb[bk][:, 0:128 * rel],
                                      0.0, waits=[("S", se)], inc="V")
                            pw = [("V", mz)]
                        tO = p.op("tensor", "matmul", psO[0:64, :],
                                  vtm[:, 64 * s:64 * s + 64], ptb[bk][:],
                                  start=ofirst, stop=(s == nst - 1),
                                  waits=pw + (as_waits(o_f) if ofirst else []),
                                  inc="T" if s == nst - 1 else None)
                        tD = p.op("tensor", "matmul", psD[:],
                                  onesb[:], ptb[bk][:],
                                  start=ofirst, stop=(s == nst - 1),
                                  waits=(as_waits(d_f) if ofirst else []),
                                  inc="T" if s == nst - 1 else None)
                        sA_f[bk] = {"S": se} if rel <= 0 else {"V": pw[0][1] if pw[0][0] == "V" else 0, "S": se}
                        ofirst = False
                    tpv = p.cnt["T"]
                    vr = p.op("vector", "reciprocal", dsb[:], psD[:],
                              waits=[("T", tpv)], inc="V")
                    d_f = {"V": vr}
                    ti, ro = h // 2, 64 * (h % 2)
                    vo = p.op("vector", "tensor_tensor",
                              yfm[ti][ro:ro + 64, lc:lc + TC],
                              psO[0:64, :], dsb[0:64, :], ALU.mult,
                              waits=[("V", vr)], inc="V")
                    o_f = {"V": vo}
                    sA_f[0] = merge_counts(sA_f[0], {"V": vo})
                    ssb_war = merge_counts(ssb_war, {"V": vo})
                    pf[0] = merge_counts(pf[0], {"V": vo})
                    pf[1] = merge_counts(pf[1], {"V": vr})
            yw = as_waits(o_f) + as_waits(d_f)
            for tc in range(NC_B):
                t0 = b * T_SEQ + tc * TC
                lc = tc * TC
                ydone = {}
                for ot in range(DT):
                    bk = ot % 2
                    tmm = mm_group(p, psS[bk][:],
                                   [(wO[i][:, 128 * ot:128 * (ot + 1)],
                                     yfm[i][:, lc:lc + TC]) for i in range(DT)],
                                   waits_first=yw + as_waits(sA_f[bk]))
                    sy = p.op("scalar", "activation", yb[ot][:, 0:TC], psS[bk][:],
                              AF.Copy, waits=[("T", tmm)] + as_waits(yb_war), inc="S")
                    sA_f[bk] = {"S": sy}
                    ydone = {"S": sy, "T": tmm}
                for _h in range(2):
                    d2_ = emit_ln(p, core, lnp2, yb, 256 * _h, 256, res_add=True,
                                  res_t0=t0 + 256 * _h,
                                  src_ready=as_waits(ydone) if _h == 0 else ())
                yb_war = {"T": lnp2.prev["T"], "Gap": d2_["Gap"], "V": d2_["V"]}
                done = d2_
        core.res_ready = merge_counts(core.res_ready, done)


def emit_l0a(p, core, aps):
    NH, HD = 16, 64
    with ExitStack() as st:
        wd = {}
        for nm in ("Wq", "Wk", "Wv"):
            scr = p.dtmp(f"l0a_{nm}", [D, D], BF16)
            p.dma("gpsimd", scr[:], aps["L0_attn_" + nm][:])
            wd[nm] = scr
        cast_done = {"LDG": p.snap("LDG")}
        wo, wo_d = load_w_nat(p, "l0a_wo", aps["L0_attn_Wo"], D, D, st)
        nwq = p.sb("l0a_nwq", [64, 1], F32, stack=st)
        nwk = p.sb("l0a_nwk", [64, 1], F32, stack=st)
        p.dma("sync", nwq[:], aps["L0_attn_q_norm_w"].rearrange("(a b) -> a b", b=1))
        p.dma("sync", nwk[:], aps["L0_attn_k_norm_w"].rearrange("(a b) -> a b", b=1))
        nw_d = {"LD": p.snap("LD")}

        cosb = core.const_sb["rope0_cos_bf"]
        sinb = core.const_sb["rope0_sin_bf"]
        perm = core.const_sb["rope0_perm_bf"]
        mle = core.const_sb["madd_f_le_p"]
        onesb = core.const_sb["ones_bf"]
        identb = core.const_sb["ident_bf"]
        epsc = core.fscal(1e-6)

        lnp = LNPool(p, st, "A", tc=256)
        xh = [p.sb(f"l0a_xh{i}", [128, T_SEQ], BF16, stack=st) for i in range(DT)]
        yfm = [p.sb(f"l0a_y{i}", [128, T_SEQ], BF16, stack=st) for i in range(DT)]
        yb = [p.sb(f"l0a_yb{i}", [128, TC], BF16, stack=st) for i in range(DT)]
        wq4 = p.sb("l0a_wq4", [128, DT * 256], BF16, stack=st)
        wk4 = p.sb("l0a_wk4", [128, DT * 256], BF16, stack=st)
        wv4 = p.sb("l0a_wv4", [128, DT * 256], BF16, stack=st)
        qh = p.sb("l0a_qh", [64, T_SEQ], BF16, stack=st)
        kh = p.sb("l0a_kh", [64, T_SEQ], BF16, stack=st)
        qsw = p.sb("l0a_qsw", [64, TC], BF16, stack=st)
        sqb = p.sb("l0a_sqb", [64, TC], BF16, stack=st)
        rsb = p.sb("l0a_rsb", [128, TC], F32, stack=st)
        rsbb = p.sb("l0a_rsbb", [128, TC], BF16, stack=st)
        vtm = p.sb("l0a_vtm", [128, NH * 64], BF16, stack=st)
        psb = p.sb("l0a_psb", [128, T_SEQ], BF16, stack=st)
        ptb = p.sb("l0a_ptb", [128, TC], BF16, stack=st)
        mparts = p.sb("l0a_mp", [128, 8], F32, stack=st)
        dparts = p.sb("l0a_dp", [128, 8], F32, stack=st)
        mcol = p.sb("l0a_mc", [128, 1], F32, stack=st)
        dcol = p.sb("l0a_dc", [128, 1], F32, stack=st)

        psS = [p.ps(f"l0a_s{i}", [128, TC], F32, stack=st) for i in range(4)]
        psT = p.ps("l0a_t", [128, TC], F32, stack=st)
        psV = p.ps("l0a_v", [128, TC], F32, stack=st)

        res_r = as_waits(core.res_ready)
        done = {}
        yb_war = {}
        xh_war = {}
        psf = {i: {} for i in range(4)}
        ptf = {}
        pvf = {}
        small_war = {}
        yfm_war = {}
        psb_war = {}
        mcol_war = {}
        for b in range(B_PER_CORE):
            for tc in range(T_SEQ // 256):
                t0 = b * T_SEQ + tc * 256
                d1_ = emit_ln(p, core, lnp, core.res, t0, 256, dst_tiles=xh,
                              dst_t0=tc * 256,
                              src_ready=res_r if (b == 0 and tc == 0) else (),
                              dst_war=as_waits(xh_war) if tc == 0 else ())
            xw = as_waits(d1_)
            for h in range(NH):
                hg, hi = h // 4, h % 4
                if hi == 0:
                    for (buf, nm) in ((wq4, "Wq"), (wk4, "Wk"), (wv4, "Wv")):
                        p.dma("sync",
                              buf[:].rearrange("p (i c) -> p i c", c=256),
                              wd[nm][:, 256 * hg:256 * (hg + 1)].rearrange(
                                  "(i p) c -> p i c", p=128),
                              waits=as_waits(cast_done) + [("T", p.cnt["T"])])
                    wg_d = [("LD", p.snap("LD"))]
                for (dst, w4, nw) in ((qh, wq4, nwq), (kh, wk4, nwk)):
                    for cc in range(T_SEQ // TC):
                        lc = cc * TC
                        t1 = mm_group(p, psS[0][0:64, :],
                                      [(w4[:, 256 * i + 64 * hi:256 * i + 64 * hi + 64],
                                        xh[i][:, lc:lc + TC]) for i in range(DT)],
                                      waits_first=xw + wg_d + as_waits(psf[0])
                                                  + as_waits(nw_d))
                        s1 = p.op("scalar", "activation", dst[0:64, lc:lc + TC],
                                  psS[0][0:64, :], AF.Copy, waits=[("T", t1)], inc="S")
                        psf[0] = {"S": s1}
                        v1 = p.op("vector", "tensor_mul", sqb[0:64, 0:TC],
                                  dst[0:64, lc:lc + TC], dst[0:64, lc:lc + TC],
                                  waits=[("S", s1)], inc="V")
                        t2 = p.op("tensor", "matmul", psS[1][:],
                                  onesb[0:64, :], sqb[0:64, 0:TC],
                                  start=True, stop=True,
                                  waits=[("V", v1)] + as_waits(psf[1]), inc="T")
                        s2 = p.op("scalar", "activation", rsb[:, 0:TC], psS[1][:],
                                  AF.Sqrt, scale=core.fscal(1.0 / HD), bias=epsc,
                                  waits=[("T", t2)], inc="S")
                        p.op("vector", "reciprocal", rsb[:, 0:TC], rsb[:, 0:TC],
                             waits=[("S", s2)])
                        v2 = p.op("vector", "tensor_copy", rsbb[:, 0:TC],
                                  rsb[:, 0:TC], inc="V")
                        psf[1] = {"V": v2}
                        p.op("vector", "tensor_mul", dst[0:64, lc:lc + TC],
                             dst[0:64, lc:lc + TC], rsbb[0:64, 0:TC])
                        v3 = p.op("vector", "tensor_scalar_mul", dst[0:64, lc:lc + TC],
                                  dst[0:64, lc:lc + TC], nw[:, 0:1], inc="V")
                        t3 = p.op("tensor", "matmul", psS[1][0:64, :],
                                  perm[0:64, 0:64], dst[0:64, lc:lc + TC],
                                  start=True, stop=True,
                                  waits=[("V", v3)] + as_waits(psf[1]), inc="T")
                        s3 = p.op("scalar", "activation", qsw[0:64, 0:TC],
                                  psS[1][0:64, :], AF.Copy, waits=[("T", t3)], inc="S")
                        psf[1] = {"S": s3}
                        p.op("vector", "tensor_mul", dst[0:64, lc:lc + TC],
                             dst[0:64, lc:lc + TC], cosb[0:64, lc:lc + TC],
                             waits=[("S", s3)])
                        p.op("vector", "tensor_mul", qsw[0:64, 0:TC],
                             qsw[0:64, 0:TC], sinb[0:64, lc:lc + TC])
                        v4 = p.op("vector", "tensor_add", dst[0:64, lc:lc + TC],
                                  dst[0:64, lc:lc + TC], qsw[0:64, 0:TC], inc="V")
                        small_war = {"V": v4}
                qk_d = [("V", p.cnt["V"])]
                vt_d = {}
                for s in range(NH):
                    bk = 2 + s % 2
                    t4 = mm_group(p, psS[bk][:, 0:64],
                                  [(xh[i][:, 128 * s:128 * (s + 1)],
                                    wv4[:, 256 * i + 64 * hi:256 * i + 64 * hi + 64])
                                   for i in range(DT)],
                                  waits_first=xw + wg_d + as_waits(psf[bk]))
                    s4 = p.op("scalar", "activation", vtm[:, 64 * s:64 * s + 64],
                              psS[bk][:, 0:64], AF.Copy, waits=[("T", t4)], inc="S")
                    psf[bk] = {"S": s4}
                    vt_d = {"S": s4}
                vw = as_waits(vt_d) + qk_d
                for ti in range(NH):
                    nfull = ti // 4
                    w_last = (ti % 4 + 1) * 128
                    nch = nfull + 1
                    for j in range(nch):
                        wdt = TC if j < nfull else w_last
                        tS = p.op("tensor", "matmul", psS[j][:, 0:wdt],
                                  qh[0:64, 128 * ti:128 * (ti + 1)],
                                  kh[0:64, TC * j:TC * j + wdt],
                                  start=True, stop=True,
                                  waits=vw + as_waits(psf[j]), inc="T")
                        psf[j] = {"T": tS}
                    tSl = p.cnt["T"]
                    vm = p.op("vector", "tensor_add",
                              psS[nfull][:, w_last - 128:w_last],
                              psS[nfull][:, w_last - 128:w_last], mle[:],
                              waits=[("T", tSl)], inc="V")
                    for j in range(nch):
                        wdt = TC if j < nfull else w_last
                        p.op("vector", "reduce_max", mparts[:, j:j + 1],
                             psS[j][:, 0:wdt], AX.X,
                             waits=([("V", vm)] + as_waits(mcol_war)) if j == 0 else ())
                    vmx = p.op("vector", "reduce_max", mcol[:], mparts[:, 0:nch],
                               AX.X)
                    vmx = p.op("vector", "tensor_scalar_mul", mcol[:], mcol[:],
                               -0.125, inc="V")
                    se = None
                    for j in range(nch):
                        wdt = TC if j < nfull else w_last
                        se = p.op("scalar", "activation",
                                  psb[:, TC * j:TC * j + wdt], psS[j][:, 0:wdt],
                                  AF.Exp, bias=mcol[:, 0:1], scale=core.fscal(0.125),
                                  accum_out=dparts[:, j:j + 1],
                                  waits=([("V", vmx)] + as_waits(psb_war)) if j == 0 else (),
                                  inc="S")
                        psf[j] = {"S": se}
                    mcol_war = {"S": se}
                    vds = p.op("vector", "reduce_sum", dcol[:], dparts[:, 0:nch],
                               AX.X, waits=[("S", se)])
                    vds = p.op("vector", "reciprocal", dcol[:], dcol[:], inc="V")
                    vdv = None
                    for j in range(nch):
                        wdt = TC if j < nfull else w_last
                        vdv = p.op("vector", "tensor_scalar_mul",
                                   psb[:, TC * j:TC * j + wdt],
                                   psb[:, TC * j:TC * j + wdt], dcol[:, 0:1],
                                   inc="V" if j == nch - 1 else None)
                    ntk = 4 * nfull + (ti % 4 + 1)
                    first = True
                    for g in range((ntk + 3) // 4):
                        gn = min(4, ntk - 4 * g)
                        for jj in range(gn):
                            p.op("tensor", "matmul", psT[:, 128 * jj:128 * (jj + 1)],
                                 psb[:, 128 * (4 * g + jj):128 * (4 * g + jj + 1)],
                                 identb[:], start=True, stop=True,
                                 waits=([("V", vdv)] + as_waits(ptf)) if jj == 0 and g == 0
                                       else (as_waits(ptf) if jj == 0 else ()),
                                 inc="T" if jj == gn - 1 else None)
                        tT = p.cnt["T"]
                        sT = p.op("scalar", "activation", ptb[:, 0:128 * gn],
                                  psT[:, 0:128 * gn], AF.Copy,
                                  waits=[("T", tT)] + as_waits(ptf), inc="S")
                        for jj in range(gn):
                            tv = p.op("tensor", "matmul", psV[0:64, 0:128],
                                      vtm[:, 64 * (4 * g + jj):64 * (4 * g + jj) + 64],
                                      ptb[:, 128 * jj:128 * (jj + 1)],
                                      start=first, stop=(4 * g + jj == ntk - 1),
                                      waits=([("S", sT)] + as_waits(pvf)) if jj == 0
                                            else [("S", sT)],
                                      inc="T" if 4 * g + jj == ntk - 1 else None)
                            first = False
                        ptf = {"S": sT}
                    tpv = p.cnt["T"]
                    psb_war = {"T": tpv}
                    tile, ro = h // 2, 64 * (h % 2)
                    sy = p.op("scalar", "activation",
                              yfm[tile][ro:ro + 64, 128 * ti:128 * (ti + 1)],
                              psV[0:64, 0:128], AF.Copy,
                              waits=[("T", tpv)] + as_waits(yfm_war), inc="S")
                    pvf = {"S": sy}
            yfm_done = {"S": p.cnt["S"]}
            yw = as_waits(yfm_done)
            for tc in range(T_SEQ // TC):
                t0 = b * T_SEQ + tc * TC
                lc = tc * TC
                ydone = {}
                for ot in range(DT):
                    bk = ot % 2
                    tmm = mm_group(p, psS[bk][:],
                                   [(wo[i][:, 128 * ot:128 * (ot + 1)],
                                     yfm[i][:, lc:lc + TC]) for i in range(DT)],
                                   waits_first=yw + as_waits(wo_d) + as_waits(psf[bk]))
                    sy = p.op("scalar", "activation", yb[ot][:, 0:TC], psS[bk][:],
                              AF.Copy, waits=[("T", tmm)] + as_waits(yb_war), inc="S")
                    psf[bk] = {"S": sy}
                    ydone = {"S": sy, "T": tmm}
                for _h2 in range(2):
                    d2_ = emit_ln(p, core, lnp, yb, 256 * _h2, 256, res_add=True,
                                  res_t0=t0 + 256 * _h2,
                                  src_ready=as_waits(ydone) if _h2 == 0 else ())
                yb_war = {"T": lnp.prev["T"], "Gap": d2_["Gap"], "V": d2_["V"]}
                done = d2_
            yfm_war = {"T": p.cnt["T"]}
            xh_war = merge_counts({"T": p.cnt["T"]}, yb_war)
        core.res_ready = merge_counts(core.res_ready, done)


def emit_l2a(p, core, aps):
    NH, HD = 16, 64
    CC = 128
    with ExitStack() as st:
        stg_stack = ExitStack()
        staging = [p.sb(f"l2a_stg{i}", [128, 3072], BF16, stack=stg_stack)
                   for i in range(2)]
        stg_free = {0: {}, 1: {}}
        wtd_qkv = WTd(p, "l2aqkv", aps["L2_attn_qkv_w"], 3072, D, staging, stg_free)
        wtd_o = WTd(p, "l2ao", aps["L2_attn_out_w"], D, D, staging, stg_free)
        stg_rel = [("LD2", p.snap("LD2"))]
        stg_stack.close()
        with ExitStack() as stS:
            scps = p.ps("l2a_scl", [128, 32], F32, stack=stS)
            bqkv, bq_d = load_scale_cols(p, core, aps["L2_attn_qkv_b"], 3072, st,
                                         "bq", scps, extra_waits=stg_rel)
            bout, bo_d = load_scale_cols(p, core, aps["L2_attn_out_b"], D, st,
                                         "bo", scps, extra_waits=as_waits(bq_d))
        bias_d = as_waits(merge_counts(bq_d, bo_d))
        mge = core.const_sb["mul_f_ge_p"]
        onesb = core.const_sb["ones_bf"]

        lnp = LNPool(p, st, "A", tc=256)
        xh = [p.sb(f"l2a_xh{i}", [128, T_SEQ], BF16, stack=st) for i in range(DT)]
        yfm = [p.sb(f"l2a_y{i}", [128, T_SEQ], BF16, stack=st) for i in range(DT)]
        yb = [p.sb(f"l2a_yb{i}", [128, TC], BF16, stack=st) for i in range(DT)]
        w4 = p.sb("l2a_w4", [128, DT * 3 * 256], BF16, stack=st)
        wob = p.sb("l2a_wob", [128, DT * 1024], BF16, stack=st)
        qfm = p.sb("l2a_qfm", [64, T_SEQ], BF16, stack=st)
        kfm = p.sb("l2a_kfm", [64, T_SEQ], BF16, stack=st)
        ktm = p.sb("l2a_ktm", [128, NH * HD], BF16, stack=st)
        vtm = p.sb("l2a_vtm", [128, NH * HD], BF16, stack=st)
        kcum = p.sb("l2a_kcum", [64, CC], F32, stack=st)
        kcumb = p.sb("l2a_kcumb", [64, CC], BF16, stack=st)
        kcarry = p.sb("l2a_kcar", [64, 1], F32, stack=st)

        ptb = p.sb("l2a_ptb", [128, CC], BF16, stack=st)
        qkb = p.sb("l2a_qkb", [64, CC], BF16, stack=st)
        denf = p.sb("l2a_denf", [128, CC], F32, stack=st)
        kvsb = p.sb("l2a_kvsb", [64, 64], BF16, stack=st)
        kvb = p.sb("l2a_kvb", [32, 512], BF16, stack=st)

        psP = [p.ps(f"l2a_pp{i}", [128, TC], F32, stack=st) for i in range(2)]
        psPT = p.ps("l2a_ppt", [128, TC], F32, stack=st)
        psN = p.ps("l2a_pn", [128, TC], F32, stack=st)
        psD = p.ps("l2a_pd", [128, TC], F32, stack=st)
        psKV = p.ps("l2a_pkv", [128, TC], F32, stack=st)

        zrow = core.const_sb["zeros64"]

        res_r = as_waits(core.res_ready)
        done = {}
        yb_war = {}
        xh_war = {}
        pf = {0: {}, 1: {}}
        ptf = dict(merge_counts(bq_d, bo_d))
        nf = dict(ptf)
        df = dict(ptf)
        kvf = dict(ptf)
        yfm_war = {}
        for b in range(B_PER_CORE):
            for tc in range(T_SEQ // 256):
                t0 = b * T_SEQ + tc * 256
                d1_ = emit_ln(p, core, lnp, core.res, t0, 256, dst_tiles=xh,
                              dst_t0=tc * 256,
                              src_ready=(res_r + stg_rel + bias_d)
                                        if (b == 0 and tc == 0) else (),
                              dst_war=as_waits(xh_war) if tc == 0 else ())
            xw = as_waits(d1_)
            for h in range(NH):
                hg, hi = h // 4, h % 4
                if hi == 0:
                    for j in range(3):
                        load_wslice(p, wtd_qkv, 1024 * j + 256 * hg, 256, w4,
                                    col0=DT * 256 * j,
                                    extra_waits=stg_rel + [("T", p.cnt["T"])])
                    for (c0, j) in ((0, 1), (256, 2)):
                        bsl = aps["L2_attn_qkv_b"][1024 * j + 256 * hg:
                                                   1024 * j + 256 * hg + 256]
                        p.dma("gpsimd", kvb[:, c0:c0 + 256],
                              bsl.rearrange("(a c) -> a c", a=1).partition_broadcast(32),
                              waits=[("T", p.cnt["T"])])
                    wg_d = [("LD", p.snap("LD"))]

                def wsl(j, i):
                    c0 = DT * 256 * j + 256 * i + 64 * hi
                    return w4[:, c0:c0 + 64]
                bq = bqkv[64 * (h % 2):64 * (h % 2) + 64, h // 2:h // 2 + 1]
                bk = bqkv[64 * (h % 2):64 * (h % 2) + 64, 8 + h // 2:8 + h // 2 + 1]
                bv = bqkv[64 * (h % 2):64 * (h % 2) + 64, 16 + h // 2:16 + h // 2 + 1]
                for (dst, j, bias) in ((qfm, 0, bq), (kfm, 1, bk)):
                    for cc in range(T_SEQ // TC):
                        lc = cc * TC
                        bkk = cc % 2
                        t1 = mm_group(p, psP[bkk][0:64, :],
                                      [(wsl(j, i), xh[i][:, lc:lc + TC])
                                       for i in range(DT)],
                                      waits_first=xw + wg_d + bias_d + as_waits(pf[bkk]))
                        s1 = p.op("scalar", "activation", dst[0:64, lc:lc + TC],
                                  psP[bkk][0:64, :], AF.Relu, bias=bias,
                                  waits=[("T", t1)], inc="S")
                        v1 = p.op("vector", "tensor_scalar_add",
                                  dst[0:64, lc:lc + TC], dst[0:64, lc:lc + TC],
                                  1e-6, waits=[("S", s1)], inc="V")
                        pf[bkk] = {"S": s1, "V": v1}
                ob = core.const_sb["ones32_bf"]
                for s in range(NH):
                    bkk = s % 2
                    t2 = mm_group(p, psP[bkk][:, 0:64],
                                  [(xh[i][:, 128 * s:128 * (s + 1)], wsl(1, i))
                                   for i in range(DT)],
                                  waits_first=xw + wg_d + as_waits(pf[bkk]),
                                  inc_last=None)
                    t2 = p.op("tensor", "matmul", psP[bkk][:, 0:64],
                              ob[:], kvb[0:32, 64 * hi:64 * hi + 64],
                              start=False, stop=True, inc="T")
                    s2 = p.op("scalar", "activation", ktm[:, 64 * s:64 * s + 64],
                              psP[bkk][:, 0:64], AF.Relu,
                              waits=[("T", t2)], inc="S")
                    t3 = mm_group(p, psP[bkk][:, 64:128],
                                  [(xh[i][:, 128 * s:128 * (s + 1)], wsl(2, i))
                                   for i in range(DT)],
                                  waits_first=[("S", s2)], inc_last=None)
                    t3 = p.op("tensor", "matmul", psP[bkk][:, 64:128],
                              ob[:], kvb[0:32, 256 + 64 * hi:256 + 64 * hi + 64],
                              start=False, stop=True, inc="T")
                    s3 = p.op("scalar", "activation", vtm[:, 64 * s:64 * s + 64],
                              psP[bkk][:, 64:128], AF.Copy,
                              waits=[("T", t3)], inc="S")
                    pf[bkk] = {"S": s3}
                v2 = p.op("vector", "tensor_scalar_add", ktm[:], ktm[:], 1e-6,
                          waits=[("S", p.cnt["S"])], inc="V")
                hw = [("V", p.cnt["V"]), ("S", p.cnt["S"])]
                for ci in range(T_SEQ // CC):
                    lc = ci * CC
                    tp = p.op("tensor", "matmul", psPT[:, 0:CC],
                              kfm[0:64, lc:lc + CC], qfm[0:64, lc:lc + CC],
                              start=True, stop=True,
                              waits=hw + as_waits(ptf), inc="T")
                    vp = p.op("vector", "tensor_tensor", ptb[:, 0:CC],
                              psPT[:, 0:CC], mge[:, 0:CC], ALU.mult,
                              waits=[("T", tp)], inc="V")
                    ptf = {"V": vp}
                    tn = p.op("tensor", "matmul", psN[0:64, 0:CC],
                              vtm[:, 64 * ci:64 * ci + 64], ptb[:, 0:CC],
                              start=True, stop=(ci == 0),
                              waits=[("V", vp)] + as_waits(nf),
                              inc="T" if ci == 0 else None)
                    if ci > 0:
                        tn = p.op("tensor", "matmul", psN[0:64, 0:CC],
                                  kvsb[:], qfm[0:64, lc:lc + CC],
                                  start=False, stop=True,
                                  waits=as_waits(kvf), inc="T")
                    vs = p.op("vector", "tensor_tensor_scan", kcum[0:64, 0:CC],
                              kfm[0:64, lc:lc + CC], zrow[0:64, 0:CC],
                              0.0 if ci == 0 else kcarry[0:64, 0:1],
                              ALU.add, ALU.add)
                    vs = p.op("vector", "tensor_copy", kcarry[0:64, 0:1],
                              kcum[0:64, CC - 1:CC])
                    vs = p.op("vector", "tensor_copy", kcumb[0:64, 0:CC],
                              kcum[0:64, 0:CC])
                    vs = p.op("vector", "tensor_mul", qkb[0:64, 0:CC],
                              qfm[0:64, lc:lc + CC], kcumb[0:64, 0:CC], inc="V")
                    td = p.op("tensor", "matmul", psD[:, 0:CC],
                              onesb[0:64, :], qkb[0:64, 0:CC],
                              start=True, stop=True,
                              waits=[("V", vs)] + as_waits(df), inc="T")
                    vd = p.op("vector", "tensor_scalar_max", denf[:, 0:CC],
                              psD[:, 0:CC], 1e-6, waits=[("T", td)])
                    vd = p.op("vector", "reciprocal", denf[:, 0:CC], denf[:, 0:CC],
                              inc="V")
                    df = {"V": vd}
                    vy = p.op("vector", "tensor_tensor",
                              yfm[h // 2][64 * (h % 2):64 * (h % 2) + 64, lc:lc + CC],
                              psN[0:64, 0:CC], denf[0:64, 0:CC], ALU.mult,
                              waits=[("T", tn), ("V", vd)] + as_waits(yfm_war), inc="V")
                    nf = {"V": vy}
                    tkv = p.op("tensor", "matmul", psKV[0:64, 0:64],
                               ktm[:, 64 * ci:64 * ci + 64],
                               vtm[:, 64 * ci:64 * ci + 64],
                               start=(ci == 0), stop=(ci == T_SEQ // CC - 1),
                               waits=as_waits(kvf), inc="T")
                    skv = p.op("scalar", "activation", kvsb[:], psKV[0:64, 0:64],
                               AF.Copy, waits=[("T", tkv)], inc="S")
                    kvf = {"S": skv}
            yfm_done = {"V": p.cnt["V"]}
            lo = load_wslice(p, wtd_o, 0, 1024, wob,
                             extra_waits=stg_rel + [("T", p.cnt["T"])])
            yw = as_waits(yfm_done) + as_waits(lo)
            for tc in range(T_SEQ // TC):
                t0 = b * T_SEQ + tc * TC
                lc = tc * TC
                ydone = {}
                for ot in range(DT):
                    bkk = ot % 2
                    tmm = mm_group(p, psP[bkk][:],
                                   [(wob[:, 1024 * i + 128 * ot:1024 * i + 128 * (ot + 1)],
                                     yfm[i][:, lc:lc + TC]) for i in range(DT)],
                                   waits_first=yw + as_waits(pf[bkk]))
                    sy = p.op("scalar", "activation", yb[ot][:, 0:TC], psP[bkk][:],
                              AF.Identity, bias=bout[:, ot:ot + 1],
                              waits=[("T", tmm)] + as_waits(yb_war), inc="S")
                    pf[bkk] = {"S": sy}
                    ydone = {"S": sy, "T": tmm}
                for _h2 in range(2):
                    d2_ = emit_ln(p, core, lnp, yb, 256 * _h2, 256, res_add=True,
                                  res_t0=t0 + 256 * _h2,
                                  src_ready=as_waits(ydone) if _h2 == 0 else ())
                yb_war = {"T": lnp.prev["T"], "Gap": d2_["Gap"], "V": d2_["V"]}
                done = d2_
            yfm_war = {"T": p.cnt["T"]}
            xh_war = merge_counts({"T": p.cnt["T"]}, yb_war)
        core.res_ready = merge_counts(core.res_ready, done)


L1A_STAGES = "ABC"


def emit_l1a(p, core, aps):
    H, HD, CC, KW = 4, 256, 128, 4
    NCH = T_SEQ // CC
    with ExitStack() as st:
        win_ap = aps["L1_attn_W_in"]
        wctrl_ap = aps["L1_attn_W_ctrl"]
        wout_ap = aps["L1_attn_W_out"]
        wbuf = p.sb("l1a_wbuf", [128, DT * 1024], BF16, stack=st)
        wab = p.sb("l1a_wab", [128, DT * 8], BF16, stack=st)
        cvw = {}
        for nm in ("w_q", "w_k", "w_v"):
            t = p.sb(f"l1a_cv{nm}", [128, 4 * DT], F32, stack=st)
            p.dma("sync", t[:].rearrange("p (i c) -> p i c", c=4),
                  aps["L1_attn_" + nm].rearrange("(i p) a c -> p i (a c)", p=128)
                  .rearrange("p i c -> p i c"))
            cvw[nm] = t
        bab = p.sb("l1a_bab", [8, 1], F32, stack=st)
        p.dma("sync", bab[:], aps["L1_attn_b_ctrl"][0:8].rearrange("(a b) -> a b", b=1))
        with ExitStack() as stS:
            scps = p.ps("l1a_scl", [128, 32], F32, stack=stS)
            bogc, bog_d = load_scale_cols(p, core, aps["L1_attn_b_ctrl"][8:1032],
                                          D, st, "bog", scps)
            boutc, bo_d = load_scale_cols(p, core, aps["L1_attn_b_out"], D, st,
                                          "bo", scps, extra_waits=as_waits(bog_d))
            rampc, rp_d = load_scale_cols(p, core, aps["L1_attn_ramp_logit"], D, st,
                                          "rp", scps, extra_waits=as_waits(bo_d))
        p.op("scalar", "activation", rampc[:], rampc[:], AF.Sigmoid,
             waits=as_waits(rp_d), inc="S")
        vec_d = [("S", p.cnt["S"]), ("LD", p.snap("LD"))] \
            + as_waits(merge_counts(bog_d, bo_d, rp_d))

        identb = core.const_sb["ident_bf"]
        mlt = core.const_sb["madd_f_lt_p"]
        mgt = core.const_sb["madd_f_gt_p"]
        mge = core.const_sb["madd_f_ge_p"]
        ones32f = core.const_sb["ones32"]
        onesb = core.const_sb["ones_bf"]
        identf = core.const_sb["ident"]

        lnp = LNPool(p, st, "A", tc=256)
        xhat_d = p.dtmp("l1a_xhat", [D, T_SEQ], BF16)
        og_d = p.dtmp("l1a_og", [D, T_SEQ], BF16)
        xhc = [p.sb(f"l1a_xhc{i}", [128, 256], BF16, stack=st) for i in range(DT)]
        _xst1 = p.sb("l1a_xst", [128, DT * TC], BF16, stack=st)
        xst = [_xst1, _xst1]
        q_d = p.dtmp("l1a_qd", [D, T_SEQ], BF16)
        k_d = p.dtmp("l1a_kd", [D, T_SEQ], BF16)
        v_d = p.dtmp("l1a_vd", [D, T_SEQ], BF16)
        qp = [p.sb(f"l1a_qp{i}", [128, T_SEQ], BF16, stack=st) for i in range(2)]
        qcb = p.sb("l1a_qcb", [128, DT * CC], BF16, stack=st)
        kcb = p.sb("l1a_kcb", [128, DT * CC], BF16, stack=st)
        vcb = p.sb("l1a_vcb", [128, DT * CC], BF16, stack=st)
        rdf = [p.sb(f"l1a_rd{i}", [128, T_SEQ], BF16, stack=st) for i in range(DT)]
        ogc = [p.sb(f"l1a_ogc{i}", [128, TC], BF16, stack=st) for i in range(DT)]
        yb = [p.sb(f"l1a_yb{i}", [128, TC], BF16, stack=st) for i in range(DT)]
        xld = p.sb("l1a_xld", [128, T_SEQ], BF16, stack=st)
        xin_d = p.dtmp("l1a_xin", [128, T_SEQ + 4], BF16)
        sqb = p.sb("l1a_sqb", [128, TC], BF16, stack=st)
        ogs = sqb
        rsf = p.sb("l1a_rsf", [128, TC], F32, stack=st)
        rsbb = p.sb("l1a_rsbb", [128, TC], BF16, stack=st)
        cums_t = p.sb("l1a_cumt", [128, H * NCH], F32, stack=st)
        beta_t = p.sb("l1a_bett", [128, H * NCH], F32, stack=st)
        abrow = p.sb("l1a_abrow", [32, TC], F32, stack=st)
        abrow2 = p.sb("l1a_abrow2", [32, TC], F32, stack=st)
        zrow = core.const_sb["zeros64"]
        rowc = p.sb("l1a_rowc", [32, CC], F32, stack=st)
        rowb = p.sb("l1a_rowb", [32, CC], F32, stack=st)
        shbuf = p.sb("l1a_shbuf", [32, CC], F32, stack=st)
        Rf = p.sb("l1a_Rf", [128, CC], F32, stack=st)
        RbN = p.sb("l1a_RbN", [128, CC], F32, stack=st)
        Rgb = p.sb("l1a_Rgb", [128, CC], BF16, stack=st)
        Dm = p.sb("l1a_Dm", [128, CC], F32, stack=st)
        Ef = p.sb("l1a_Ef", [128, CC], F32, stack=st)
        Gsb = p.sb("l1a_Gsb", [128, CC], F32, stack=st)
        Xa = p.sb("l1a_Xa", [128, CC], F32, stack=st)
        XaT = p.sb("l1a_XaT", [128, CC], F32, stack=st)
        PTf = p.sb("l1a_PTf", [128, CC], F32, stack=st)
        Wsb = p.sb("l1a_Wsb", [128, HD], F32, stack=st)
        Vf = p.sb("l1a_Vf", [128, HD], F32, stack=st)
        Ktf = p.sb("l1a_Ktf", [128, HD], F32, stack=st)
        qtf = p.sb("l1a_qtf", [128, HD], F32, stack=st)
        Rsb = p.sb("l1a_Rsb", [128, HD], F32, stack=st)
        rdt = p.sb("l1a_rdt", [128, HD], BF16, stack=st)
        colA = p.sb("l1a_colA", [128, 1], F32, stack=st)
        colB = p.sb("l1a_colB", [128, 1], F32, stack=st)
        colG = p.sb("l1a_colG", [128, 1], F32, stack=st)
        Z = [p.sb(f"l1a_Z{i}", [128, HD], F32, stack=st) for i in range(2 * H)]

        psA = p.ps("l1a_pA", [128, TC], F32, stack=st)
        psB = p.ps("l1a_pB", [128, TC], F32, stack=st)
        psW = p.ps("l1a_pW", [128, TC], F32, stack=st)
        psR = p.ps("l1a_pR", [128, TC], F32, stack=st)
        psZ = p.ps("l1a_pZ", [128, TC], F32, stack=st)
        psQ = p.ps("l1a_pQ", [128, TC], F32, stack=st)

        p.dma("gpsimd", xin_d[:, 0:3], core.const_sb["zeros128b"][:, 0:3],
              waits=as_waits(core.res_ready))
        zm = 0
        stream_war = {0: {}, 1: {}}
        ogc_war = {}
        res_r = as_waits(core.res_ready)
        done = {}
        yb_war = {}
        xh_war = {}
        fA = {}
        fB = {}
        fW = {}
        fR = {}
        fZ = {}
        fQ = {}
        ct_war = {}
        conv_war = {}
        for b in range(B_PER_CORE):
            for tc in range(T_SEQ // 256):
                t0 = b * T_SEQ + tc * 256
                d1_ = emit_ln(p, core, lnp, core.res, t0, 256, dst_tiles=xhc,
                              dst_t0=0,
                              src_ready=(res_r + vec_d) if (b == 0 and tc == 0) else (),
                              dst_war=as_waits(xh_war))
                xst_c = None
                for i in range(DT):
                    xst_c = p.dma("scalar",
                                  xhat_d[128 * i:128 * (i + 1), tc * 256:tc * 256 + 256],
                                  xhc[i][:, 0:256], sem="LD2",
                                  waits=as_waits(d1_) if i == 0 else ())
                xh_war = {"LD2": xst_c}
            xw = vec_d + [("LD2", p.snap("LD2"))]

            def stream_xh(cc2, par, extra=()):
                buf = xst[par]
                c = p.dma("gpsimd", buf[:].rearrange("p (i t) -> p i t", t=TC),
                          xhat_d[:, TC * cc2:TC * (cc2 + 1)].rearrange(
                              "(i p) t -> p i t", p=128),
                          waits=xw + list(extra))
                return buf, [("LDG", c)]
            slabs = ((q_d, 0, "w_q"), (k_d, 1024, "w_k"), (v_d, 2048, "w_v"))
            if "2" in L1A_STAGES:
                slabs = ()
            elif "V" in L1A_STAGES:
                slabs = ((v_d, 2048, "w_v"),)
            elif "Q" in L1A_STAGES:
                slabs = ((q_d, 0, "w_q"),)
            qk_d = {}
            pr_d = {}
            for (dram, wcol, cw) in slabs:
                for i in range(DT):
                    p.dma("gpsimd", wbuf[:, 1024 * i:1024 * (i + 1)],
                          win_ap[128 * i:128 * (i + 1), wcol:wcol + 1024],
                          waits=[("T", p.cnt["T"])] if i == 0 else ())
                wld = [("LDG", p.snap("LDG"))]
                for ot in range(DT):
                    dst = qp
                    bkk = ot % 2
                    ps_ = psA if bkk == 0 else psB
                    f_ = fA if bkk == 0 else fB
                    if "N" in L1A_STAGES:
                        for cc2 in range(T_SEQ // TC):
                            lc = cc2 * TC
                            sbuf_x, sxw = stream_xh(cc2, 0,
                                                    extra=as_waits(stream_war[0]))
                            t1 = mm_group(p, ps_[:],
                                          [(wbuf[:, 1024 * i + 128 * ot:1024 * i + 128 * (ot + 1)],
                                            sbuf_x[:, TC * i:TC * (i + 1)]) for i in range(DT)],
                                          waits_first=xw + wld + sxw + as_waits(f_))
                            stream_war[0] = {"T": t1}
                            s1 = p.op("scalar", "activation",
                                      dst[ot % 2][:, lc:lc + TC], ps_[:], AF.Silu,
                                      waits=[("T", t1)] + (as_waits(conv_war) if cc2 == 0 else []),
                                      inc="S")
                            f_.clear()
                            f_.update({"S": s1})
                        if ot % 2 == 1:
                            wv = [("S", p.cnt["S"])]
                            for tt in range(2):
                                c2 = p.dma("scalar",
                                           dram[128 * (ot - 1 + tt):128 * (ot + tt), :],
                                           dst[tt][:], sem="LD2",
                                           waits=wv if tt == 0 else ())
                            conv_war = {"LD2": c2, "S": p.cnt["S"]}
                        continue
                    for cc2 in range(T_SEQ // TC):
                        lc = cc2 * TC
                        sbuf_x, sxw = stream_xh(cc2, 0,
                                                extra=as_waits(stream_war[0]))
                        t1 = mm_group(p, ps_[:],
                                      [(wbuf[:, 1024 * i + 128 * ot:1024 * i + 128 * (ot + 1)],
                                        sbuf_x[:, TC * i:TC * (i + 1)]) for i in range(DT)],
                                      waits_first=xw + wld + sxw + as_waits(f_))
                        stream_war[0] = {"T": t1}
                        s1 = p.op("scalar", "activation", xld[:, lc:lc + TC],
                                  ps_[:], AF.Copy,
                                  waits=[("T", t1)] + (as_waits(conv_war) if cc2 == 0 else []),
                                  inc="S")
                        f_.clear()
                        f_.update({"S": s1})
                    cwt = cvw[cw]
                    dstt = dst[ot % 2]
                    sst = p.dma("gpsimd", xin_d[:, 3:3 + T_SEQ], xld[:],
                                waits=[("S", p.cnt["S"])] + as_waits(conv_war),
                                sem="LDG")
                    vy = None
                    for j in range(KW):
                        lj = p.dma("gpsimd", xld[:], xin_d[:, j:j + T_SEQ],
                                   sem="LDG",
                                   waits=[("LDG", sst)] if j == 0 else
                                         [("V", vy)])
                        if j == 0:
                            vy = p.op("vector", "tensor_scalar_mul", dstt[:],
                                      xld[:], cwt[:, 4 * ot:4 * ot + 1],
                                      waits=[("LDG", lj)], inc="V")
                        else:
                            vy = p.op("vector", "scalar_tensor_tensor", dstt[:],
                                      xld[:], cwt[:, 4 * ot + j:4 * ot + j + 1],
                                      dstt[:], ALU.mult, ALU.add,
                                      waits=[("LDG", lj)], inc="V")
                    sv = p.op("scalar", "activation", xld[:], dstt[:],
                              AF.Silu, waits=[("V", vy)], inc="S")
                    vcp = p.op("vector", "tensor_copy", dstt[:], xld[:],
                               waits=[("S", sv)], inc="V")
                    conv_war = {"S": sv, "V": vcp, "G": p.cnt["G"]}
                    if ot % 2 == 1:
                        h = ot // 2
                        pr_d = {"S": p.cnt["S"]}
                        if wcol != 2048:
                            for cc2 in range(T_SEQ // TC):
                                lc = cc2 * TC
                                v1 = p.op("vector", "tensor_mul", sqb[:, 0:TC],
                                          dst[0][:, lc:lc + TC],
                                          dst[0][:, lc:lc + TC],
                                          waits=as_waits(pr_d) if cc2 == 0 else (),
                                          inc="V")
                                p.op("tensor", "matmul", psA[:], onesb[:],
                                     sqb[:, 0:TC], start=True, stop=False,
                                     waits=[("V", v1)] + as_waits(fA))
                                v2 = p.op("vector", "tensor_mul", sqb[:, 0:TC],
                                          dst[1][:, lc:lc + TC],
                                          dst[1][:, lc:lc + TC],
                                          waits=[("V", v1)], inc="V")
                                t2 = p.op("tensor", "matmul", psA[:], onesb[:],
                                          sqb[:, 0:TC], start=False, stop=True,
                                          waits=[("V", v2)], inc="T")
                                s2 = p.op("scalar", "activation", rsf[:, 0:TC],
                                          psA[:], AF.Sqrt,
                                          waits=[("T", t2)], inc="S")
                                fA.clear()
                                fA.update({"S": s2})
                                p.op("vector", "tensor_scalar_max", rsf[:, 0:TC],
                                     rsf[:, 0:TC], 1e-6, waits=[("S", s2)])
                                p.op("vector", "reciprocal", rsf[:, 0:TC],
                                     rsf[:, 0:TC])
                                v3 = p.op("vector", "tensor_copy", rsbb[:, 0:TC],
                                          rsf[:, 0:TC], inc="V")
                                for tt in range(2):
                                    p.op("vector", "tensor_mul",
                                         dst[tt][:, lc:lc + TC],
                                         dst[tt][:, lc:lc + TC],
                                         rsbb[:, 0:TC],
                                         waits=[("V", v3)] if tt == 0 else ())
                        vfin = p.op("vector", "engine_nop", inc="V") if False else None
                        wv = [("V", p.cnt["V"]), ("S", p.cnt["S"]), ("G", p.cnt["G"])]
                        for tt in range(2):
                            c2 = p.dma("scalar",
                                       dram[128 * (2 * h + tt):128 * (2 * h + tt + 1), :],
                                       dst[tt][:], sem="LD2",
                                       waits=wv if tt == 0 else ())
                        conv_war = merge_counts(conv_war, {"LD2": c2})
                qk_d = {"V": p.cnt["V"], "S": p.cnt["S"], "G": p.cnt["G"],
                        "LD2": p.snap("LD2")}
            if "3" in L1A_STAGES:
                qk_d = {}
                ctl_d = {}
                continue
            for i in range(DT):
                p.dma("gpsimd", wbuf[:, 1024 * i:1024 * (i + 1)],
                      wctrl_ap[128 * i:128 * (i + 1), 8:1032],
                      waits=[("T", p.cnt["T"])] if i == 0 else ())
            p.dma("gpsimd", wab[:].rearrange("p (i c) -> p i c", c=8),
                  wctrl_ap[:, 0:8].rearrange("(i p) c -> p i c", p=128))
            wld2 = [("LDG", p.snap("LDG"))]
            og_war = {}
            for ot in range(DT):
                bkk = ot % 2
                ps_ = psA if bkk == 0 else psB
                f_ = fA if bkk == 0 else fB
                for cc2 in range(T_SEQ // TC):
                    lc = cc2 * TC
                    sbuf_x, sxw = stream_xh(cc2, 0,
                                            extra=as_waits(stream_war[0]))
                    t1 = mm_group(p, ps_[:],
                                  [(wbuf[:, 1024 * i + 128 * ot:1024 * i + 128 * (ot + 1)],
                                    sbuf_x[:, TC * i:TC * (i + 1)]) for i in range(DT)],
                                  waits_first=xw + wld2 + sxw + as_waits(f_))
                    stream_war[0] = {"T": t1}
                    s1 = p.op("scalar", "activation", ogs[:, 0:TC], ps_[:],
                              AF.Sigmoid, bias=bogc[:, ot:ot + 1],
                              waits=[("T", t1)] + as_waits(og_war), inc="S")
                    if b == 0 and ot == 0 and cc2 == 0:
                        tap(p, "l1a_ogs0", ogs[:, 0:TC], [("S", s1)],
                            block_eng="scalar")
                        tap(p, "l1a_ogx0", sbuf_x[:, 0:TC], [("S", s1)],
                            block_eng=None)
                        tap(p, "l1a_ogw0", wbuf[:, 0:1024], [("S", s1)],
                            block_eng=None)
                        tap(p, "l1a_ogb0", bogc[:], [("S", s1)], block_eng=None)
                    c1 = p.dma("scalar", og_d[128 * ot:128 * (ot + 1), lc:lc + TC],
                               ogs[:, 0:TC], sem="LD2", waits=[("S", s1)])
                    og_war = {"LD2": c1}
                    f_.clear()
                    f_.update({"S": s1})
            ab_d = {}
            for cc2 in range(T_SEQ // TC):
                lc = cc2 * TC
                sbuf_x, sxw = stream_xh(cc2, 0,
                                        extra=as_waits(stream_war[0]))
                t1 = mm_group(p, psQ[0:8, :],
                              [(wab[:, 8 * i:8 * (i + 1)],
                                sbuf_x[:, TC * i:TC * (i + 1)])
                               for i in range(DT)],
                              waits_first=xw + wld2 + sxw + as_waits(fQ))
                stream_war[0] = {"T": t1}
                s1 = p.op("scalar", "activation", rowc[0:8, 0:TC] if False else
                          abrow[0:8, 0:TC], psQ[0:8, :], AF.Sigmoid,
                          bias=bab[0:8, 0:1], waits=[("T", t1)], inc="S")
                fQ = {"S": s1}
                v1 = p.op("vector", "tensor_scalar", abrow2[0:4, 0:TC],
                          abrow[0:4, 0:TC], 0.98, 0.01, ALU.mult, ALU.add,
                          waits=[("S", s1)])
                if b == 0 and cc2 == 0:
                    tap(p, "l1a_ab", abrow[0:8, 0:TC], [("S", s1)])
                    tap(p, "l1a_ap", abrow2[0:4, 0:TC], [("V", p.cnt["V"])])
                s2 = p.op("scalar", "activation", abrow2[0:4, 0:TC],
                          abrow2[0:4, 0:TC], AF.Ln, waits=[("V", p.cnt["V"])], inc="S")
                if b == 0 and cc2 == 0:
                    tap(p, "l1a_la", abrow2[0:4, 0:TC], [("S", s2)], block_eng="scalar")
                for q2 in range(TC // CC):
                    ci = cc2 * 4 + q2
                    l2 = q2 * CC
                    v2 = p.op("vector", "tensor_tensor_scan", shbuf[0:4, 0:CC],
                              abrow2[0:4, l2:l2 + CC], zrow[0:4, 0:CC], 0.0,
                              ALU.add, ALU.add,
                              waits=[("S", s2)] + as_waits(fW), inc="V")
                    if b == 0 and ci == 0:
                        tap(p, "l1a_sc0", shbuf[0:4, 0:CC], [("V", v2)])
                    t3 = p.op("tensor", "matmul", psW[0:128, 0:4],
                              shbuf[0:32, 0:CC], identf[0:32, 0:4],
                              start=True, stop=True,
                              waits=[("V", v2)] + as_waits(fW), inc="T")
                    s3 = p.op("scalar", "activation",
                              cums_t[:, 4 * ci:4 * ci + 4], psW[0:128, 0:4],
                              AF.Copy, waits=[("T", t3)] + as_waits(ct_war), inc="S")
                    fW = {"S": s3}
                    t4 = p.op("tensor", "matmul", psW[0:128, 4:8],
                              abrow[0:32, l2:l2 + CC], identf[0:32, 4:8],
                              start=True, stop=True, waits=[("S", s3)], inc="T")
                    s4 = p.op("scalar", "activation",
                              beta_t[:, 4 * ci:4 * ci + 4], psW[0:128, 4:8],
                              AF.Copy, waits=[("T", t4)], inc="S")
                    fW = {"S": s4}
            ctl_d = {"S": p.cnt["S"], "V": p.cnt["V"]}
            if b == 0:
                tap(p, "l1a_qd", q_d[:], as_waits(qk_d))
                tap(p, "l1a_kd", k_d[:], as_waits(qk_d))
                tap(p, "l1a_vd", v_d[:], as_waits(qk_d))
                tap(p, "l1a_xh", xhat_d[:], [("LD2", p.snap("LD2"))])
                tap(p, "l1a_ct", cums_t[:], as_waits(ctl_d))
                tap(p, "l1a_bt", beta_t[:], as_waits(ctl_d))
            aw = as_waits(merge_counts(qk_d, ctl_d))
            qkv_war = {}
            fQt = {}
            for ci in range(NCH if "B" in L1A_STAGES else 0):
                lc = ci * CC
                cl = None
                for (dram, buf) in ((q_d, qcb), (k_d, kcb), (v_d, vcb)):
                    cl = p.dma("gpsimd", buf[:].rearrange("p (i t) -> p i t", t=CC),
                               dram[:, lc:lc + CC].rearrange("(i p) t -> p i t", p=128),
                               waits=(aw + as_waits(qkv_war)) if dram is q_d else ())
                qkv_war = {}
                cw2 = [("LDG", cl)]

                def khs(tt):
                    return kcb[:, CC * (2 * h + tt):CC * (2 * h + tt) + CC]

                def qhs(tt):
                    return qcb[:, CC * (2 * h + tt):CC * (2 * h + tt) + CC]

                def vhs(tt):
                    return vcb[:, CC * (2 * h + tt):CC * (2 * h + tt) + CC]
                t5 = p.op("tensor", "matmul", psQ[0:4, 0:CC],
                          cums_t[:, 4 * ci:4 * ci + 4], identf[:],
                          start=True, stop=True, waits=aw + as_waits(fQ), inc="T")
                s5 = p.op("scalar", "activation", rowc[0:4, 0:CC], psQ[0:4, 0:CC],
                          AF.Copy, waits=[("T", t5)], inc="S")
                t6 = p.op("tensor", "matmul", psQ[0:4, 0:CC],
                          beta_t[:, 4 * ci:4 * ci + 4], identf[:],
                          start=True, stop=True, waits=[("S", s5)], inc="T")
                s6 = p.op("scalar", "activation", rowb[0:4, 0:CC], psQ[0:4, 0:CC],
                          AF.Copy, waits=[("T", t6)], inc="S")
                fQ = {"S": s6}
                row_d = [("S", s6)]
                for h in range(H):
                    ccol = cums_t[:, 4 * ci + h:4 * ci + h + 1]
                    bcol = beta_t[:, 4 * ci + h:4 * ci + h + 1]
                    v7 = p.op("vector", "stream_shuffle", shbuf[:, 0:CC],
                              rowc[0:32, 0:CC], [h] * 32, waits=row_d, inc="V")
                    t7 = p.op("tensor", "matmul", psQ[:, 0:CC], ones32f[:],
                              shbuf[:, 0:CC], start=True, stop=True,
                              waits=[("V", v7)] + as_waits(fQ), inc="T")
                    v8 = p.op("vector", "tensor_copy", Rf[:, 0:CC], psQ[:, 0:CC],
                              waits=[("T", t7)], inc="V")
                    v9 = p.op("vector", "stream_shuffle", shbuf[:, 0:CC],
                              rowb[0:32, 0:CC], [h] * 32, inc="V")
                    t8 = p.op("tensor", "matmul", psQ[:, 0:CC], ones32f[:],
                              shbuf[:, 0:CC], start=True, stop=True,
                              waits=[("V", v9), ("V", v8)], inc="T")
                    s8 = p.op("scalar", "activation", RbN[:, 0:CC], psQ[:, 0:CC],
                              AF.Copy, scale=core.fscal(-1.0),
                              waits=[("T", t8)], inc="S")
                    fQ = {"S": s8, "V": v8}
                    tG = mm_group(p, psA[:, 0:CC],
                                  [(khs(tt), khs(tt)) for tt in range(2)],
                                  waits_first=cw2 + as_waits(fA))
                    vG = p.op("vector", "tensor_copy", Gsb[:, 0:CC], psA[:, 0:CC],
                              waits=[("T", tG)], inc="V")
                    fA = {"V": vG}
                    p.op("vector", "tensor_scalar", Dm[:, 0:CC], Rf[:, 0:CC],
                         ccol, -1.0, ALU.subtract, ALU.mult, waits=[("V", v8)])
                    p.op("vector", "tensor_add", Dm[:, 0:CC], Dm[:, 0:CC], mlt[:, 0:CC])
                    sE = p.op("scalar", "activation", Ef[:, 0:CC], Dm[:, 0:CC],
                              AF.Exp, waits=[("V", p.cnt["V"])], inc="S")
                    p.op("vector", "tensor_mul", Ef[:, 0:CC], Ef[:, 0:CC],
                         Gsb[:, 0:CC], waits=[("S", sE), ("V", vG)])
                    vX = p.op("vector", "tensor_scalar", Xa[:, 0:CC], Ef[:, 0:CC],
                              bcol, -1.0, ALU.mult, ALU.mult, inc="V")
                    p.op("vector", "tensor_scalar", Dm[:, 0:CC], Rf[:, 0:CC],
                         ccol, None, ALU.subtract)
                    p.op("vector", "tensor_add", Dm[:, 0:CC], Dm[:, 0:CC], mgt[:, 0:CC])
                    sE2 = p.op("scalar", "activation", Ef[:, 0:CC], Dm[:, 0:CC],
                               AF.Exp, waits=[("V", p.cnt["V"])], inc="S")
                    p.op("vector", "tensor_mul", Ef[:, 0:CC], Ef[:, 0:CC],
                         Gsb[:, 0:CC], waits=[("S", sE2)])
                    vXT = p.op("vector", "tensor_mul", XaT[:, 0:CC], Ef[:, 0:CC],
                               RbN[:, 0:CC], waits=[("S", s8)], inc="V")
                    tKQ = mm_group(p, psB[:, 0:CC],
                                   [(khs(tt), qhs(tt)) for tt in range(2)],
                                   waits_first=cw2 + as_waits(fB))
                    p.op("vector", "tensor_scalar", Dm[:, 0:CC], Rf[:, 0:CC],
                         ccol, None, ALU.subtract)
                    p.op("vector", "tensor_add", Dm[:, 0:CC], Dm[:, 0:CC], mge[:, 0:CC])
                    sE3 = p.op("scalar", "activation", PTf[:, 0:CC], Dm[:, 0:CC],
                               AF.Exp, waits=[("V", p.cnt["V"])], inc="S")
                    vPT = p.op("vector", "tensor_tensor", PTf[:, 0:CC], PTf[:, 0:CC],
                               psB[:, 0:CC], ALU.mult,
                               waits=[("S", sE3), ("T", tKQ)], inc="V")
                    fB = {"V": vPT}
                    for (srcf, dstt) in ((vhs, Vf), (khs, Ktf)):
                        for tt in range(2):
                            tT = p.op("tensor", "matmul", psZ[:, 128 * tt:128 * (tt + 1)],
                                      srcf(tt), identb[:],
                                      start=True, stop=True,
                                      waits=(cw2 + as_waits(fZ)) if tt == 0 else (),
                                      inc="T")
                        vT = p.op("vector", "tensor_copy", dstt[:, 0:HD],
                                  psZ[:, 0:HD], waits=[("T", p.cnt["T"])], inc="V")
                        fZ = {"V": vT}
                    sRg = p.op("scalar", "activation", Rgb[:, 0:CC], Rf[:, 0:CC],
                               AF.Exp, waits=[("V", v8)], inc="S")
                    for tt in range(2):
                        p.op("vector", "tensor_tensor", sqb[:, 0:CC],
                             qhs(tt), Rgb[:, 0:CC], ALU.mult,
                             waits=[("S", sRg)] + cw2 + as_waits(fQt))
                        p.op("vector", "tensor_copy",
                             qtf[:, 128 * tt:128 * (tt + 1)], sqb[:, 0:CC],
                             inc="V")
                        fQt = {}
                    if ci > 0:
                        vkc = None
                        for tt in range(2):
                            vkc = p.op("vector", "tensor_copy",
                                       Rsb[:, 128 * tt:128 * (tt + 1)], khs(tt),
                                       waits=cw2 + as_waits(fR) if tt == 0 else (),
                                       inc="V")
                        tKZ = mm_group(p, psW[:, 0:HD],
                                       [(Rsb[:, 128 * tt:128 * (tt + 1)],
                                         Z[2 * h + tt][:])
                                        for tt in range(2)],
                                       waits_first=[("V", vkc)] + as_waits(fW))
                        sG = p.op("scalar", "activation", colG[:], ccol, AF.Exp,
                                  waits=[("V", p.cnt["V"])], inc="S")
                        vbg = p.op("vector", "tensor_mul", colA[:], colG[:], bcol,
                                   waits=[("S", sG)], inc="V")
                        vW = p.op("vector", "tensor_scalar", Wsb[:, 0:HD],
                                  psW[:, 0:HD], colA[:, 0:1], -1.0,
                                  ALU.mult, ALU.mult,
                                  waits=[("T", tKZ), ("V", vbg)], inc="V")
                        vW = p.op("vector", "scalar_tensor_tensor", Wsb[:, 0:HD],
                                  Vf[:, 0:HD], bcol, Wsb[:, 0:HD],
                                  ALU.mult, ALU.add, inc="V")
                        fW = {"V": vW}
                    else:
                        vW = p.op("vector", "tensor_scalar_mul", Wsb[:, 0:HD],
                                  Vf[:, 0:HD], bcol, waits=as_waits(fW), inc="V")
                        fW = {"V": vW}
                    Xc, XcT, Xn, XnT = Xa, XaT, Dm, Ef
                    for lvl in range(7):
                        ta = p.op("tensor", "matmul", psW[:, 0:HD], XcT[:, 0:CC],
                                  Wsb[:, 0:HD], start=True, stop=True,
                                  waits=[("V", p.cnt["V"])] +
                                        ([("V", vX), ("V", vXT)] if lvl == 0 else []),
                                  inc="T")
                        vW = p.op("vector", "tensor_add", Wsb[:, 0:HD], Wsb[:, 0:HD],
                                  psW[:, 0:HD], waits=[("T", ta)], inc="V")
                        if lvl < 6:
                            tsq = p.op("tensor", "matmul", psR[:, 0:CC], XcT[:, 0:CC],
                                       Xc[:, 0:CC], start=True, stop=True,
                                       waits=as_waits(fR), inc="T")
                            vsq = p.op("vector", "tensor_copy", Xn[:, 0:CC],
                                       psR[:, 0:CC], waits=[("T", tsq)], inc="V")
                            tsqT = p.op("tensor", "matmul", psR[:, 0:CC], Xc[:, 0:CC],
                                        XcT[:, 0:CC], start=True, stop=True,
                                        waits=[("V", vsq)], inc="T")
                            vsqT = p.op("vector", "tensor_copy", XnT[:, 0:CC],
                                        psR[:, 0:CC], waits=[("T", tsqT)], inc="V")
                            fR = {"V": vsqT}
                            Xc, XcT, Xn, XnT = Xn, XnT, Xc, XcT
                    fW = {"V": vW}
                    nR = 0
                    if ci > 0:
                        for tt in range(2):
                            p.op("tensor", "matmul", psR[:, 0:HD],
                                 qtf[:, 128 * tt:128 * (tt + 1)], Z[2 * h + tt][:],
                                 start=(tt == 0), stop=False,
                                 waits=as_waits(fR) if tt == 0 else ())
                        nR = 2
                    tR = p.op("tensor", "matmul", psR[:, 0:HD], PTf[:, 0:CC],
                              Wsb[:, 0:HD], start=(nR == 0), stop=True,
                              waits=(as_waits(fR) + [("V", vPT), ("V", vW)])
                                    if nR == 0 else [("V", vPT), ("V", vW)],
                              inc="T")
                    fQt = {"T": tR}
                    vcl = p.op("vector", "tensor_scalar", colB[:],
                               Rf[:, CC - 1:CC], ccol, None, ALU.subtract,
                               waits=[("V", v8)])
                    se2 = p.op("scalar", "activation", colB[:], colB[:], AF.Exp,
                               waits=[("V", p.cnt["V"])], inc="S")
                    sgc = p.op("scalar", "activation", colG[:], Rf[:, CC - 1:CC],
                               AF.Exp, waits=[("V", p.cnt["V"])], inc="S")
                    vwt = p.op("vector", "tensor_scalar_mul", Wsb[:, 0:HD],
                               Wsb[:, 0:HD], colB[:, 0:1],
                               waits=[("S", se2), ("V", vW), ("T", tR)], inc="V")
                    for tt in range(2):
                        tz = p.op("tensor", "matmul", psZ[:, 0:HD],
                                  Ktf[:, 128 * tt:128 * (tt + 1)], Wsb[:, 0:HD],
                                  start=True, stop=True,
                                  waits=[("V", vwt)] + as_waits(fZ), inc="T")
                        if ci > 0:
                            vz2 = p.op("vector", "scalar_tensor_tensor",
                                       Z[2 * h + tt][:], Z[2 * h + tt][:],
                                       colG[:, 0:1], psZ[:, 0:HD],
                                       ALU.mult, ALU.add,
                                       waits=[("T", tz), ("S", sgc)], inc="V")
                        else:
                            vz2 = p.op("vector", "tensor_copy", Z[2 * h + tt][:],
                                       psZ[:, 0:HD], waits=[("T", tz)], inc="V")
                        fZ = {"V": vz2}
                    vR = p.op("vector", "tensor_copy", Rsb[:, 0:HD], psR[:, 0:HD],
                              waits=[("T", tR)], inc="V")
                    fR = {"V": vR}
                    vsq2 = p.op("vector", "tensor_mul", Vf[:, 0:HD], Rsb[:, 0:HD],
                                Rsb[:, 0:HD])
                    vrs = p.op("vector", "reduce_sum", colA[:], Vf[:, 0:HD], AX.X)
                    srs = p.op("scalar", "activation", colA[:], colA[:], AF.Sqrt,
                               scale=core.fscal(1.0 / HD), bias=core.fscal(1e-6),
                               waits=[("V", p.cnt["V"])], inc="S")
                    vrc = p.op("vector", "reciprocal", colB[:], colA[:],
                               waits=[("S", srs)])
                    vrd = p.op("vector", "tensor_scalar_mul", rdt[:, 0:HD],
                               Rsb[:, 0:HD], colB[:, 0:1], inc="V")
                    for tt in range(2):
                        tt2 = p.op("tensor", "matmul", psQ[:, 0:CC],
                                   rdt[:, 128 * tt:128 * (tt + 1)], identb[:],
                                   start=True, stop=True,
                                   waits=[("V", vrd)] + as_waits(fQ), inc="T")
                        sfm = p.op("scalar", "activation",
                                   rdf[2 * h + tt][:, lc:lc + CC], psQ[:, 0:CC],
                                   AF.Copy, waits=[("T", tt2)], inc="S")
                        fQ = {"S": sfm}
                    qkv_war = merge_counts(qkv_war, {"T": p.cnt["T"],
                                                     "V": p.cnt["V"]})
            rd_done = {"S": p.cnt["S"], "V": p.cnt["V"]}
            if b == 0:
                for i in range(DT):
                    tap(p, f"l1a_rdf{i}", rdf[i][:], as_waits(rd_done))
            if "C" not in L1A_STAGES:
                continue
            for i in range(DT):
                p.dma("gpsimd", wbuf[:, 1024 * i:1024 * (i + 1)],
                      wout_ap[128 * i:128 * (i + 1), :],
                      waits=[("T", p.cnt["T"])] if i == 0 else ())
            wld3 = [("LDG", p.snap("LDG"))]
            yw = as_waits(rd_done)
            for tc in range(T_SEQ // TC):
                t0 = b * T_SEQ + tc * TC
                lc = tc * TC
                ydone = {}
                ogl = None
                for i in range(DT):
                    ogl = p.dma("gpsimd", ogc[i][:, 0:TC],
                                og_d[128 * i:128 * (i + 1), lc:lc + TC],
                                waits=(as_waits(ogc_war) + [("LD2", p.snap("LD2"))])
                                      if i == 0 else ())
                for ot in range(DT):
                    bkk = ot % 2
                    ps_ = psA if bkk == 0 else psB
                    f_ = fA if bkk == 0 else fB
                    tmm = mm_group(p, ps_[:],
                                   [(wbuf[:, 1024 * i + 128 * ot:1024 * (i) + 128 * ot + 128],
                                     rdf[i][:, lc:lc + TC]) for i in range(DT)],
                                   waits_first=yw + wld3 + as_waits(f_))
                    sy = p.op("scalar", "activation", yb[ot][:, 0:TC], ps_[:],
                              AF.Identity, bias=boutc[:, ot:ot + 1],
                              waits=[("T", tmm)] + as_waits(yb_war), inc="S")
                    if b == 0 and tc == 0 and ot == 0:
                        tap(p, "l1a_cy0", yb[0][:, 0:TC], [("S", sy)],
                            block_eng="scalar")
                    vy1 = p.op("vector", "tensor_mul", yb[ot][:, 0:TC],
                               yb[ot][:, 0:TC], ogc[ot][:, 0:TC],
                               waits=[("S", sy), ("LDG", ogl)])
                    if b == 0 and tc == 0 and ot == 0:
                        tap(p, "l1a_cy1", yb[0][:, 0:TC], [("V", p.cnt["V"])])
                        tap(p, "l1a_cog0", ogc[0][:, 0:TC], [("LDG", ogl)],
                            block_eng=None)
                    vy2 = p.op("vector", "tensor_scalar_mul", yb[ot][:, 0:TC],
                               yb[ot][:, 0:TC], rampc[:, ot:ot + 1], inc="V")
                    if b == 0 and tc == 0 and ot == 0:
                        tap(p, "l1a_cy2", yb[0][:, 0:TC], [("V", vy2)])
                        tap(p, "l1a_cramp", rampc[:], [("V", vy2)])
                        tap(p, "l1a_cbout", boutc[:], [("V", vy2)])
                        tap(p, "l1a_cwb", wbuf[:, 0:1024], [("V", vy2)])
                    f_.clear()
                    f_.update({"V": vy2})
                    ydone = {"V": vy2}
                ogc_war = {"V": p.cnt["V"]}
                for _h2 in range(2):
                    d2_ = emit_ln(p, core, lnp, yb, 256 * _h2, 256, res_add=True,
                                  res_t0=t0 + 256 * _h2,
                                  src_ready=as_waits(ydone) if _h2 == 0 else ())
                yb_war = {"T": lnp.prev["T"], "Gap": d2_["Gap"], "V": d2_["V"]}
                done = d2_
            xh_war = merge_counts({"T": p.cnt["T"]}, yb_war)
            ct_war = {"T": p.cnt["T"]}
            conv_war = merge_counts(conv_war, {"T": p.cnt["T"]})
        core.res_ready = merge_counts(core.res_ready, done)

BLOCK_CONSTS = {
    "l0f": ["eqmask16", "eqmask4", "ones_f32"],
    "l2f": ["ident"],
    "l3a": ["rope3_cos_bf", "rope3_sin_bf", "rope3_perm_bf", "madd_f_ge_p"],
    "l0a": ["rope0_cos_bf", "rope0_sin_bf", "rope0_perm_bf", "bones64_bf",
            "madd_f_le_p"],
    "l2a": ["mul_f_ge_p", "ident", "zeros64"],
    "l1a": ["madd_f_lt_p", "madd_f_ge_p", "madd_f_gt_p", "ones32", "ident",
            "zeros64", "zeros128b"],
}

BLOCK_WEIGHTS = {
    "l3f": ["L3_ffn_weight"],
    "l1f": ["L1_ffn_w_gate", "L1_ffn_w_up", "L1_ffn_w_down"],
    "l0f": ["L0_ffn_w_gate", "L0_ffn_w1_e", "L0_ffn_w3_e", "L0_ffn_w2_e",
            "L0_ffn_w1_s", "L0_ffn_w3_s", "L0_ffn_w2_s"],
    "l3a": ["L3_attn_W_DQ", "L3_attn_W_UQ", "L3_attn_W_QR", "L3_attn_W_DKV",
            "L3_attn_W_UK", "L3_attn_W_UV", "L3_attn_W_KR", "L3_attn_W_O"],
    "l0a": ["L0_attn_Wq", "L0_attn_Wk", "L0_attn_Wv", "L0_attn_Wo",
            "L0_attn_q_norm_w", "L0_attn_k_norm_w"],
    "l2a": ["L2_attn_qkv_w", "L2_attn_qkv_b", "L2_attn_out_w", "L2_attn_out_b"],
    "l1a": ["L1_attn_W_in", "L1_attn_W_ctrl", "L1_attn_b_ctrl", "L1_attn_w_q",
            "L1_attn_w_k", "L1_attn_w_v", "L1_attn_W_out", "L1_attn_b_out",
            "L1_attn_ramp_logit"],
    "l2f": ["L2_ffn_w1_w", "L2_ffn_w1_s", "L2_ffn_w2_w", "L2_ffn_w2_s",
            "L2_ffn_h1_w", "L2_ffn_h1_s", "L2_ffn_h2_w", "L2_ffn_h2_s",
            "L2_ffn_w3_w", "L2_ffn_w3_s"],
}

WEIGHT_SHAPES = {
    "L3_ffn_weight": (1024, 1024),
    "L0_ffn_w_gate": (16, 1024),
    "L0_ffn_w1_e": (16, 512, 1024),
    "L0_ffn_w3_e": (16, 512, 1024),
    "L0_ffn_w2_e": (16, 1024, 512),
    "L0_ffn_w1_s": (1024, 1024),
    "L0_ffn_w3_s": (1024, 1024),
    "L0_ffn_w2_s": (1024, 1024),
    "L3_attn_W_DQ": (1024, 256),
    "L3_attn_W_UQ": (256, 1024),
    "L3_attn_W_QR": (256, 512),
    "L3_attn_W_DKV": (1024, 256),
    "L3_attn_W_UK": (256, 1024),
    "L3_attn_W_UV": (256, 1024),
    "L3_attn_W_KR": (1024, 32),
    "L3_attn_W_O": (1024, 1024),
    "L0_attn_Wq": (1024, 1024),
    "L0_attn_Wk": (1024, 1024),
    "L0_attn_Wv": (1024, 1024),
    "L0_attn_Wo": (1024, 1024),
    "L0_attn_q_norm_w": (64,),
    "L0_attn_k_norm_w": (64,),
    "L2_attn_qkv_w": (3072, 1024),
    "L2_attn_qkv_b": (3072,),
    "L2_attn_out_w": (1024, 1024),
    "L2_attn_out_b": (1024,),
    "L1_attn_W_in": (1024, 3072),
    "L1_attn_W_ctrl": (1024, 1032),
    "L1_attn_b_ctrl": (1032,),
    "L1_attn_w_q": (1024, 1, 4),
    "L1_attn_w_k": (1024, 1, 4),
    "L1_attn_w_v": (1024, 1, 4),
    "L1_attn_W_out": (1024, 1024),
    "L1_attn_b_out": (1024,),
    "L1_attn_ramp_logit": (1024,),
    "L1_ffn_w_gate": (3072, 1024),
    "L1_ffn_w_up": (3072, 1024),
    "L1_ffn_w_down": (1024, 3072),
    "L2_ffn_w1_w": (4096, 1024),
    "L2_ffn_w1_s": (4096,),
    "L2_ffn_w2_w": (4096, 1024),
    "L2_ffn_w2_s": (4096,),
    "L2_ffn_h1_w": (4096, 4096),
    "L2_ffn_h1_s": (4096,),
    "L2_ffn_h2_w": (4096, 4096),
    "L2_ffn_h2_s": (4096,),
    "L2_ffn_w3_w": (1024, 4096),
    "L2_ffn_w3_s": (1024,),
}

WEIGHT_DTYPES = {
    "L1_ffn_w_gate": I32,
    "L1_ffn_w_up": I32,
    "L1_ffn_w_down": I32,
}

BLOCK_EMITTERS = {}


def _register():
    BLOCK_EMITTERS["l3f"] = lambda p, core, aps: emit_l3f(p, core, aps["L3_ffn_weight"])
    BLOCK_EMITTERS["l1f"] = emit_l1f
    BLOCK_EMITTERS["l2f"] = emit_l2f
    BLOCK_EMITTERS["l0f"] = emit_l0f
    BLOCK_EMITTERS["l3a"] = emit_l3a
    BLOCK_EMITTERS["l0a"] = emit_l0a
    BLOCK_EMITTERS["l2a"] = emit_l2a
    BLOCK_EMITTERS["l1a"] = emit_l1a


_register()


def build_chain_prog(blocks, first, last):
    p = Prog("_".join(blocks))
    cnames = ["ident_bf", "ones_bf", "ones32_bf", "ident", "zeros64"]
    for b in blocks:
        for cn in BLOCK_CONSTS.get(b, []):
            if cn not in cnames:
                cnames.append(cn)
    core = Core(p, const_names=tuple(cnames))
    xin = p.din("xin", [D, NT], F32 if first else BF16)
    emit_load_res(p, core, xin, fp32_src=first)
    if "res_in" in TAPS:
        ri = p.dout("tap_res_in", [D, 512], F32 if first else BF16)
        for i in range(DT):
            p.dma("gpsimd", ri[128 * i:128 * (i + 1), :],
                  core.res[i][:, 0:512], sem="STG",
                  waits=as_waits(core.res_ready) if i == 0 else ())
    cld = core.load_consts()
    core.res_ready = merge_counts(core.res_ready, cld)
    weight_aps = {}
    for b in blocks:
        for wname in BLOCK_WEIGHTS[b]:
            weight_aps[wname] = p.din(wname, WEIGHT_SHAPES[wname],
                                      WEIGHT_DTYPES.get(wname, F32))
    for b in blocks:
        BLOCK_EMITTERS[b](p, core, weight_aps)
    xout = p.dout("xout", [D, NT], F32 if last else BF16)
    stc = emit_store_res(p, core, xout, fp32_dst=last)
    if _WTD3_ECHO:
        wt = _WTD3_ECHO.pop()
        w3e = p.dout("tap_wtd3end", [wt.I, wt.O], BF16)
        p.dma("gpsimd", w3e[:], wt.wtd[:], sem="STG", waits=as_waits(stc))
        stc = {"STG": p.snap("STG")}
    if DEBUG_ECHO:
        xe = p.dout("xin_echo", [D, 64], F32 if first else BF16)
        p.dma("gpsimd", xe[:], xin[:, 0:64], sem="STG", waits=as_waits(stc))
        for wname in DEBUG_ECHO:
            if wname in weight_aps:
                wa = weight_aps[wname]
                we = p.dout(wname + "_echo", [128, 128],
                            WEIGHT_DTYPES.get(wname, F32))
                p.dma("gpsimd", we[:], wa[0:128, 0:128], sem="STG",
                      waits=as_waits(stc))
        stc = {"STG": p.snap("STG")}
    p.wait("sync", *as_waits(stc))
    nc = p.finish()
    in_names = ["xin"] + list(weight_aps.keys()) + list(core.const_aps.keys())
    return p, nc, core


BLOCK_ORDER = ["l0a", "l0f", "l1a", "l1f", "l2a", "l2f", "l3a", "l3f"]


def kernel(**inputs):
    n_cores = 8
    bpc = B_FULL // n_cores
    x = np.asarray(inputs["L0_attn_x"], dtype=np.float32)
    acts = []
    for c in range(n_cores):
        xc = x[c * bpc:(c + 1) * bpc]
        acts.append(np.ascontiguousarray(xc.reshape(bpc * T_SEQ, D).T))
    for i, blk in enumerate(BLOCK_ORDER):
        first = (i == 0)
        last = (i == len(BLOCK_ORDER) - 1)
        p, nc, core = build_chain_prog([blk], first=first, last=last)
        cdata = core.get_const_data()
        in_maps = []
        for c in range(n_cores):
            m = {"xin": acts[c]}
            for w in BLOCK_WEIGHTS[blk]:
                m[w] = np.asarray(inputs[w])
            m.update(cdata)
            in_maps.append(m)
        res = run_bass_kernel_spmd(nc, in_maps, core_ids=list(range(n_cores)))
        acts = [res.results[c]["xout"] for c in range(n_cores)]
    out = np.empty((B_FULL, T_SEQ, D), np.float32)
    for c in range(n_cores):
        out[c * bpc:(c + 1) * bpc] = np.asarray(acts[c], np.float32).T.reshape(
            bpc, T_SEQ, D)
    return out


# version 03
# speedup vs baseline: 1.0053x; 1.0053x over previous
import math
from contextlib import ExitStack

import numpy as np
import ml_dtypes

import concourse.bass as bass
import concourse.mybir as mybir
from concourse.bass_utils import run_bass_kernel_spmd

F32 = mybir.dt.float32
BF16 = mybir.dt.bfloat16
I32 = mybir.dt.int32
AF = mybir.ActivationFunctionType
ALU = mybir.AluOpType
AX = mybir.AxisListType
BF = ml_dtypes.bfloat16

NCORES = 8
B_PER_CORE = 2
B_FULL = 16
T_SEQ = 2048
NT = B_PER_CORE * T_SEQ
D = 1024
DT = D // 128
TC = 512
NEG = -1.0e30

ENGS = ("sync", "scalar", "vector", "tensor", "gpsimd")


_CONSTS = None


def make_consts():
    global _CONSTS
    if _CONSTS is not None:
        return _CONSTS
    c = {}
    c["ident"] = np.eye(128, dtype=np.float32)
    c["ident_bf"] = np.eye(128, dtype=BF)
    c["ones_bf"] = np.ones((128, 128), dtype=BF)
    c["ones_f32"] = np.ones((128, 128), dtype=np.float32)
    eq4 = np.zeros((128, 16), dtype=np.float32)
    for j in range(4):
        eq4[:, 4 * j:4 * j + 4] = (j < np.arange(4)[None, :])
    c["eqmask4"] = eq4
    c["ones32"] = np.full((32, 128), 1.0 / 32.0, dtype=np.float32)
    c["ones32_bf"] = np.full((32, 128), 1.0 / 32.0, dtype=BF)
    b64 = np.zeros((128, 128), dtype=np.float32)
    b64[:64, :64] = 1.0
    b64[64:, 64:] = 1.0
    c["bones64_bf"] = b64.astype(BF)
    p = np.arange(128)[:, None]
    f = np.arange(128)[None, :]
    c["madd_f_le_p"] = np.where(f <= p, 0.0, NEG).astype(np.float32)
    c["madd_f_lt_p"] = np.where(f < p, 0.0, NEG).astype(np.float32)
    c["madd_f_ge_p"] = np.where(f >= p, 0.0, NEG).astype(np.float32)
    c["madd_f_gt_p"] = np.where(f > p, 0.0, NEG).astype(np.float32)
    c["mul_f_le_p_bf"] = (f <= p).astype(BF)
    c["mul_f_ge_p_bf"] = (f >= p).astype(BF)
    c["mul_f_ge_p"] = (f >= p).astype(np.float32)
    eq = np.zeros((128, 256), dtype=np.float32)
    for j in range(16):
        eq[:, 16 * j:16 * j + 16] = (j < np.arange(16)[None, :])
    c["eqmask16"] = eq
    c["zeros64"] = np.zeros((64, 128), dtype=np.float32)
    c["zeros128b"] = np.zeros((128, 8), dtype=BF)
    hd = 64
    inv = (1.0 / 10000.0 ** (np.arange(0, hd, 2).astype(np.float64) / hd))
    t = np.arange(T_SEQ, dtype=np.float64)
    ang = np.outer(inv, t)
    cosv, sinv = np.cos(ang), np.sin(ang)
    c64 = np.concatenate([cosv, cosv], 0)
    s64 = np.concatenate([-sinv, sinv], 0)
    c["rope0_cos"] = np.tile(c64, (2, 1)).astype(np.float32)
    c["rope0_sin"] = np.tile(s64, (2, 1)).astype(np.float32)
    c["rope0_cos_bf"] = np.tile(c64, (2, 1)).astype(BF)
    c["rope0_sin_bf"] = np.tile(s64, (2, 1)).astype(BF)
    perm = np.zeros((128, 128), dtype=np.float32)
    for m in range(128):
        base, i = (m // 64) * 64, m % 64
        perm[base + (i + 32) % 64, m] = 1.0
    c["rope0_perm_bf"] = perm.astype(BF)
    d3 = 32
    inv3 = (10000.0 ** (-np.arange(0, d3, 2).astype(np.float64) / d3))
    ang3 = np.outer(inv3, t)
    cos3, sin3 = np.cos(ang3), np.sin(ang3)
    c32 = np.zeros((32, T_SEQ))
    s32 = np.zeros((32, T_SEQ))
    c32[0::2] = cos3
    c32[1::2] = cos3
    s32[0::2] = -sin3
    s32[1::2] = sin3
    c["rope3_cos"] = np.tile(c32, (4, 1)).astype(np.float32)
    c["rope3_sin"] = np.tile(s32, (4, 1)).astype(np.float32)
    c["rope3_cos_bf"] = np.tile(c32, (4, 1)).astype(BF)
    c["rope3_sin_bf"] = np.tile(s32, (4, 1)).astype(BF)
    perm3 = np.zeros((128, 128), dtype=np.float32)
    for m in range(128):
        perm3[m ^ 1, m] = 1.0
    c["rope3_perm_bf"] = perm3.astype(BF)
    _CONSTS = c
    return c


class Prog:
    def __init__(self, name="prog"):
        self.nc = bass.Bass("TRN2", target_bir_lowering=False)
        self.stack = ExitStack()
        self.q = {e: [] for e in ENGS}
        self.sems = {}
        self.cnt = {}
        self.n_ops = 0
        self._uid = 0

    def uid(self):
        self._uid += 1
        return self._uid

    def sem(self, name):
        if name not in self.sems:
            self.sems[name] = self.stack.enter_context(self.nc.semaphore(name))
            self.cnt[name] = 0
        return self.sems[name]

    def din(self, name, shape, dtype):
        return self.nc.dram_tensor(name, list(shape), dtype, kind="ExternalInput").ap()

    def dout(self, name, shape, dtype):
        return self.nc.dram_tensor(name, list(shape), dtype, kind="ExternalOutput").ap()

    def dtmp(self, name, shape, dtype):
        return self.nc.dram_tensor(name, list(shape), dtype, kind="Internal").ap()

    def sb(self, name, shape, dtype, stack=None):
        return (stack or self.stack).enter_context(
            self.nc.sbuf_tensor(f"{name}_{self.uid()}", list(shape), dtype))

    def ps(self, name, shape, dtype=F32, stack=None):
        shape = list(shape)
        elem = 4 if dtype == F32 else 2
        per_part = shape[1] * elem
        if per_part < 2048:
            shape = [shape[0], 2048 // elem]
        return (stack or self.stack).enter_context(
            self.nc.psum_tensor(f"{name}_{self.uid()}", list(shape), dtype))

    GUARD_THRESH = 512
    GUARD_ENGS = ("vector", "gpsimd", "scalar")

    def barrier(self):
        snap = [(s, c) for s, c in self.cnt.items() if c > 0]
        self._pending_bar = {eng: list(snap) for eng in ENGS}

    def op(self, eng, meth, *args, waits=(), inc=None, incn=1, **kw):
        sems = self.sems
        waits = list(waits)
        pend = getattr(self, "_pending_bar", None)
        if pend and eng in pend:
            waits = waits + pend.pop(eng)
        if eng in self.GUARD_ENGS and meth != "dma_start":
            g = getattr(self, "_guard", None)
            if g is None:
                g = self._guard = {}
            prev = g.get(eng)
            if prev is not None:
                waits.append(prev)
            small = False
            if args and isinstance(args[0], bass.AP):
                try:
                    small = args[0].free_size() < self.GUARD_THRESH
                except Exception:
                    small = False
            if small:
                if inc is None:
                    inc = {"vector": "V", "scalar": "S", "gpsimd": "G"}[eng]
        for s, _ in waits:
            self.sem(s)
        if inc:
            self.sem(inc)
        wl = tuple((s, c) for (s, c) in waits if c > 0)

        def f(e, meth=meth, args=args, kw=kw, wl=wl, inc=inc, incn=incn):
            for s, c in wl:
                e.wait_ge(sems[s], c)
            r = getattr(e, meth)(*args, **kw)
            if inc:
                r.then_inc(sems[inc], incn)
        self.q[eng].append(f)
        self.n_ops += 1
        if inc:
            self.cnt[inc] += incn
            if eng in self.GUARD_ENGS and meth != "dma_start" and args                     and isinstance(args[0], bass.AP):
                try:
                    if args[0].free_size() < self.GUARD_THRESH:
                        self._guard[eng] = (inc, self.cnt[inc])
                except Exception:
                    pass
            return self.cnt[inc]
        return None

    def wait(self, eng, *waits):
        sems = self.sems
        for s, _ in waits:
            self.sem(s)
        wl = tuple((s, c) for (s, c) in waits if c > 0)

        def f(e, wl=wl):
            for s, c in wl:
                e.wait_ge(sems[s], c)
        self.q[eng].append(f)

    def dma(self, queue, out, in_, sem=None, waits=(), **kw):
        if sem is None:
            sem = {"sync": "LD", "scalar": "LD2", "gpsimd": "LDG"}[queue]
        return self.op(queue, "dma_start", waits=waits, inc=sem, incn=16,
                       out=out, in_=in_, **kw)

    def snap(self, sem):
        self.sem(sem)
        return self.cnt[sem]

    def finish(self):
        nc = self.nc
        with nc.Block() as block:
            for eng in ENGS:
                ops = self.q[eng]
                if not ops:
                    continue
                getattr(block, eng)(lambda e, ops=tuple(ops): [f(e) for f in ops])
        return nc


TAPS = []
_WTD3_ECHO = []
DEBUG_ECHO = []


def tap(p, name, ap, waits, block_eng="vector"):
    if name not in TAPS:
        return
    out = p.dout("tap_" + name, list(ap.shape), ap.dtype)
    stc = p.dma("gpsimd", out[:], ap, sem="STG", waits=list(waits))
    p.wait("sync", ("STG", p.snap("STG")))
    if block_eng:
        g = getattr(p, "_guard", None)
        if g is None:
            g = p._guard = {}
        g[block_eng] = ("STG", stc)


def merge_counts(*dicts):
    out = {}
    for d in dicts:
        for k, v in d.items():
            out[k] = max(out.get(k, 0), v)
    return out


def as_waits(d):
    return [(k, v) for k, v in d.items() if v > 0]


NFSCAL = 48


class Core:
    def __init__(self, p: Prog,
                 const_names=("ident_bf", "ones_bf", "ones32_bf")):
        self.p = p
        self.res = [p.sb(f"res{i}", [128, NT], BF16) for i in range(DT)]
        self.const_aps = {}
        self.const_sb = {}
        cs = make_consts()
        self._cs = cs
        for n in const_names:
            a = cs[n]
            dt = BF16 if a.dtype == BF else F32
            self.const_aps[n] = p.din("c_" + n, list(a.shape), dt)
            self.const_sb[n] = p.sb("cs_" + n, list(a.shape), dt)
        self._cdata = {("c_" + n): np.asarray(cs[n]) for n in const_names}
        self.fs_vals = []
        self.fs_sb = p.sb("fscal", [128, NFSCAL], F32)
        self.const_aps["fscal"] = p.din("c_fscal", [128, NFSCAL], F32)
        self.const_sb["fscal"] = self.fs_sb
        self.res_ready = {}

    def fscal(self, v):
        v = float(v)
        if v not in self.fs_vals:
            assert len(self.fs_vals) < NFSCAL
            self.fs_vals.append(v)
        return self.fs_sb[:, self.fs_vals.index(v):self.fs_vals.index(v) + 1]

    def get_const_data(self):
        d = dict(self._cdata)
        a = np.zeros((128, NFSCAL), np.float32)
        a[:, :len(self.fs_vals)] = np.asarray(self.fs_vals, np.float32)[None, :]
        d["c_fscal"] = a
        return d

    def load_consts(self):
        p = self.p
        for n, ap in self.const_aps.items():
            p.dma("sync", self.const_sb[n][:], ap[:])
        return {"LD": p.snap("LD")}

    def local_const(self, stack, name):
        a = self._cs[name]
        dt = BF16 if a.dtype == BF else F32
        if "c_" + name not in self._cdata:
            self.const_aps[name] = self.p.din("c_" + name, list(a.shape), dt)
            self._cdata["c_" + name] = np.asarray(a)
        t = self.p.sb("lc_" + name, list(a.shape), dt, stack=stack)
        self.p.dma("sync", t[:], self.const_aps[name][:])
        self.res_ready = merge_counts(self.res_ready,
                                      {"LD": self.p.snap("LD")})
        return t


def emit_load_res(p, core, x_ap, fp32_src):
    if fp32_src:
        for i in range(DT):
            p.dma("gpsimd", core.res[i][:], x_ap[128 * i:128 * (i + 1), :])
        core.res_ready = {"LDG": p.snap("LDG")}
    else:
        for i in range(DT):
            p.dma("sync", core.res[i][:], x_ap[128 * i:128 * (i + 1), :])
        core.res_ready = {"LD": p.snap("LD")}
    return core.res_ready


def emit_store_res(p, core, out_ap, fp32_dst):
    q, sem = ("gpsimd", "STG") if fp32_dst else ("sync", "ST")
    w = as_waits(core.res_ready)
    for i in range(DT):
        p.dma(q, out_ap[128 * i:128 * (i + 1), :], core.res[i][:], sem=sem, waits=w)
    return {sem: p.snap(sem)}


class LNPool:

    def __init__(self, p, stack, tag, tc=TC, src_f32=False):
        self.tc = tc
        self.src_f32 = src_f32
        sdt = F32 if src_f32 else BF16
        self.psA = p.ps(f"lnA{tag}", [128, tc], F32, stack=stack)
        self.psB = p.ps(f"lnB{tag}", [128, tc], F32, stack=stack)
        self.sq = [p.sb(f"lnsq{tag}{i}", [128, tc], sdt, stack=stack) for i in range(DT)]
        self.mf = p.sb(f"lnmf{tag}", [128, tc], F32, stack=stack)
        self.qf = p.sb(f"lnqf{tag}", [128, tc], F32, stack=stack)
        self.vf = p.sb(f"lnvf{tag}", [128, tc], F32, stack=stack)
        self.m_bf = p.sb(f"lnmb{tag}", [128, tc], BF16, stack=stack)
        self.r_bf = p.sb(f"lnrb{tag}", [128, tc], BF16, stack=stack)
        self.t_bf = self.sq
        self.prev = {}


def emit_ln(p, core, pool, src_tiles, t0, tn, dst_tiles=None, dst_t0=None,
            res_add=False, res_t0=None, src_ready=(), dst_war=(), eps=1e-5,
            tap_tag=None):
    ones = core.const_sb["ones_f32" if pool.src_f32 else "ones_bf"]
    sl = slice(t0, t0 + tn)
    d0 = dst_t0 if dst_t0 is not None else t0
    dsl = slice(d0, d0 + tn)
    rsl = slice(res_t0 if res_t0 is not None else t0,
                (res_t0 if res_t0 is not None else t0) + tn)
    pr = pool.prev
    src_ready = list(src_ready)
    sq_w = src_ready + as_waits({k: v for k, v in pr.items()
                                 if k in ("T", "V")})
    g1 = None
    for i in range(DT):
        g1 = p.op("gpsimd", "tensor_tensor", pool.sq[i][:, 0:tn],
                  src_tiles[i][:, sl], src_tiles[i][:, sl], ALU.mult,
                  waits=sq_w if i == 0 else (), inc="G" if i == DT - 1 else None)
    mm_w = src_ready + as_waits({k: v for k, v in pr.items() if k == "Vst"})
    for i in range(DT):
        p.op("tensor", "matmul", pool.psA[:, 0:tn], ones[:], src_tiles[i][:, sl],
             start=(i == 0), stop=(i == DT - 1), waits=mm_w if i == 0 else ())
    for i in range(DT):
        last = i == DT - 1
        p.op("tensor", "matmul", pool.psB[:, 0:tn], ones[:], pool.sq[i][:, 0:tn],
             start=(i == 0), stop=last,
             waits=[("G", g1)] if i == 0 else (), inc="T" if last else None)
    tdone = p.cnt["T"]
    if tap_tag:
        tap(p, tap_tag + "_sq0", pool.sq[0][:, 0:tn], [("G", g1)],
            block_eng="gpsimd")
    p.sem("Vst")
    war_mr = as_waits({k: v for k, v in pr.items() if k in ("S", "Gap")})
    p.op("vector", "tensor_scalar_mul", pool.mf[:, 0:tn], pool.psA[:, 0:tn],
         1.0 / D, waits=[("T", tdone)] + war_mr)
    p.op("vector", "scalar_tensor_tensor", pool.vf[:, 0:tn], pool.mf[:, 0:tn],
         -1.0, pool.mf[:, 0:tn], ALU.mult, ALU.mult)
    p.op("vector", "tensor_scalar", pool.qf[:, 0:tn], pool.psB[:, 0:tn],
         1.0 / D, None, ALU.mult, inc="Vst")
    vst = p.cnt["Vst"]
    if tap_tag:
        tap(p, tap_tag + "_mf", pool.mf[:, 0:tn], [("Vst", vst)])
        tap(p, tap_tag + "_qf", pool.qf[:, 0:tn], [("Vst", vst)])
    if pool.src_f32:
        p.op("vector", "tensor_add", pool.vf[:, 0:tn], pool.vf[:, 0:tn],
             pool.qf[:, 0:tn], inc="V")
    else:
        p.op("vector", "tensor_add", pool.vf[:, 0:tn], pool.vf[:, 0:tn],
             pool.qf[:, 0:tn])
        p.op("vector", "tensor_copy", pool.m_bf[:, 0:tn], pool.mf[:, 0:tn], inc="V")
    vm = p.cnt["V"]
    if tap_tag:
        tap(p, tap_tag + "_vf", pool.vf[:, 0:tn], [("V", vm)])
    p.op("scalar", "activation", pool.vf[:, 0:tn], pool.vf[:, 0:tn], AF.Sqrt,
         bias=core.fscal(eps), waits=[("V", vm)], inc="S")
    sdone = p.cnt["S"]
    if pool.src_f32:
        p.op("vector", "reciprocal", pool.qf[:, 0:tn], pool.vf[:, 0:tn],
             waits=[("S", sdone)], inc="V")
    else:
        p.op("vector", "reciprocal", pool.qf[:, 0:tn], pool.vf[:, 0:tn],
             waits=[("S", sdone)])
        p.op("vector", "tensor_copy", pool.r_bf[:, 0:tn], pool.qf[:, 0:tn], inc="V")
    vr = p.cnt["V"]
    if tap_tag:
        tap(p, tap_tag + "_rbf",
            (pool.qf if pool.src_f32 else pool.r_bf)[:, 0:tn], [("V", vr)])
        tap(p, tap_tag + "_mbf",
            (pool.mf if pool.src_f32 else pool.m_bf)[:, 0:tn], [("V", vr)])
    dst_war = list(dst_war)
    m_op = pool.mf if pool.src_f32 else pool.m_bf
    r_op = pool.qf if pool.src_f32 else pool.r_bf
    if pool.src_f32:
        pass
    vlast = glast = None
    for i in range(DT):
        eng = "vector" if i % 2 == 0 else "gpsimd"
        w = []
        if i < 2:
            w += [("V", vr), ("T", tdone)]
            w += dst_war
        p.op(eng, "tensor_sub", pool.t_bf[i][:, 0:tn], src_tiles[i][:, sl],
             m_op[:, 0:tn], waits=w)
        if res_add:
            if pool.src_f32:
                cast_t = pool.m_bf if eng == "vector" else pool.r_bf
                p.op(eng, "tensor_tensor", cast_t[:, 0:tn],
                     pool.t_bf[i][:, 0:tn], r_op[:, 0:tn], ALU.mult)
                r = p.op(eng, "tensor_add", core.res[i][:, rsl], core.res[i][:, rsl],
                         cast_t[:, 0:tn],
                         inc="V" if eng == "vector" else "Gap")
            else:
                p.op(eng, "tensor_mul", pool.t_bf[i][:, 0:tn], pool.t_bf[i][:, 0:tn],
                     r_op[:, 0:tn])
                r = p.op(eng, "tensor_add", core.res[i][:, rsl], core.res[i][:, rsl],
                         pool.t_bf[i][:, 0:tn],
                         inc="V" if eng == "vector" else "Gap")
        else:
            r = p.op(eng, "tensor_mul", dst_tiles[i][:, dsl], pool.t_bf[i][:, 0:tn],
                     r_op[:, 0:tn],
                     inc="V" if eng == "vector" else "Gap")
        if eng == "vector":
            vlast = r
        else:
            glast = r
    done = {"V": vlast, "Gap": glast}
    pool.prev = {"T": tdone, "Vst": vst, "S": sdone, "Gap": glast, "V": vlast}
    return done


class WScr:

    def __init__(self, p, name, w_ap, O, I):
        self.scr = p.dtmp(f"scr_{name}_{p.uid()}", [O, I], BF16)
        p.dma("gpsimd", self.scr[:], w_ap[:])
        self.done = p.snap("LDG")
        self.O, self.I = O, I


def load_wT(p, wscr, stack, o0=0, osz=None, sem="LD", tag="w", bufs=None,
            extra_waits=()):
    osz = osz or wscr.O
    I = wscr.I
    tiles = bufs or [p.sb(f"wT_{tag}{i}", [128, osz], BF16, stack=stack)
                     for i in range(I // 128)]
    w = [("LDG", wscr.done)] + list(extra_waits)
    for i in range(I // 128):
        p.dma("sync", tiles[i][:, 0:osz],
              wscr.scr[o0:o0 + osz, 128 * i:128 * (i + 1)],
              transpose=True, waits=w if i == 0 else list(extra_waits), sem=sem)
    return tiles, {sem: p.snap(sem)}


def load_w_nat(p, name, w_ap, I, O, stack, sem="LDG", bufs=None, extra_waits=()):
    tiles = bufs or [p.sb(f"wN_{name}{i}", [128, O], BF16, stack=stack)
                     for i in range(I // 128)]
    for i in range(I // 128):
        p.dma("gpsimd", tiles[i][:, 0:O], w_ap[128 * i:128 * (i + 1), :],
              sem=sem, waits=list(extra_waits))
    return tiles, {sem: p.snap(sem)}


def emit_l3f(p, core, w_ap):
    with ExitStack() as st:
        wscr = WScr(p, "l3f", w_ap, D, D)
        wT, wld = load_wT(p, wscr, st, tag="l3f")
        wsT = [p.sb(f"l3f_ws{i}", [128, D], BF16, stack=st) for i in range(DT)]
        for i in range(DT):
            p.op("scalar", "activation", wsT[i][:], wT[i][:], AF.Sign,
                 waits=as_waits(wld) if i == 0 else (), inc="S")
        sdone = {"S": p.cnt["S"]}

        lnp = LNPool(p, st, "A")
        lnp2 = LNPool(p, st, "B")
        xh = [p.sb(f"l3f_xh{i}", [128, TC], BF16, stack=st) for i in range(DT)]
        yb = [p.sb(f"l3f_y{i}", [128, TC], BF16, stack=st) for i in range(DT)]
        psy = [p.ps(f"l3f_ps{i}", [128, TC], F32, stack=st) for i in range(4)]
        bank_free = {i: {} for i in range(4)}
        xh_war = {}
        yb_war = {}
        res_r = as_waits(core.res_ready)
        done = {}
        for tc in range(NT // TC):
            t0 = tc * TC
            d1 = emit_ln(p, core, lnp, core.res, t0, TC, dst_tiles=xh, dst_t0=0,
                         src_ready=res_r if tc == 0 else (),
                         dst_war=as_waits(xh_war))
            xw = as_waits(merge_counts(d1, sdone))
            last_mm = 0
            evs = {}
            for o in range(DT):
                bank = psy[o % 4]
                for i in range(DT):
                    w = []
                    if i == 0:
                        w = xw + as_waits(bank_free[o % 4])
                    last = i == DT - 1
                    p.op("tensor", "matmul", bank[:],
                         wsT[i][:, 128 * o:128 * (o + 1)], xh[i][:, 0:TC],
                         start=(i == 0), stop=last, waits=w,
                         inc="T" if last else None)
                tmm = p.cnt["T"]
                last_mm = tmm
                vv = p.op("scalar", "activation", yb[o][:, 0:TC], bank[:],
                          AF.Copy, waits=[("T", tmm)] + as_waits(yb_war), inc="S")
                bank_free[o % 4] = {"S": vv}
                evs[o] = vv
            xh_war = {"T": last_mm}
            d2 = emit_ln(p, core, lnp2, yb, 0, TC, res_add=True, res_t0=t0,
                         src_ready=[("S", evs[DT - 1])])
            yb_war = {"T": lnp2.prev["T"], "Gap": d2["Gap"], "V": d2["V"]}
            done = d2
        core.res_ready = merge_counts(core.res_ready, done)



def load_scale_cols(p, core, vec_ap, n, stack, tag, ps_shared=None, extra_waits=()):
    nt = n // 128
    nr = max(nt, 32)
    rows = p.sb(f"scv_{tag}", [nr, 128], F32, stack=stack)
    ps = ps_shared if ps_shared is not None else p.ps(
        f"scp_{tag}", [128, max(nt, 1)], F32, stack=stack)
    cols = p.sb(f"scc_{tag}", [128, max(nt, 1)], F32, stack=stack)
    kk = min(nr, 32)
    vz = 0
    if nt < kk:
        p.op("vector", "tensor_copy", rows[0:kk, :],
             core.const_sb["zeros64"][0:kk, 0:128],
             waits=list(extra_waits) + as_waits(core.res_ready))
        vz = p.cnt["V"]
    ld = p.dma("sync", rows[0:nt, :], vec_ap.rearrange("(a b) -> a b", b=128),
               waits=list(extra_waits) + ([("V", vz)] if vz else []))
    p.op("tensor", "matmul", ps[:, 0:nt], rows[0:kk, :] if nt <= 32 else rows[:],
         core.const_sb["ident"][0:(kk if nt <= 32 else nr), 0:nt],
         start=True, stop=True,
         waits=[("LD", ld)] + list(extra_waits), inc="T")
    p.op("vector", "tensor_copy", cols[:], ps[:, 0:nt],
         waits=[("T", p.cnt["T"])], inc="V")
    return cols, {"V": p.cnt["V"]}


def mm_group(p, psum_ap, pairs, waits_first=(), inc_last="T"):
    n = len(pairs)
    for i, (l, r) in enumerate(pairs):
        p.op("tensor", "matmul", psum_ap, l, r,
             start=(i == 0), stop=(i == n - 1) and inc_last is not None,
             waits=waits_first if i == 0 else (),
             inc=inc_last if (i == n - 1 and inc_last) else None)
    return p.cnt[inc_last] if inc_last else None


class WTd:

    def __init__(self, p, name, w_ap, O, I, staging, staging_free):
        self.O, self.I = O, I
        scr = p.dtmp(f"scr_{name}_{p.uid()}", [O, I], BF16)
        p.dma("gpsimd", scr[:], w_ap[:])
        cast_done = p.snap("LDG")
        self.wtd = p.dtmp(f"wtd_{name}_{p.uid()}", [I, O], BF16)
        ns = len(staging)
        ocap = staging[0].shape[1]
        k = 0
        for i in range(I // 128):
            for o0 in range(0, O, ocap):
                osz = min(ocap, O - o0)
                sbuf = staging[k % ns]
                w = [("LDG", cast_done)] + as_waits(staging_free[k % ns])
                ld = p.dma("sync", sbuf[:, 0:osz],
                           scr[o0:o0 + osz, 128 * i:128 * (i + 1)],
                           transpose=True, waits=w)
                stc = p.dma("scalar",
                            self.wtd[128 * i:128 * (i + 1), o0:o0 + osz],
                            sbuf[:, 0:osz], sem="LD2", waits=[("LD", ld)])
                staging_free[k % ns] = {"LD2": stc}
                k += 1
        self.ready = {"LD2": p.snap("LD2")}


def load_wslice(p, wtd: WTd, o0, osz, buf, col0=0, sem="LD", extra_waits=(),
                i0=0, nI=None):
    if nI is None:
        nI = wtd.I // 128
    w = as_waits(wtd.ready) + list(extra_waits)
    G = 8
    for g in range(0, nI, G):
        n = min(G, nI - g)
        out_ap = buf[:, col0 + g * osz: col0 + (g + n) * osz].rearrange(
            "p (i o) -> p i o", o=osz)
        in_ap = wtd.wtd[128 * (i0 + g):128 * (i0 + g + n), o0:o0 + osz].rearrange(
            "(i p) o -> p i o", p=128)
        p.dma("sync", out_ap, in_ap, sem=sem, waits=w if g == 0 else ())
    return {sem: p.snap(sem)}


def emit_l1f(p, core, aps):
    HID = 3072
    with ExitStack() as st:
        stg_stack = ExitStack()
        staging = [p.sb(f"l1f_stg{i}", [128, HID], BF16, stack=stg_stack)
                   for i in range(2)]
        stg_free = {0: {}, 1: {}}
        wtd_g = WTd(p, "l1fg", aps["L1_ffn_w_gate"], HID, D, staging, stg_free)
        wtd_u = WTd(p, "l1fu", aps["L1_ffn_w_up"], HID, D, staging, stg_free)
        stg_rel = [("LD2", p.snap("LD2"))]
        stg_stack.close()
        scr_d = WScr(p, "l1fd", aps["L1_ffn_w_down"], D, HID)
        wdT, wd_ld = load_wT(p, scr_d, st, tag="l1fd", extra_waits=stg_rel)
        lnp = LNPool(p, st, "A")
        xh = [p.sb(f"l1f_xh{i}", [128, TC], BF16, stack=st) for i in range(DT)]
        yb = [p.sb(f"l1f_y{i}", [128, TC], BF16, stack=st) for i in range(DT)]
        h_all = [p.sb(f"l1f_h{i}", [128, TC], BF16, stack=st) for i in range(HID // 128)]
        wgb = p.sb("l1f_wgb", [128, DT * 1024], BF16, stack=st)
        wub = p.sb("l1f_wub", [128, DT * 1024], BF16, stack=st)
        gps = [p.ps(f"l1f_g{i}", [128, TC], F32, stack=st) for i in range(2)]
        ups = [p.ps(f"l1f_u{i}", [128, TC], F32, stack=st) for i in range(2)]
        yps = [p.ps(f"l1f_yp{i}", [128, TC], F32, stack=st) for i in range(2)]
        gbuf = [p.sb(f"l1f_gb{i}", [128, TC], BF16, stack=st) for i in range(2)]
        s127 = core.fscal(1.0 / 127.0)
        res_r = as_waits(core.res_ready)
        wg_free = {}
        bank_free = {0: {}, 1: {}}
        ybank_free = {0: {}, 1: {}}
        yb_war = {}
        xh_war = {}
        done = {}

        def wg_slice(i, ot):
            return wgb[:, 1024 * i + 128 * ot:1024 * i + 128 * (ot + 1)]

        def wu_slice(i, ot):
            return wub[:, 1024 * i + 128 * ot:1024 * i + 128 * (ot + 1)]

        for tc in range(NT // TC):
            t0 = tc * TC
            d1 = emit_ln(p, core, lnp, core.res, t0, TC, dst_tiles=xh, dst_t0=0,
                         src_ready=(res_r + stg_rel) if tc == 0 else (),
                         dst_war=as_waits(xh_war) + (stg_rel if tc == 0 else []))
            xw = as_waits(d1)
            last_mm = 0
            hv = None
            for sl in range(3):
                g_ld = load_wslice(p, wtd_g, 1024 * sl, 1024, wgb,
                                   extra_waits=as_waits(wg_free) + stg_rel)
                u_ld = load_wslice(p, wtd_u, 1024 * sl, 1024, wub,
                                   extra_waits=as_waits(wg_free) + stg_rel)
                wld = as_waits(merge_counts(g_ld, u_ld))
                for ot in range(8):
                    og = sl * 8 + ot
                    bk = ot % 2
                    tg = mm_group(p, gps[bk][:],
                                  [(wg_slice(i, ot), xh[i][:, 0:TC]) for i in range(DT)],
                                  waits_first=xw + wld + as_waits(bank_free[bk]))
                    sg = p.op("scalar", "activation", gbuf[bk][:], gps[bk][:],
                              AF.Silu, scale=s127, waits=[("T", tg)], inc="S")
                    tu = mm_group(p, ups[bk][:],
                                  [(wu_slice(i, ot), xh[i][:, 0:TC]) for i in range(DT)],
                                  waits_first=as_waits(bank_free[bk]))
                    su = p.op("scalar", "activation", h_all[og][:], ups[bk][:],
                              AF.Copy, scale=s127, waits=[("T", tu)], inc="S")
                    hv = p.op("vector", "tensor_mul", h_all[og][:], h_all[og][:],
                              gbuf[bk][:], waits=[("S", su)], inc="V")
                    bank_free[bk] = {"V": hv, "S": su}
                    last_mm = tu
                wg_free = {"T": last_mm}
            ydone = {}
            for ot in range(DT):
                bk = ot % 2
                ty = mm_group(p, yps[bk][:],
                              [(wdT[f][:, 128 * ot:128 * (ot + 1)], h_all[f][:])
                               for f in range(HID // 128)],
                              waits_first=[("V", hv)] + as_waits(wd_ld)
                                          + as_waits(ybank_free[bk]))
                sy = p.op("scalar", "activation", yb[ot][:], yps[bk][:],
                          AF.Copy, scale=s127,
                          waits=[("T", ty)] + as_waits(yb_war), inc="S")
                ybank_free[bk] = {"S": sy}
                ydone = {"S": sy, "T": ty}
            xh_war = {"T": ydone["T"]}
            d2 = emit_ln(p, core, lnp, yb, 0, TC, res_add=True, res_t0=t0,
                         src_ready=as_waits(ydone))
            yb_war = {"T": lnp.prev["T"], "Gap": d2["Gap"], "V": d2["V"]}
            done = d2
        core.res_ready = merge_counts(core.res_ready, done)


def emit_l2f(p, core, aps):
    HID = 4096
    HT = HID // 128
    with ExitStack() as st:
        stg_stack = ExitStack()
        staging = [p.sb(f"l2f_stg{i}", [128, HID], BF16, stack=stg_stack)
                   for i in range(2)]
        stg_free = {0: {}, 1: {}}
        wtd1 = WTd(p, "l2f1", aps["L2_ffn_w1_w"], HID, D, staging, stg_free)
        wtd2 = WTd(p, "l2f2", aps["L2_ffn_w2_w"], HID, D, staging, stg_free)
        wtdh1 = WTd(p, "l2fh1", aps["L2_ffn_h1_w"], HID, HID, staging, stg_free)
        wtdh2 = WTd(p, "l2fh2", aps["L2_ffn_h2_w"], HID, HID, staging, stg_free)
        wtd3 = WTd(p, "l2f3", aps["L2_ffn_w3_w"], D, HID, staging, stg_free)
        stg_rel = [("LD2", p.snap("LD2"))]
        if "l2f_wtd3end" in TAPS:
            _WTD3_ECHO.append(wtd3)
        stg_stack.close()
        scps = p.ps("l2f_scl", [128, 32], F32, stack=st)
        s1c, s1d = load_scale_cols(p, core, aps["L2_ffn_w1_s"], HID, st, "s1",
                                   scps, extra_waits=stg_rel)
        s2c, s2d = load_scale_cols(p, core, aps["L2_ffn_w2_s"], HID, st, "s2", scps,
                                   extra_waits=as_waits(s1d))
        h1c, h1d = load_scale_cols(p, core, aps["L2_ffn_h1_s"], HID, st, "h1", scps,
                                   extra_waits=as_waits(s2d))
        h2c, h2d = load_scale_cols(p, core, aps["L2_ffn_h2_s"], HID, st, "h2", scps,
                                   extra_waits=as_waits(h1d))
        s3c, s3d = load_scale_cols(p, core, aps["L2_ffn_w3_s"], D, st, "s3", scps,
                                   extra_waits=as_waits(h2d))
        scl_d = as_waits(merge_counts(s1d, s2d, h1d, h2d, s3d))
        tap(p, "l2f_s1c", s1c[:], scl_d)
        tap(p, "l2f_s3c", s3c[:], scl_d)

        lnp = LNPool(p, st, "A")
        xh = [p.sb(f"l2f_xh{i}", [128, TC], BF16, stack=st) for i in range(DT)]
        ha = [p.sb(f"l2f_ha{i}", [128, TC], BF16, stack=st) for i in range(HT)]
        hb = [p.sb(f"l2f_hb{i}", [128, TC], BF16, stack=st) for i in range(HT)]
        yb = hb[0:DT]
        w12 = p.sb("l2f_w12", [128, DT * 512 * 2], BF16, stack=st)
        whb = p.sb("l2f_whb", [128, HT * 512], BF16, stack=st)
        ps = [p.ps(f"l2f_ps{i}", [128, TC], F32, stack=st) for i in range(4)]
        bank_free = {i: {} for i in range(4)}
        whb_free = {}
        w12_free = {}
        xh_war = {}
        yb_war = {}
        res_r = as_waits(core.res_ready)
        done = {}
        for tc in range(NT // TC):
            t0 = tc * TC
            d1 = emit_ln(p, core, lnp, core.res, t0, TC, dst_tiles=xh, dst_t0=0,
                         src_ready=(res_r + stg_rel) if tc == 0 else (),
                         dst_war=as_waits(xh_war) + (stg_rel if tc == 0 else []))
            xw = as_waits(d1)
            if tc == 0 and "l2f_xh0" in TAPS:
                xo = p.dout("tap_l2f_xh0", [128, TC], BF16)
                p.dma("gpsimd", xo[:], xh[0][:, 0:TC], sem="STG", waits=xw)
            hv = None
            last_mm = 0
            for sl in range(DT):
                a_ld = load_wslice(p, wtd1, TC * sl, TC, w12, col0=0,
                                   extra_waits=as_waits(w12_free) + stg_rel)
                b_ld = load_wslice(p, wtd2, TC * sl, TC, w12, col0=DT * 512,
                                   extra_waits=as_waits(w12_free) + stg_rel)
                wld = as_waits(merge_counts(a_ld, b_ld))
                for ot in range(4):
                    og = sl * 4 + ot
                    bka, bkb = ot % 2, 2 + ot % 2
                    tg = mm_group(p, ps[bka][:],
                                  [(w12[:, 512 * i + 128 * ot:512 * i + 128 * (ot + 1)],
                                    xh[i][:, 0:TC]) for i in range(DT)],
                                  waits_first=xw + wld + scl_d + as_waits(bank_free[bka]))
                    sg = p.op("scalar", "activation", ha[og][:], ps[bka][:], AF.Silu,
                              scale=s1c[:, og:og + 1], waits=[("T", tg)], inc="S")
                    tu = mm_group(p, ps[bkb][:],
                                  [(w12[:, DT * 512 + 512 * i + 128 * ot:
                                        DT * 512 + 512 * i + 128 * (ot + 1)],
                                    xh[i][:, 0:TC]) for i in range(DT)],
                                  waits_first=as_waits(bank_free[bkb]))
                    su = p.op("scalar", "activation", hb[og][:], ps[bkb][:], AF.Copy,
                              scale=s2c[:, og:og + 1],
                              waits=[("T", tu)] +
                                    (as_waits(yb_war) if og == 0 else []),
                              inc="S")
                    hv = p.op("vector", "tensor_mul", ha[og][:], ha[og][:], hb[og][:],
                              waits=[("S", sg), ("S", su)], inc="V")
                    bank_free[bka] = {"V": hv}
                    bank_free[bkb] = {"V": hv}
                    last_mm = tu
                w12_free = {"T": last_mm}
            hw = [("V", hv)]
            if tc == 0:
                tap(p, "l2f_ha1", ha[0][:], hw)
                tap(p, "l2f_ha1b", ha[31][:], hw)
            for (wtdH, srcb, dstb, scol) in ((wtdh1, ha, hb, h1c), (wtdh2, hb, ha, h2c)):
                sv = None
                lm = 0
                for sl in range(DT):
                    w_ld = load_wslice(p, wtdH, TC * sl, TC, whb,
                                       extra_waits=as_waits(whb_free) + stg_rel)
                    wld2 = as_waits(w_ld)
                    for ot in range(4):
                        og = sl * 4 + ot
                        bk = ot % 2
                        tg = mm_group(p, ps[bk][:],
                                      [(whb[:, 512 * i + 128 * ot:512 * i + 128 * (ot + 1)],
                                        srcb[i][:, 0:TC]) for i in range(HT)],
                                      waits_first=hw + wld2 + as_waits(bank_free[bk]))
                        sv = p.op("scalar", "activation", dstb[og][:], ps[bk][:], AF.Gelu,
                                  scale=scol[:, og:og + 1], waits=[("T", tg)], inc="S")
                        bank_free[bk] = {"S": sv}
                        lm = tg
                    whb_free = {"T": lm}
                hw = [("S", sv)]
            if tc == 0:
                tap(p, "l2f_g2" if dstb is hb else "l2f_g3", dstb[17][:], hw,
                    block_eng="scalar")
            ydone = {}
            lm = 0
            for half in range(2):
                w_ld = load_wslice(p, wtd3, TC * half, TC, whb,
                                   extra_waits=as_waits(whb_free) + stg_rel)
                wld3 = as_waits(w_ld)
                for ot in range(4):
                    og = half * 4 + ot
                    bk = ot % 2
                    ty = mm_group(p, ps[bk][:],
                                  [(whb[:, 512 * i + 128 * ot:512 * i + 128 * (ot + 1)],
                                    ha[i][:, 0:TC]) for i in range(HT)],
                                  waits_first=hw + wld3 + as_waits(bank_free[bk]))
                    sy = p.op("scalar", "activation", yb[og][:], ps[bk][:], AF.Copy,
                              scale=s3c[:, og:og + 1],
                              waits=[("T", ty)] + as_waits(yb_war), inc="S")
                    bank_free[bk] = {"S": sy}
                    lm = ty
                    ydone = {"S": sy, "T": ty}
                whb_free = {"T": lm}
            xh_war = {"T": last_mm}
            if tc == 0:
                tap(p, "l2f_yb", yb[3][:], as_waits(ydone), block_eng="scalar")
            if "l2f_pertc" in TAPS:
                yo = p.dout(f"tap_l2f_y{tc}", [128, TC], BF16)
                p.dma("gpsimd", yo[:], yb[3][:, 0:TC], sem="STG",
                      waits=as_waits(ydone))
                yo0 = p.dout(f"tap_l2f_ya{tc}", [128, TC], BF16)
                p.dma("gpsimd", yo0[:], yb[0][:, 0:TC], sem="STG",
                      waits=as_waits(ydone))
                yo1 = p.dout(f"tap_l2f_yc{tc}", [128, TC], BF16)
                p.dma("gpsimd", yo1[:], yb[1][:, 0:TC], sem="STG",
                      waits=as_waits(ydone))
            d2 = emit_ln(p, core, lnp, yb, 0, TC, res_add=True, res_t0=t0,
                         src_ready=as_waits(ydone),
                         tap_tag="l2d2" if tc == 0 else None)
            if "l2f_pertc" in TAPS:
                ro = p.dout(f"tap_l2f_r{tc}", [128, TC], BF16)
                p.dma("gpsimd", ro[:], core.res[0][:, t0:t0 + TC], sem="STG",
                      waits=as_waits(d2))
            yb_war = {"T": lnp.prev["T"], "Gap": d2["Gap"], "V": d2["V"]}
            done = d2
        core.res_ready = merge_counts(core.res_ready, done)



def emit_l0f(p, core, aps):
    E, Fd = 16, 512
    QT = 1024
    with ExitStack() as st:
        stg_stack = ExitStack()
        staging = [p.sb(f"l0f_stg{i}", [128, 8192], BF16, stack=stg_stack)
                   for i in range(2)]
        stg_free = {0: {}, 1: {}}
        w1_flat = aps["L0_ffn_w1_e"].rearrange("e f h -> (e f) h")
        w3_flat = aps["L0_ffn_w3_e"].rearrange("e f h -> (e f) h")
        w2_flat = aps["L0_ffn_w2_e"].rearrange("e h f -> (e h) f")
        wtd1 = WTd(p, "l0f1", w1_flat, E * Fd, D, staging, stg_free)
        wtd3 = WTd(p, "l0f3", w3_flat, E * Fd, D, staging, stg_free)
        wtd2 = WTd(p, "l0f2", w2_flat, E * D, Fd, staging, stg_free)
        wtdS1 = WTd(p, "l0fs1", aps["L0_ffn_w1_s"], D, D, staging, stg_free)
        wtdS3 = WTd(p, "l0fs3", aps["L0_ffn_w3_s"], D, D, staging, stg_free)
        wtdS2 = WTd(p, "l0fs2", aps["L0_ffn_w2_s"], D, D, staging, stg_free)
        stg_rel = [("LD2", p.snap("LD2"))]
        stg_stack.close()

        xhat_d = p.dtmp("l0f_xhat", [D, NT], BF16)
        comb_d = p.dtmp("l0f_combd", [NT, 128], BF16)
        comb = p.sb("l0f_comb", [128, 32 * 16], F32, stack=st)
        comb_fm = p.sb("l0f_combfm", [128, NT], BF16, stack=st)
        wg_nat = p.sb("l0f_wgn", [16, D], BF16, stack=st)
        wgT = [p.sb(f"l0f_wgt{i}", [128, 16], BF16, stack=st) for i in range(DT)]
        ldg = p.dma("gpsimd", wg_nat[:], aps["L0_ffn_w_gate"][:],
                    waits=stg_rel)
        with ExitStack() as stA:
            lnp = LNPool(p, stA, "A")
            xh = [p.sb(f"l0f_xh{i}", [128, TC], BF16, stack=stA) for i in range(DT)]
            ps_sc = p.ps("l0f_psc", [128, 128], F32, stack=stA)
            ps_wgt = p.ps("l0f_pwg", [128, 16], BF16, stack=stA)
            sc = p.sb("l0f_sc", [128, 16], F32, stack=stA)
            g4 = p.sb("l0f_g4", [128, 16], F32, stack=stA)
            t4a = p.sb("l0f_t4a", [128, 4], F32, stack=stA)
            t4b = p.sb("l0f_t4b", [128, 4], F32, stack=stA)
            gs = p.sb("l0f_gs", [128, 4], F32, stack=stA)
            rk4 = p.sb("l0f_rk4", [128, 4], F32, stack=stA)
            eqv = p.sb("l0f_eqv", [128, 16], F32, stack=stA)
            rk = p.sb("l0f_rk", [128, 16], F32, stack=stA)
            msk = p.sb("l0f_msk", [128, 16], F32, stack=stA)
            dcol = p.sb("l0f_dcol", [128, 1], F32, stack=stA)
            eqm16 = core.local_const(stA, "eqmask16")
            eqm4 = core.local_const(stA, "eqmask4")
            for i in range(DT):
                p.op("tensor", "transpose", ps_wgt[:, 0:16],
                     wg_nat[0:16, 128 * i:128 * (i + 1)],
                     core.const_sb["ident_bf"][0:16, 0:16],
                     waits=[("LDG", ldg)] if i == 0
                           else [("S", p.cnt["S"])], inc="T")
                p.op("scalar", "activation", wgT[i][:], ps_wgt[:, 0:16],
                     AF.Copy, waits=[("T", p.cnt["T"])], inc="S")
            wgT_done = p.cnt["S"]
            res_r = as_waits(core.res_ready)
            xh_war = {}
            sc_prev = {}
            for tc in range(NT // TC):
                t0 = tc * TC
                d1 = emit_ln(p, core, lnp, core.res, t0, TC, dst_tiles=xh, dst_t0=0,
                             src_ready=(res_r + stg_rel) if tc == 0 else (),
                             dst_war=as_waits(xh_war) + (stg_rel if tc == 0 else []))
                xw = as_waits(d1)
                xst = None
                for i in range(DT):
                    xst = p.dma("scalar", xhat_d[128 * i:128 * (i + 1), t0:t0 + TC],
                                xh[i][:, 0:TC], sem="LD2", waits=xw if i == 0 else ())
                for tt in range(4):
                    gt = tc * 4 + tt
                    ccols = slice(16 * gt, 16 * gt + 16)
                    tmm = mm_group(p, ps_sc[0:128, 0:16],
                                   [(xh[i][:, 128 * tt:128 * (tt + 1)], wgT[i][:])
                                    for i in range(DT)],
                                   waits_first=xw + [("S", wgT_done)]
                                               + as_waits(sc_prev))
                    p.op("scalar", "activation", sc[:], ps_sc[0:128, 0:16],
                         AF.Sigmoid, waits=[("T", tmm)], inc="S")
                    sw = [("S", p.cnt["S"])]
                    v = lambda *a, **k: p.op("vector", *a, **k)
                    _sb = sc[:, 0:16]
                    def _sv(j):
                        return bass.AP(_sb.tensor, _sb.offset + j,
                                       [list(_sb.ap[0]), [4, 4]])
                    v("tensor_copy", g4[:, 0:4], _sv(0), waits=sw)
                    v("tensor_copy", g4[:, 4:8], _sv(1))
                    v("tensor_copy", g4[:, 8:12], _sv(2))
                    v("tensor_copy", g4[:, 12:16], _sv(3))
                    ga, gb, gc, gd = (g4[:, 0:4], g4[:, 4:8], g4[:, 8:12], g4[:, 12:16])
                    v("tensor_max", t4a[:], ga, gb)
                    v("tensor_max", t4b[:], gc, gd)
                    v("tensor_add", t4a[:], t4a[:], t4b[:])
                    v("tensor_add", gs[:], ga, gb)
                    v("tensor_max", gs[:], gs[:], t4a[:])
                    v("tensor_add", t4b[:], gc, gd)
                    v("tensor_max", gs[:], gs[:], t4b[:])
                    v("memset", rk4[:], 0.0)
                    for j in range(4):
                        v("scalar_tensor_tensor", rk4[:], gs[:], gs[:, j:j + 1],
                          rk4[:], ALU.is_lt, ALU.add)
                        v("tensor_single_scalar", t4a[:], gs[:], gs[:, j:j + 1],
                          ALU.is_equal)
                        v("tensor_mul", t4a[:], t4a[:], eqm4[:, 4 * j:4 * j + 4])
                        v("tensor_add", rk4[:], rk4[:], t4a[:])
                    v("tensor_single_scalar", rk4[:], rk4[:], 2.0, ALU.is_lt)
                    _b = rk4[:, 0:4]
                    allow_rep = bass.AP(_b.tensor, _b.offset,
                                        [list(_b.ap[0]), [1, 4], [0, 4]])
                    v("tensor_copy", eqv[:], allow_rep)
                    v("tensor_single_scalar", rk[:], eqv[:], 1.0, ALU.subtract)
                    v("tensor_single_scalar", rk[:], rk[:], 1e30, ALU.mult)
                    v("tensor_mul", msk[:], sc[:], eqv[:])
                    v("tensor_add", msk[:], msk[:], rk[:])
                    v("memset", rk[:], 0.0)
                    for j in range(16):
                        v("scalar_tensor_tensor", rk[:], msk[:], msk[:, j:j + 1],
                          rk[:], ALU.is_lt, ALU.add)
                        v("tensor_single_scalar", eqv[:], msk[:], msk[:, j:j + 1],
                          ALU.is_equal)
                        v("tensor_mul", eqv[:], eqv[:], eqm16[:, 16 * j:16 * j + 16])
                        v("tensor_add", rk[:], rk[:], eqv[:])
                    if gt == 0:
                        tap(p, "r_gs", gs[:], [("GDv", p.cnt.get("GDv", 0))])
                        tap(p, "r_rk4", rk4[:], [("GDv", p.cnt.get("GDv", 0))])
                        tap(p, "r_msk", msk[:], [("GDv", p.cnt.get("GDv", 0))])
                        tap(p, "r_rk", rk[:], [("GDv", p.cnt.get("GDv", 0))])
                        tap(p, "r_sc", sc[:], [("GDv", p.cnt.get("GDv", 0))])
                    v("tensor_single_scalar", rk[:], rk[:], 4.0, ALU.is_lt)
                    v("tensor_mul", sc[:], sc[:], rk[:])
                    v("reduce_sum", dcol[:], sc[:], AX.X)
                    v("tensor_single_scalar", dcol[:], dcol[:], 1e-6, ALU.add)
                    v("reciprocal", dcol[:], dcol[:])
                    v("tensor_scalar_mul", comb[:, ccols], sc[:], dcol[:, 0:1],
                      inc="V")
                    vc = p.cnt["V"]
                    p.dma("gpsimd", comb_d[128 * gt:128 * (gt + 1), 0:16],
                          comb[:, ccols], sem="LDG", waits=[("V", vc)])
                    sc_prev = {"V": vc}
                xh_war = {"T": p.cnt["T"], "LD2": xst}
            p.dma("sync", comb_fm[:], comb_d[:],
                  transpose=True, waits=[("LDG", p.snap("LDG"))])
            tap(p, "combfm", comb_fm[0:16, :], [("LD", p.snap("LD"))])
            phaseA_done = merge_counts(
                {"LD": p.snap("LD"), "LD2": p.snap("LD2"), "S": p.cnt["S"],
                 "V": p.cnt["V"], "T": p.cnt["T"], "Gap": p.cnt.get("Gap", 0)})
        pw = as_waits(phaseA_done)
        y_acc = [p.sb(f"l0f_ya{i}", [128, QT], F32, stack=st) for i in range(DT)]
        xhs = p.sb("l0f_xhs", [128, DT * TC], BF16, stack=st)
        w1b = p.sb("l0f_w1b", [128, DT * TC], BF16, stack=st)
        w3b = p.sb("l0f_w3b", [128, DT * TC], BF16, stack=st)
        w2b = p.sb("l0f_w2b", [128, 4 * 1024], BF16, stack=st)
        g_sb = [p.sb(f"l0f_g{i}", [128, TC], BF16, stack=st) for i in range(4)]
        u_sb = [p.sb(f"l0f_u{i}", [128, TC], BF16, stack=st) for i in range(4)]
        cb = p.sb("l0f_cb", [128, TC], BF16, stack=st)
        shuf = p.sb("l0f_shuf", [32, TC], BF16, stack=st)
        yb_war = {}
        done = {}
        passes = [("r", e) for e in range(E)] + [("s", 0), ("s", 1)]
        with ExitStack() as stB:
            lnp2 = LNPool(p, stB, "B", src_f32=True)
            psg = [p.ps(f"l0f_pg{i}", [128, TC], F32, stack=stB) for i in range(2)]
            psu = [p.ps(f"l0f_pu{i}", [128, TC], F32, stack=stB) for i in range(2)]
            psy = [p.ps(f"l0f_py{i}", [128, TC], F32, stack=stB) for i in range(2)]
            psc = psu[0]
            gW = {}
            xhs_war = {}
            cb_war = {}
            gb_free = {0: {}, 1: {}}
            yb_free = {0: {}, 1: {}}
            ya_ready = {}
            for q in range(NT // QT):
                qw = pw if q == 0 else []
                zv = None
                for i in range(DT):
                    zv = p.op("vector", "memset", y_acc[i][:], 0.0,
                              waits=(as_waits(ya_ready) + qw) if i == 0 else (),
                              inc="V" if i == DT - 1 else None)
                ya_w = [("V", zv)]
                for (kind, e) in passes:
                    if kind == "r":
                        l1 = load_wslice(p, wtd1, Fd * e, Fd, w1b,
                                         extra_waits=as_waits(gW) + qw)
                        l3 = load_wslice(p, wtd3, Fd * e, Fd, w3b,
                                         extra_waits=as_waits(gW))
                        l2 = load_wslice(p, wtd2, D * e, 1024, w2b,
                                         extra_waits=as_waits(gW))
                    else:
                        wt1, wt3, wt2 = wtdS1, wtdS3, wtdS2
                        l1 = load_wslice(p, wt1, Fd * e, Fd, w1b,
                                         extra_waits=as_waits(gW) + qw)
                        l3 = load_wslice(p, wt3, Fd * e, Fd, w3b,
                                         extra_waits=as_waits(gW))
                        l2 = load_wslice(p, wt2, 0, 1024, w2b,
                                         extra_waits=as_waits(gW),
                                         i0=4 * e, nI=4)
                    wld = as_waits(merge_counts(l1, l3, l2))
                    for sp in range(QT // TC):
                        t0 = q * QT + sp * TC
                        xld = p.dma("gpsimd",
                                    xhs.rearrange("p (i t) -> p i t", t=TC),
                                    xhat_d[:, t0:t0 + TC].rearrange(
                                        "(i p) t -> p i t", p=128),
                                    sem="LDG",
                                    waits=as_waits(xhs_war) + (pw if q == 0 else []))
                        xw2 = [("LDG", xld)]
                        if kind == "r":
                            sv = p.op("vector", "stream_shuffle", shuf[:],
                                      comb_fm[0:32, t0:t0 + TC], [e] * 32,
                                      waits=as_waits(cb_war) + (pw if q == 0 else []),
                                      inc="V")
                            tcb = p.op("tensor", "matmul", psc[:],
                                       core.const_sb["ones32_bf"][:], shuf[:],
                                       start=True, stop=True,
                                       waits=[("V", sv)] + as_waits(cb_war)
                                             + as_waits(gb_free[0]), inc="T")
                            scb = p.op("scalar", "activation", cb[:], psc[:],
                                       AF.Copy, waits=[("T", tcb)], inc="S")
                            cbw = [("S", scb)]
                        else:
                            cbw = []
                        if q == 0 and sp == 0:
                            tap(p, f"xhs_{kind}{e}", xhs[:], [("LDG", xld)])
                        hv = None
                        for ft in range(4):
                            bk = ft % 2
                            tg = mm_group(p, psg[bk][:],
                                          [(w1b[:, TC * i + 128 * ft:TC * i + 128 * (ft + 1)],
                                            xhs[:, TC * i:TC * i + TC])
                                           for i in range(DT)],
                                          waits_first=xw2 + wld + as_waits(gb_free[bk]))
                            sg = p.op("scalar", "activation", g_sb[ft][:], psg[bk][:],
                                      AF.Silu, waits=[("T", tg)], inc="S")
                            tu = mm_group(p, psu[bk][:],
                                          [(w3b[:, TC * i + 128 * ft:TC * i + 128 * (ft + 1)],
                                            xhs[:, TC * i:TC * i + TC])
                                           for i in range(DT)],
                                          waits_first=cbw if (ft == 0 and kind == "r") else ())
                            su = p.op("scalar", "activation", u_sb[ft][:], psu[bk][:],
                                      AF.Copy, waits=[("T", tu)], inc="S")
                            hv = p.op("vector", "tensor_mul", g_sb[ft][:], g_sb[ft][:],
                                      u_sb[ft][:], waits=[("S", su), ("S", sg)])
                            if kind == "r":
                                hv = p.op("vector", "tensor_mul", g_sb[ft][:],
                                          g_sb[ft][:], cb[:],
                                          waits=cbw if ft == 0 else (), inc="V")
                            else:
                                hv = p.op("vector", "tensor_mul", g_sb[ft][:],
                                          g_sb[ft][:], g_sb[ft][:], ALU_BYPASS_NOOP=None)                                     if False else hv
                                hv = p.op("vector", "engine_nop", inc="V")
                            gb_free[bk] = {"V": hv}
                        hw2 = [("V", hv)]
                        if q == 0 and sp == 0:
                            tap(p, f"h_{kind}{e}", g_sb[0][:], [("V", p.cnt["V"])])
                            if kind == "r":
                                tap(p, f"cb_{kind}{e}", cb[:], [("S", p.cnt["S"])])
                        for ot in range(DT):
                            bk = ot % 2
                            ty = mm_group(p, psy[bk][:],
                                          [(w2b[:, 1024 * f + 128 * ot:1024 * f + 128 * (ot + 1)],
                                            g_sb[f][:]) for f in range(4)],
                                          waits_first=hw2 + wld + as_waits(yb_free[bk]))
                            av = p.op("vector", "tensor_add",
                                      y_acc[ot][:, TC * sp:TC * sp + TC],
                                      y_acc[ot][:, TC * sp:TC * sp + TC],
                                      psy[bk][:],
                                      waits=[("T", ty)] + (ya_w if ot < 2 else []),
                                      inc="V")
                            yb_free[bk] = {"V": av}
                        xhs_war = {"T": p.cnt["T"]}
                        if kind == "r":
                            cb_war = {"T": tcb, "S": scb, "V": p.cnt["V"]}
                    gW = {"T": p.cnt["T"]}
                ydone = {"V": p.cnt["V"]}
                if q == 0:
                    for i in range(DT):
                        tap(p, f"yacc{i}", y_acc[i][:], as_waits(ydone))
                for sp in range(QT // TC):
                    t0 = q * QT + sp * TC
                    d2 = emit_ln(p, core, lnp2,
                                 [y_acc[i] for i in range(DT)],
                                 sp * TC, TC, res_add=True, res_t0=t0,
                                 src_ready=as_waits(ydone) + as_waits(yb_war))
                    yb_war = {"T": lnp2.prev["T"], "Gap": d2["Gap"], "V": d2["V"]}
                    done = d2
                ya_ready = merge_counts({"T": lnp2.prev["T"]},
                                        {k: v for k, v in done.items()})
        core.res_ready = merge_counts(core.res_ready, done)


def emit_l3a(p, core, aps):
    DL, DH, DHR = 256, 64, 32
    NH = 16
    SQ = 1.0 / math.sqrt(96.0)
    with ExitStack() as st:
        wDQ, d1 = load_w_nat(p, "wdq", aps["L3_attn_W_DQ"], D, DL, st)
        wDKV, d2 = load_w_nat(p, "wdkv", aps["L3_attn_W_DKV"], D, DL, st)
        wUQ, d3 = load_w_nat(p, "wuq", aps["L3_attn_W_UQ"], DL, D, st)
        wQR, d4 = load_w_nat(p, "wqr", aps["L3_attn_W_QR"], DL, 512, st)
        wUK, d5 = load_w_nat(p, "wuk", aps["L3_attn_W_UK"], DL, D, st)
        wUV, d6 = load_w_nat(p, "wuv", aps["L3_attn_W_UV"], DL, D, st)
        wKR, d7 = load_w_nat(p, "wkr", aps["L3_attn_W_KR"], D, 32, st)
        wO, d8 = load_w_nat(p, "wo", aps["L3_attn_W_O"], D, D, st)
        wld = as_waits(merge_counts(d1, d2, d3, d4, d5, d6, d7, d8))
        cosb = core.local_const(st, "rope3_cos_bf")
        sinb = core.local_const(st, "rope3_sin_bf")
        perm = core.local_const(st, "rope3_perm_bf")
        mge = core.local_const(st, "madd_f_ge_p")
        onesb = core.const_sb["ones_bf"]
        identb = core.const_sb["ident_bf"]

        lnp = LNPool(p, st, "A", tc=256)
        lnp2 = lnp
        xh = [p.sb(f"l3a_xh{i}", [128, TC], BF16, stack=st) for i in range(DT)]
        cq = [p.sb(f"l3a_cq{i}", [128, T_SEQ], BF16, stack=st) for i in range(2)]
        ckv = [p.sb(f"l3a_ckv{i}", [128, T_SEQ], BF16, stack=st) for i in range(2)]
        kr = p.sb("l3a_kr", [32, T_SEQ], BF16, stack=st)
        krt = p.sb("l3a_krt", [32, TC], BF16, stack=st)
        yfm = [p.sb(f"l3a_y{i}", [128, T_SEQ], BF16, stack=st) for i in range(DT)]
        yb = xh
        qc = p.sb("l3a_qc", [64, T_SEQ], BF16, stack=st)
        qr = p.sb("l3a_qr", [32, T_SEQ], BF16, stack=st)
        qrt = krt
        kc = p.sb("l3a_kc", [64, T_SEQ], BF16, stack=st)
        vtm = p.sb("l3a_vtm", [128, NH * DH], BF16, stack=st)
        ssb = [p.sb("l3a_s0", [128, TC], F32, stack=st)] * 2
        ptb = [p.sb(f"l3a_pt{i}", [128, TC], BF16, stack=st) for i in range(2)]
        dsb = ssb[0]

        psP = [p.ps(f"l3a_pp{i}", [128, TC], F32, stack=st) for i in range(2)]
        psS = [p.ps(f"l3a_ps{i}", [128, TC], F32, stack=st) for i in range(2)]
        psO = psP[0]
        psD = psP[1]

        res_r = as_waits(core.res_ready)
        done = {}
        yb_war = {}
        xh_war = {}
        krt_war = {}
        qrt_war = krt_war
        ssb_war = {}
        pf = {0: {}, 1: {}}
        vt_f = {0: {}, 1: {}}
        sA_f = {0: {}, 1: {}}
        o_f = {}
        d_f = {}
        NC_B = T_SEQ // TC
        for b in range(B_PER_CORE):
            cq_done = {}
            for tc in range(NC_B):
                t0 = b * T_SEQ + tc * TC
                lc = tc * TC
                for _h in range(2):
                    d1_ = emit_ln(p, core, lnp, core.res, t0 + 256 * _h, 256,
                                  dst_tiles=xh, dst_t0=256 * _h,
                                  src_ready=res_r if (b == 0 and tc == 0 and _h == 0) else (),
                                  dst_war=as_waits(xh_war) if _h == 0 else ())
                xw = as_waits(d1_) + wld
                for (dst, wt, not_) in ((cq, wDQ, 2), (ckv, wDKV, 2)):
                    for ot in range(not_):
                        bk = ot % 2
                        tmm = mm_group(p, psP[bk][:],
                                       [(wt[i][:, 128 * ot:128 * (ot + 1)],
                                         xh[i][:, 0:TC]) for i in range(DT)],
                                       waits_first=xw + as_waits(pf[bk]))
                        sev = p.op("scalar", "activation", dst[ot][:, lc:lc + TC],
                                   psP[bk][:], AF.Copy, waits=[("T", tmm)], inc="S")
                        pf[bk] = {"S": sev}
                tmm = mm_group(p, psP[0][0:32, 0:TC],
                               [(wKR[i][:, 0:32], xh[i][:, 0:TC]) for i in range(DT)],
                               waits_first=xw + as_waits(pf[0]))
                sev = p.op("scalar", "activation", krt[0:32, 0:TC],
                           psP[0][0:32, 0:TC], AF.Copy,
                           waits=[("T", tmm)] + as_waits(krt_war), inc="S")
                pf[0] = {"S": sev}
                tpm = p.op("tensor", "matmul", psP[1][0:32, 0:TC],
                           perm[0:32, 0:32], krt[0:32, 0:TC], start=True, stop=True,
                           waits=[("S", sev)] + as_waits(pf[1]), inc="T")
                spm = p.op("scalar", "activation", kr[0:32, lc:lc + TC],
                           psP[1][0:32, 0:TC], AF.Copy, waits=[("T", tpm)], inc="S")
                pf[1] = {"S": spm}
                v1 = p.op("vector", "tensor_mul", krt[0:32, 0:TC], krt[0:32, 0:TC],
                          cosb[0:32, lc:lc + TC], waits=[("S", spm)])
                v1 = p.op("vector", "tensor_mul", kr[0:32, lc:lc + TC],
                          kr[0:32, lc:lc + TC], sinb[0:32, lc:lc + TC])
                v1 = p.op("vector", "tensor_add", kr[0:32, lc:lc + TC],
                          kr[0:32, lc:lc + TC], krt[0:32, 0:TC], inc="V")
                krt_war = merge_counts(krt_war, {"V": v1})
                cq_done = {"V": v1, "S": p.cnt["S"]}
            xh_war = {"T": p.cnt["T"]}
            cqw = as_waits(cq_done)
            for h in range(NH):
                hw = cqw
                qcd = {}
                for tc in range(NC_B):
                    lc = tc * TC
                    t1 = mm_group(p, psP[0][0:64, 0:TC],
                                  [(wUQ[i][:, 64 * h:64 * h + 64], cq[i][:, lc:lc + TC])
                                   for i in range(2)],
                                  waits_first=hw + as_waits(pf[0]))
                    s1 = p.op("scalar", "activation", qc[0:64, lc:lc + TC],
                              psP[0][0:64, 0:TC], AF.Copy, waits=[("T", t1)], inc="S")
                    pf[0] = {"S": s1}
                    t2 = mm_group(p, psP[1][0:64, 0:TC],
                                  [(wUK[i][:, 64 * h:64 * h + 64], ckv[i][:, lc:lc + TC])
                                   for i in range(2)],
                                  waits_first=as_waits(pf[1]))
                    s2 = p.op("scalar", "activation", kc[0:64, lc:lc + TC],
                              psP[1][0:64, 0:TC], AF.Copy, waits=[("T", t2)], inc="S")
                    pf[1] = {"S": s2}
                    t4 = mm_group(p, psP[1][0:32, 0:TC],
                                  [(wQR[i][:, 32 * h:32 * h + 32], cq[i][:, lc:lc + TC])
                                   for i in range(2)],
                                  waits_first=[("S", s2)])
                    s4 = p.op("scalar", "activation", qrt[0:32, 0:TC],
                              psP[1][0:32, 0:TC], AF.Copy,
                              waits=[("T", t4)] + as_waits(krt_war), inc="S")
                    t5 = p.op("tensor", "matmul", psP[1][32:64, 0:TC],
                              perm[0:32, 0:32], qrt[0:32, 0:TC],
                              start=True, stop=True, waits=[("S", s4)], inc="T")
                    s5 = p.op("scalar", "activation", qr[0:32, lc:lc + TC],
                              psP[1][32:64, 0:TC], AF.Copy, waits=[("T", t5)], inc="S")
                    pf[1] = {"S": s5}
                    v1 = p.op("vector", "tensor_mul", qrt[0:32, 0:TC],
                              qrt[0:32, 0:TC], cosb[0:32, lc:lc + TC],
                              waits=[("S", s5)])
                    v1 = p.op("vector", "tensor_mul", qr[0:32, lc:lc + TC],
                              qr[0:32, lc:lc + TC], sinb[0:32, lc:lc + TC])
                    v1 = p.op("vector", "tensor_add", qr[0:32, lc:lc + TC],
                              qr[0:32, lc:lc + TC], qrt[0:32, 0:TC], inc="V")
                    qrt_war = krt_war = merge_counts(krt_war, {"V": v1})
                    qcd = {"V": v1, "S": p.cnt["S"]}
                hpw = as_waits(qcd)
                vt_done = {}
                for s in range(NH):
                    bk = s % 2
                    tt = mm_group(p, psP[bk][:, 0:64],
                                  [(ckv[i][:, 128 * s:128 * (s + 1)],
                                    wUV[i][:, 64 * h:64 * h + 64]) for i in range(2)],
                                  waits_first=hpw + as_waits(pf[bk]))
                    sv = p.op("scalar", "activation", vtm[:, 64 * s:64 * s + 64],
                              psP[bk][:, 0:64], AF.Copy, waits=[("T", tt)], inc="S")
                    pf[bk] = {"S": sv}
                    vt_done = {"S": sv}
                vtw = as_waits(vt_done)
                o_f = merge_counts(o_f, pf[0])
                d_f = merge_counts(d_f, pf[1])
                for tc in range(NC_B):
                    lc = tc * TC
                    nst = 4 * (tc + 1)
                    ofirst = True
                    for s in range(nst):
                        bk = s % 2
                        tS = p.op("tensor", "matmul", psS[bk][:],
                                  kc[0:64, 128 * s:128 * (s + 1)], qc[0:64, lc:lc + TC],
                                  start=True, stop=False,
                                  waits=hpw + as_waits(sA_f[bk]))
                        tS = p.op("tensor", "matmul", psS[bk][:],
                                  kr[0:32, 128 * s:128 * (s + 1)], qr[0:32, lc:lc + TC],
                                  start=False, stop=True, inc="T")
                        vc = p.op("vector", "tensor_scalar", ssb[bk][:], psS[bk][:],
                                  -80.0, 80.0, ALU.max, ALU.min,
                                  waits=[("T", tS)] + as_waits(ssb_war), inc="V")
                        rel = s - 4 * tc
                        if rel >= 0:
                            vc = p.op("vector", "tensor_add",
                                      ssb[bk][:, 128 * rel:128 * (rel + 1)],
                                      ssb[bk][:, 128 * rel:128 * (rel + 1)],
                                      mge[:], inc="V")
                        se = p.op("scalar", "activation", ptb[bk][:], ssb[bk][:],
                                  AF.Exp, scale=core.fscal(SQ),
                                  waits=[("V", vc)], inc="S")
                        ssb_war = {"S": se}
                        pw = [("S", se)]
                        if rel > 0:
                            mz = p.op("vector", "memset", ptb[bk][:, 0:128 * rel],
                                      0.0, waits=[("S", se)], inc="V")
                            pw = [("V", mz)]
                        tO = p.op("tensor", "matmul", psO[0:64, :],
                                  vtm[:, 64 * s:64 * s + 64], ptb[bk][:],
                                  start=ofirst, stop=(s == nst - 1),
                                  waits=pw + (as_waits(o_f) if ofirst else []),
                                  inc="T" if s == nst - 1 else None)
                        tD = p.op("tensor", "matmul", psD[:],
                                  onesb[:], ptb[bk][:],
                                  start=ofirst, stop=(s == nst - 1),
                                  waits=(as_waits(d_f) if ofirst else []),
                                  inc="T" if s == nst - 1 else None)
                        sA_f[bk] = {"S": se} if rel <= 0 else {"V": pw[0][1] if pw[0][0] == "V" else 0, "S": se}
                        ofirst = False
                    tpv = p.cnt["T"]
                    vr = p.op("vector", "reciprocal", dsb[:], psD[:],
                              waits=[("T", tpv)], inc="V")
                    d_f = {"V": vr}
                    ti, ro = h // 2, 64 * (h % 2)
                    vo = p.op("vector", "tensor_tensor",
                              yfm[ti][ro:ro + 64, lc:lc + TC],
                              psO[0:64, :], dsb[0:64, :], ALU.mult,
                              waits=[("V", vr)], inc="V")
                    o_f = {"V": vo}
                    sA_f[0] = merge_counts(sA_f[0], {"V": vo})
                    ssb_war = merge_counts(ssb_war, {"V": vo})
                    pf[0] = merge_counts(pf[0], {"V": vo})
                    pf[1] = merge_counts(pf[1], {"V": vr})
            yw = as_waits(o_f) + as_waits(d_f)
            for tc in range(NC_B):
                t0 = b * T_SEQ + tc * TC
                lc = tc * TC
                ydone = {}
                for ot in range(DT):
                    bk = ot % 2
                    tmm = mm_group(p, psS[bk][:],
                                   [(wO[i][:, 128 * ot:128 * (ot + 1)],
                                     yfm[i][:, lc:lc + TC]) for i in range(DT)],
                                   waits_first=yw + as_waits(sA_f[bk]))
                    sy = p.op("scalar", "activation", yb[ot][:, 0:TC], psS[bk][:],
                              AF.Copy, waits=[("T", tmm)] + as_waits(yb_war), inc="S")
                    sA_f[bk] = {"S": sy}
                    ydone = {"S": sy, "T": tmm}
                for _h in range(2):
                    d2_ = emit_ln(p, core, lnp2, yb, 256 * _h, 256, res_add=True,
                                  res_t0=t0 + 256 * _h,
                                  src_ready=as_waits(ydone) if _h == 0 else ())
                yb_war = {"T": lnp2.prev["T"], "Gap": d2_["Gap"], "V": d2_["V"]}
                done = d2_
        core.res_ready = merge_counts(core.res_ready, done)


def emit_l0a(p, core, aps):
    NH, HD = 16, 64
    with ExitStack() as st:
        wd = {}
        for nm in ("Wq", "Wk", "Wv"):
            scr = p.dtmp(f"l0a_{nm}", [D, D], BF16)
            p.dma("gpsimd", scr[:], aps["L0_attn_" + nm][:])
            wd[nm] = scr
        cast_done = {"LDG": p.snap("LDG")}
        wo, wo_d = load_w_nat(p, "l0a_wo", aps["L0_attn_Wo"], D, D, st)
        nwq = p.sb("l0a_nwq", [64, 1], F32, stack=st)
        nwk = p.sb("l0a_nwk", [64, 1], F32, stack=st)
        p.dma("sync", nwq[:], aps["L0_attn_q_norm_w"].rearrange("(a b) -> a b", b=1))
        p.dma("sync", nwk[:], aps["L0_attn_k_norm_w"].rearrange("(a b) -> a b", b=1))
        nw_d = {"LD": p.snap("LD")}

        cosb = core.local_const(st, "rope0_cos_bf")
        sinb = core.local_const(st, "rope0_sin_bf")
        perm = core.local_const(st, "rope0_perm_bf")
        mle = core.local_const(st, "madd_f_le_p")
        onesb = core.const_sb["ones_bf"]
        identb = core.const_sb["ident_bf"]
        epsc = core.fscal(1e-6)

        lnp = LNPool(p, st, "A", tc=256)
        xh = [p.sb(f"l0a_xh{i}", [128, T_SEQ], BF16, stack=st) for i in range(DT)]
        yfm = [p.sb(f"l0a_y{i}", [128, T_SEQ], BF16, stack=st) for i in range(DT)]
        yb = [p.sb(f"l0a_yb{i}", [128, TC], BF16, stack=st) for i in range(DT)]
        wq4 = p.sb("l0a_wq4", [128, DT * 256], BF16, stack=st)
        wk4 = p.sb("l0a_wk4", [128, DT * 256], BF16, stack=st)
        wv4 = p.sb("l0a_wv4", [128, DT * 256], BF16, stack=st)
        qh = p.sb("l0a_qh", [64, T_SEQ], BF16, stack=st)
        kh = p.sb("l0a_kh", [64, T_SEQ], BF16, stack=st)
        qsw = p.sb("l0a_qsw", [64, TC], BF16, stack=st)
        sqb = p.sb("l0a_sqb", [64, TC], BF16, stack=st)
        rsb = p.sb("l0a_rsb", [128, TC], F32, stack=st)
        rsbb = p.sb("l0a_rsbb", [128, TC], BF16, stack=st)
        vtm = p.sb("l0a_vtm", [128, NH * 64], BF16, stack=st)
        psb = p.sb("l0a_psb", [128, T_SEQ], BF16, stack=st)
        ptb = p.sb("l0a_ptb", [128, TC], BF16, stack=st)
        mparts = p.sb("l0a_mp", [128, 8], F32, stack=st)
        dparts = p.sb("l0a_dp", [128, 8], F32, stack=st)
        mcol = p.sb("l0a_mc", [128, 1], F32, stack=st)
        dcol = p.sb("l0a_dc", [128, 1], F32, stack=st)

        psS = [p.ps(f"l0a_s{i}", [128, TC], F32, stack=st) for i in range(4)]
        psT = p.ps("l0a_t", [128, TC], F32, stack=st)
        psV = p.ps("l0a_v", [128, TC], F32, stack=st)

        res_r = as_waits(core.res_ready)
        done = {}
        yb_war = {}
        xh_war = {}
        psf = {i: {} for i in range(4)}
        ptf = {}
        pvf = {}
        small_war = {}
        yfm_war = {}
        psb_war = {}
        mcol_war = {}
        for b in range(B_PER_CORE):
            for tc in range(T_SEQ // 256):
                t0 = b * T_SEQ + tc * 256
                d1_ = emit_ln(p, core, lnp, core.res, t0, 256, dst_tiles=xh,
                              dst_t0=tc * 256,
                              src_ready=res_r if (b == 0 and tc == 0) else (),
                              dst_war=as_waits(xh_war) if tc == 0 else ())
            xw = as_waits(d1_)
            for h in range(NH):
                hg, hi = h // 4, h % 4
                if hi == 0:
                    for (buf, nm) in ((wq4, "Wq"), (wk4, "Wk"), (wv4, "Wv")):
                        p.dma("sync",
                              buf[:].rearrange("p (i c) -> p i c", c=256),
                              wd[nm][:, 256 * hg:256 * (hg + 1)].rearrange(
                                  "(i p) c -> p i c", p=128),
                              waits=as_waits(cast_done) + [("T", p.cnt["T"])])
                    wg_d = [("LD", p.snap("LD"))]
                for (dst, w4, nw) in ((qh, wq4, nwq), (kh, wk4, nwk)):
                    for cc in range(T_SEQ // TC):
                        lc = cc * TC
                        t1 = mm_group(p, psS[0][0:64, :],
                                      [(w4[:, 256 * i + 64 * hi:256 * i + 64 * hi + 64],
                                        xh[i][:, lc:lc + TC]) for i in range(DT)],
                                      waits_first=xw + wg_d + as_waits(psf[0])
                                                  + as_waits(nw_d))
                        s1 = p.op("scalar", "activation", dst[0:64, lc:lc + TC],
                                  psS[0][0:64, :], AF.Copy, waits=[("T", t1)], inc="S")
                        psf[0] = {"S": s1}
                        v1 = p.op("vector", "tensor_mul", sqb[0:64, 0:TC],
                                  dst[0:64, lc:lc + TC], dst[0:64, lc:lc + TC],
                                  waits=[("S", s1)], inc="V")
                        t2 = p.op("tensor", "matmul", psS[1][:],
                                  onesb[0:64, :], sqb[0:64, 0:TC],
                                  start=True, stop=True,
                                  waits=[("V", v1)] + as_waits(psf[1]), inc="T")
                        s2 = p.op("scalar", "activation", rsb[:, 0:TC], psS[1][:],
                                  AF.Sqrt, scale=core.fscal(1.0 / HD), bias=epsc,
                                  waits=[("T", t2)], inc="S")
                        p.op("vector", "reciprocal", rsb[:, 0:TC], rsb[:, 0:TC],
                             waits=[("S", s2)])
                        v2 = p.op("vector", "tensor_copy", rsbb[:, 0:TC],
                                  rsb[:, 0:TC], inc="V")
                        psf[1] = {"V": v2}
                        p.op("vector", "tensor_mul", dst[0:64, lc:lc + TC],
                             dst[0:64, lc:lc + TC], rsbb[0:64, 0:TC])
                        v3 = p.op("vector", "tensor_scalar_mul", dst[0:64, lc:lc + TC],
                                  dst[0:64, lc:lc + TC], nw[:, 0:1], inc="V")
                        t3 = p.op("tensor", "matmul", psS[1][0:64, :],
                                  perm[0:64, 0:64], dst[0:64, lc:lc + TC],
                                  start=True, stop=True,
                                  waits=[("V", v3)] + as_waits(psf[1]), inc="T")
                        s3 = p.op("scalar", "activation", qsw[0:64, 0:TC],
                                  psS[1][0:64, :], AF.Copy, waits=[("T", t3)], inc="S")
                        psf[1] = {"S": s3}
                        p.op("vector", "tensor_mul", dst[0:64, lc:lc + TC],
                             dst[0:64, lc:lc + TC], cosb[0:64, lc:lc + TC],
                             waits=[("S", s3)])
                        p.op("vector", "tensor_mul", qsw[0:64, 0:TC],
                             qsw[0:64, 0:TC], sinb[0:64, lc:lc + TC])
                        v4 = p.op("vector", "tensor_add", dst[0:64, lc:lc + TC],
                                  dst[0:64, lc:lc + TC], qsw[0:64, 0:TC], inc="V")
                        small_war = {"V": v4}
                qk_d = [("V", p.cnt["V"])]
                vt_d = {}
                for s in range(NH):
                    bk = 2 + s % 2
                    t4 = mm_group(p, psS[bk][:, 0:64],
                                  [(xh[i][:, 128 * s:128 * (s + 1)],
                                    wv4[:, 256 * i + 64 * hi:256 * i + 64 * hi + 64])
                                   for i in range(DT)],
                                  waits_first=xw + wg_d + as_waits(psf[bk]))
                    s4 = p.op("scalar", "activation", vtm[:, 64 * s:64 * s + 64],
                              psS[bk][:, 0:64], AF.Copy, waits=[("T", t4)], inc="S")
                    psf[bk] = {"S": s4}
                    vt_d = {"S": s4}
                vw = as_waits(vt_d) + qk_d
                for ti in range(NH):
                    nfull = ti // 4
                    w_last = (ti % 4 + 1) * 128
                    nch = nfull + 1
                    for j in range(nch):
                        wdt = TC if j < nfull else w_last
                        tS = p.op("tensor", "matmul", psS[j][:, 0:wdt],
                                  qh[0:64, 128 * ti:128 * (ti + 1)],
                                  kh[0:64, TC * j:TC * j + wdt],
                                  start=True, stop=True,
                                  waits=vw + as_waits(psf[j]), inc="T")
                        psf[j] = {"T": tS}
                    tSl = p.cnt["T"]
                    vm = p.op("vector", "tensor_add",
                              psS[nfull][:, w_last - 128:w_last],
                              psS[nfull][:, w_last - 128:w_last], mle[:],
                              waits=[("T", tSl)], inc="V")
                    for j in range(nch):
                        wdt = TC if j < nfull else w_last
                        p.op("vector", "reduce_max", mparts[:, j:j + 1],
                             psS[j][:, 0:wdt], AX.X,
                             waits=([("V", vm)] + as_waits(mcol_war)) if j == 0 else ())
                    vmx = p.op("vector", "reduce_max", mcol[:], mparts[:, 0:nch],
                               AX.X)
                    vmx = p.op("vector", "tensor_scalar_mul", mcol[:], mcol[:],
                               -0.125, inc="V")
                    se = None
                    for j in range(nch):
                        wdt = TC if j < nfull else w_last
                        se = p.op("scalar", "activation",
                                  psb[:, TC * j:TC * j + wdt], psS[j][:, 0:wdt],
                                  AF.Exp, bias=mcol[:, 0:1], scale=core.fscal(0.125),
                                  accum_out=dparts[:, j:j + 1],
                                  waits=([("V", vmx)] + as_waits(psb_war)) if j == 0 else (),
                                  inc="S")
                        psf[j] = {"S": se}
                    mcol_war = {"S": se}
                    vds = p.op("vector", "reduce_sum", dcol[:], dparts[:, 0:nch],
                               AX.X, waits=[("S", se)])
                    vds = p.op("vector", "reciprocal", dcol[:], dcol[:], inc="V")
                    vdv = None
                    for j in range(nch):
                        wdt = TC if j < nfull else w_last
                        vdv = p.op("vector", "tensor_scalar_mul",
                                   psb[:, TC * j:TC * j + wdt],
                                   psb[:, TC * j:TC * j + wdt], dcol[:, 0:1],
                                   inc="V" if j == nch - 1 else None)
                    ntk = 4 * nfull + (ti % 4 + 1)
                    first = True
                    for g in range((ntk + 3) // 4):
                        gn = min(4, ntk - 4 * g)
                        for jj in range(gn):
                            p.op("tensor", "matmul", psT[:, 128 * jj:128 * (jj + 1)],
                                 psb[:, 128 * (4 * g + jj):128 * (4 * g + jj + 1)],
                                 identb[:], start=True, stop=True,
                                 waits=([("V", vdv)] + as_waits(ptf)) if jj == 0 and g == 0
                                       else (as_waits(ptf) if jj == 0 else ()),
                                 inc="T" if jj == gn - 1 else None)
                        tT = p.cnt["T"]
                        sT = p.op("scalar", "activation", ptb[:, 0:128 * gn],
                                  psT[:, 0:128 * gn], AF.Copy,
                                  waits=[("T", tT)] + as_waits(ptf), inc="S")
                        for jj in range(gn):
                            tv = p.op("tensor", "matmul", psV[0:64, 0:128],
                                      vtm[:, 64 * (4 * g + jj):64 * (4 * g + jj) + 64],
                                      ptb[:, 128 * jj:128 * (jj + 1)],
                                      start=first, stop=(4 * g + jj == ntk - 1),
                                      waits=([("S", sT)] + as_waits(pvf)) if jj == 0
                                            else [("S", sT)],
                                      inc="T" if 4 * g + jj == ntk - 1 else None)
                            first = False
                        ptf = {"S": sT}
                    tpv = p.cnt["T"]
                    psb_war = {"T": tpv}
                    tile, ro = h // 2, 64 * (h % 2)
                    sy = p.op("scalar", "activation",
                              yfm[tile][ro:ro + 64, 128 * ti:128 * (ti + 1)],
                              psV[0:64, 0:128], AF.Copy,
                              waits=[("T", tpv)] + as_waits(yfm_war), inc="S")
                    pvf = {"S": sy}
            yfm_done = {"S": p.cnt["S"]}
            yw = as_waits(yfm_done)
            for tc in range(T_SEQ // TC):
                t0 = b * T_SEQ + tc * TC
                lc = tc * TC
                ydone = {}
                for ot in range(DT):
                    bk = ot % 2
                    tmm = mm_group(p, psS[bk][:],
                                   [(wo[i][:, 128 * ot:128 * (ot + 1)],
                                     yfm[i][:, lc:lc + TC]) for i in range(DT)],
                                   waits_first=yw + as_waits(wo_d) + as_waits(psf[bk]))
                    sy = p.op("scalar", "activation", yb[ot][:, 0:TC], psS[bk][:],
                              AF.Copy, waits=[("T", tmm)] + as_waits(yb_war), inc="S")
                    psf[bk] = {"S": sy}
                    ydone = {"S": sy, "T": tmm}
                for _h2 in range(2):
                    d2_ = emit_ln(p, core, lnp, yb, 256 * _h2, 256, res_add=True,
                                  res_t0=t0 + 256 * _h2,
                                  src_ready=as_waits(ydone) if _h2 == 0 else ())
                yb_war = {"T": lnp.prev["T"], "Gap": d2_["Gap"], "V": d2_["V"]}
                done = d2_
            yfm_war = {"T": p.cnt["T"]}
            xh_war = merge_counts({"T": p.cnt["T"]}, yb_war)
        core.res_ready = merge_counts(core.res_ready, done)


def emit_l2a(p, core, aps):
    NH, HD = 16, 64
    CC = 128
    with ExitStack() as st:
        stg_stack = ExitStack()
        staging = [p.sb(f"l2a_stg{i}", [128, 3072], BF16, stack=stg_stack)
                   for i in range(2)]
        stg_free = {0: {}, 1: {}}
        wtd_qkv = WTd(p, "l2aqkv", aps["L2_attn_qkv_w"], 3072, D, staging, stg_free)
        wtd_o = WTd(p, "l2ao", aps["L2_attn_out_w"], D, D, staging, stg_free)
        stg_rel = [("LD2", p.snap("LD2"))]
        stg_stack.close()
        with ExitStack() as stS:
            scps = p.ps("l2a_scl", [128, 32], F32, stack=stS)
            bqkv, bq_d = load_scale_cols(p, core, aps["L2_attn_qkv_b"], 3072, st,
                                         "bq", scps, extra_waits=stg_rel)
            bout, bo_d = load_scale_cols(p, core, aps["L2_attn_out_b"], D, st,
                                         "bo", scps, extra_waits=as_waits(bq_d))
        bias_d = as_waits(merge_counts(bq_d, bo_d))
        mge = core.local_const(st, "mul_f_ge_p")
        onesb = core.const_sb["ones_bf"]

        lnp = LNPool(p, st, "A", tc=256)
        xh = [p.sb(f"l2a_xh{i}", [128, T_SEQ], BF16, stack=st) for i in range(DT)]
        yfm = [p.sb(f"l2a_y{i}", [128, T_SEQ], BF16, stack=st) for i in range(DT)]
        yb = [p.sb(f"l2a_yb{i}", [128, TC], BF16, stack=st) for i in range(DT)]
        w4 = p.sb("l2a_w4", [128, DT * 3 * 256], BF16, stack=st)
        wob = p.sb("l2a_wob", [128, DT * 1024], BF16, stack=st)
        qfm = p.sb("l2a_qfm", [64, T_SEQ], BF16, stack=st)
        kfm = p.sb("l2a_kfm", [64, T_SEQ], BF16, stack=st)
        ktm = p.sb("l2a_ktm", [128, NH * HD], BF16, stack=st)
        vtm = p.sb("l2a_vtm", [128, NH * HD], BF16, stack=st)
        kcum = p.sb("l2a_kcum", [64, CC], F32, stack=st)
        kcumb = p.sb("l2a_kcumb", [64, CC], BF16, stack=st)
        kcarry = p.sb("l2a_kcar", [64, 1], F32, stack=st)

        ptb = p.sb("l2a_ptb", [128, CC], BF16, stack=st)
        qkb = p.sb("l2a_qkb", [64, CC], BF16, stack=st)
        denf = p.sb("l2a_denf", [128, CC], F32, stack=st)
        kvsb = p.sb("l2a_kvsb", [64, 64], BF16, stack=st)
        kvb = p.sb("l2a_kvb", [32, 512], BF16, stack=st)

        psP = [p.ps(f"l2a_pp{i}", [128, TC], F32, stack=st) for i in range(2)]
        psPT = p.ps("l2a_ppt", [128, TC], F32, stack=st)
        psN = p.ps("l2a_pn", [128, TC], F32, stack=st)
        psD = p.ps("l2a_pd", [128, TC], F32, stack=st)
        psKV = p.ps("l2a_pkv", [128, TC], F32, stack=st)

        zrow = core.const_sb["zeros64"]

        res_r = as_waits(core.res_ready)
        done = {}
        yb_war = {}
        xh_war = {}
        pf = {0: {}, 1: {}}
        ptf = dict(merge_counts(bq_d, bo_d))
        nf = dict(ptf)
        df = dict(ptf)
        kvf = dict(ptf)
        yfm_war = {}
        for b in range(B_PER_CORE):
            for tc in range(T_SEQ // 256):
                t0 = b * T_SEQ + tc * 256
                d1_ = emit_ln(p, core, lnp, core.res, t0, 256, dst_tiles=xh,
                              dst_t0=tc * 256,
                              src_ready=(res_r + stg_rel + bias_d)
                                        if (b == 0 and tc == 0) else (),
                              dst_war=as_waits(xh_war) if tc == 0 else ())
            xw = as_waits(d1_)
            for h in range(NH):
                hg, hi = h // 4, h % 4
                if hi == 0:
                    for j in range(3):
                        load_wslice(p, wtd_qkv, 1024 * j + 256 * hg, 256, w4,
                                    col0=DT * 256 * j,
                                    extra_waits=stg_rel + [("T", p.cnt["T"])])
                    for (c0, j) in ((0, 1), (256, 2)):
                        bsl = aps["L2_attn_qkv_b"][1024 * j + 256 * hg:
                                                   1024 * j + 256 * hg + 256]
                        p.dma("gpsimd", kvb[:, c0:c0 + 256],
                              bsl.rearrange("(a c) -> a c", a=1).partition_broadcast(32),
                              waits=[("T", p.cnt["T"])])
                    wg_d = [("LD", p.snap("LD"))]

                def wsl(j, i):
                    c0 = DT * 256 * j + 256 * i + 64 * hi
                    return w4[:, c0:c0 + 64]
                bq = bqkv[64 * (h % 2):64 * (h % 2) + 64, h // 2:h // 2 + 1]
                bk = bqkv[64 * (h % 2):64 * (h % 2) + 64, 8 + h // 2:8 + h // 2 + 1]
                bv = bqkv[64 * (h % 2):64 * (h % 2) + 64, 16 + h // 2:16 + h // 2 + 1]
                for (dst, j, bias) in ((qfm, 0, bq), (kfm, 1, bk)):
                    for cc in range(T_SEQ // TC):
                        lc = cc * TC
                        bkk = cc % 2
                        t1 = mm_group(p, psP[bkk][0:64, :],
                                      [(wsl(j, i), xh[i][:, lc:lc + TC])
                                       for i in range(DT)],
                                      waits_first=xw + wg_d + bias_d + as_waits(pf[bkk]))
                        s1 = p.op("scalar", "activation", dst[0:64, lc:lc + TC],
                                  psP[bkk][0:64, :], AF.Relu, bias=bias,
                                  waits=[("T", t1)], inc="S")
                        v1 = p.op("vector", "tensor_scalar_add",
                                  dst[0:64, lc:lc + TC], dst[0:64, lc:lc + TC],
                                  1e-6, waits=[("S", s1)], inc="V")
                        pf[bkk] = {"S": s1, "V": v1}
                ob = core.const_sb["ones32_bf"]
                for s in range(NH):
                    bkk = s % 2
                    t2 = mm_group(p, psP[bkk][:, 0:64],
                                  [(xh[i][:, 128 * s:128 * (s + 1)], wsl(1, i))
                                   for i in range(DT)],
                                  waits_first=xw + wg_d + as_waits(pf[bkk]),
                                  inc_last=None)
                    t2 = p.op("tensor", "matmul", psP[bkk][:, 0:64],
                              ob[:], kvb[0:32, 64 * hi:64 * hi + 64],
                              start=False, stop=True, inc="T")
                    s2 = p.op("scalar", "activation", ktm[:, 64 * s:64 * s + 64],
                              psP[bkk][:, 0:64], AF.Relu,
                              waits=[("T", t2)], inc="S")
                    t3 = mm_group(p, psP[bkk][:, 64:128],
                                  [(xh[i][:, 128 * s:128 * (s + 1)], wsl(2, i))
                                   for i in range(DT)],
                                  waits_first=[("S", s2)], inc_last=None)
                    t3 = p.op("tensor", "matmul", psP[bkk][:, 64:128],
                              ob[:], kvb[0:32, 256 + 64 * hi:256 + 64 * hi + 64],
                              start=False, stop=True, inc="T")
                    s3 = p.op("scalar", "activation", vtm[:, 64 * s:64 * s + 64],
                              psP[bkk][:, 64:128], AF.Copy,
                              waits=[("T", t3)], inc="S")
                    pf[bkk] = {"S": s3}
                v2 = p.op("vector", "tensor_scalar_add", ktm[:], ktm[:], 1e-6,
                          waits=[("S", p.cnt["S"])], inc="V")
                hw = [("V", p.cnt["V"]), ("S", p.cnt["S"])]
                for ci in range(T_SEQ // CC):
                    lc = ci * CC
                    tp = p.op("tensor", "matmul", psPT[:, 0:CC],
                              kfm[0:64, lc:lc + CC], qfm[0:64, lc:lc + CC],
                              start=True, stop=True,
                              waits=hw + as_waits(ptf), inc="T")
                    vp = p.op("vector", "tensor_tensor", ptb[:, 0:CC],
                              psPT[:, 0:CC], mge[:, 0:CC], ALU.mult,
                              waits=[("T", tp)], inc="V")
                    ptf = {"V": vp}
                    tn = p.op("tensor", "matmul", psN[0:64, 0:CC],
                              vtm[:, 64 * ci:64 * ci + 64], ptb[:, 0:CC],
                              start=True, stop=(ci == 0),
                              waits=[("V", vp)] + as_waits(nf),
                              inc="T" if ci == 0 else None)
                    if ci > 0:
                        tn = p.op("tensor", "matmul", psN[0:64, 0:CC],
                                  kvsb[:], qfm[0:64, lc:lc + CC],
                                  start=False, stop=True,
                                  waits=as_waits(kvf), inc="T")
                    vs = p.op("vector", "tensor_tensor_scan", kcum[0:64, 0:CC],
                              kfm[0:64, lc:lc + CC], zrow[0:64, 0:CC],
                              0.0 if ci == 0 else kcarry[0:64, 0:1],
                              ALU.add, ALU.add)
                    vs = p.op("vector", "tensor_copy", kcarry[0:64, 0:1],
                              kcum[0:64, CC - 1:CC])
                    vs = p.op("vector", "tensor_copy", kcumb[0:64, 0:CC],
                              kcum[0:64, 0:CC])
                    vs = p.op("vector", "tensor_mul", qkb[0:64, 0:CC],
                              qfm[0:64, lc:lc + CC], kcumb[0:64, 0:CC], inc="V")
                    td = p.op("tensor", "matmul", psD[:, 0:CC],
                              onesb[0:64, :], qkb[0:64, 0:CC],
                              start=True, stop=True,
                              waits=[("V", vs)] + as_waits(df), inc="T")
                    vd = p.op("vector", "tensor_scalar_max", denf[:, 0:CC],
                              psD[:, 0:CC], 1e-6, waits=[("T", td)])
                    vd = p.op("vector", "reciprocal", denf[:, 0:CC], denf[:, 0:CC],
                              inc="V")
                    df = {"V": vd}
                    vy = p.op("vector", "tensor_tensor",
                              yfm[h // 2][64 * (h % 2):64 * (h % 2) + 64, lc:lc + CC],
                              psN[0:64, 0:CC], denf[0:64, 0:CC], ALU.mult,
                              waits=[("T", tn), ("V", vd)] + as_waits(yfm_war), inc="V")
                    nf = {"V": vy}
                    tkv = p.op("tensor", "matmul", psKV[0:64, 0:64],
                               ktm[:, 64 * ci:64 * ci + 64],
                               vtm[:, 64 * ci:64 * ci + 64],
                               start=(ci == 0), stop=(ci == T_SEQ // CC - 1),
                               waits=as_waits(kvf), inc="T")
                    skv = p.op("scalar", "activation", kvsb[:], psKV[0:64, 0:64],
                               AF.Copy, waits=[("T", tkv)], inc="S")
                    kvf = {"S": skv}
            yfm_done = {"V": p.cnt["V"]}
            lo = load_wslice(p, wtd_o, 0, 1024, wob,
                             extra_waits=stg_rel + [("T", p.cnt["T"])])
            yw = as_waits(yfm_done) + as_waits(lo)
            for tc in range(T_SEQ // TC):
                t0 = b * T_SEQ + tc * TC
                lc = tc * TC
                ydone = {}
                for ot in range(DT):
                    bkk = ot % 2
                    tmm = mm_group(p, psP[bkk][:],
                                   [(wob[:, 1024 * i + 128 * ot:1024 * i + 128 * (ot + 1)],
                                     yfm[i][:, lc:lc + TC]) for i in range(DT)],
                                   waits_first=yw + as_waits(pf[bkk]))
                    sy = p.op("scalar", "activation", yb[ot][:, 0:TC], psP[bkk][:],
                              AF.Identity, bias=bout[:, ot:ot + 1],
                              waits=[("T", tmm)] + as_waits(yb_war), inc="S")
                    pf[bkk] = {"S": sy}
                    ydone = {"S": sy, "T": tmm}
                for _h2 in range(2):
                    d2_ = emit_ln(p, core, lnp, yb, 256 * _h2, 256, res_add=True,
                                  res_t0=t0 + 256 * _h2,
                                  src_ready=as_waits(ydone) if _h2 == 0 else ())
                yb_war = {"T": lnp.prev["T"], "Gap": d2_["Gap"], "V": d2_["V"]}
                done = d2_
            yfm_war = {"T": p.cnt["T"]}
            xh_war = merge_counts({"T": p.cnt["T"]}, yb_war)
        core.res_ready = merge_counts(core.res_ready, done)


L1A_STAGES = "ABC"


def emit_l1a(p, core, aps):
    H, HD, CC, KW = 4, 256, 128, 4
    NCH = T_SEQ // CC
    with ExitStack() as st:
        win_ap = aps["L1_attn_W_in"]
        wctrl_ap = aps["L1_attn_W_ctrl"]
        wout_ap = aps["L1_attn_W_out"]
        wbuf = p.sb("l1a_wbuf", [128, DT * 1024], BF16, stack=st)
        wab = p.sb("l1a_wab", [128, DT * 8], BF16, stack=st)
        cvw = {}
        for nm in ("w_q", "w_k", "w_v"):
            t = p.sb(f"l1a_cv{nm}", [128, 4 * DT], F32, stack=st)
            p.dma("sync", t[:].rearrange("p (i c) -> p i c", c=4),
                  aps["L1_attn_" + nm].rearrange("(i p) a c -> p i (a c)", p=128)
                  .rearrange("p i c -> p i c"))
            cvw[nm] = t
        bab = p.sb("l1a_bab", [8, 1], F32, stack=st)
        p.dma("sync", bab[:], aps["L1_attn_b_ctrl"][0:8].rearrange("(a b) -> a b", b=1))
        with ExitStack() as stS:
            scps = p.ps("l1a_scl", [128, 32], F32, stack=stS)
            bogc, bog_d = load_scale_cols(p, core, aps["L1_attn_b_ctrl"][8:1032],
                                          D, st, "bog", scps)
            boutc, bo_d = load_scale_cols(p, core, aps["L1_attn_b_out"], D, st,
                                          "bo", scps, extra_waits=as_waits(bog_d))
            rampc, rp_d = load_scale_cols(p, core, aps["L1_attn_ramp_logit"], D, st,
                                          "rp", scps, extra_waits=as_waits(bo_d))
        p.op("scalar", "activation", rampc[:], rampc[:], AF.Sigmoid,
             waits=as_waits(rp_d), inc="S")
        vec_d = [("S", p.cnt["S"]), ("LD", p.snap("LD"))] \
            + as_waits(merge_counts(bog_d, bo_d, rp_d))

        identb = core.const_sb["ident_bf"]
        mlt = core.local_const(st, "madd_f_lt_p")
        mgt = core.local_const(st, "madd_f_gt_p")
        mge = core.local_const(st, "madd_f_ge_p")
        ones32f = core.local_const(st, "ones32")
        onesb = core.const_sb["ones_bf"]
        identf = core.const_sb["ident"]

        lnp = LNPool(p, st, "A", tc=256)
        xhat_d = p.dtmp("l1a_xhat", [D, T_SEQ], BF16)
        og_d = p.dtmp("l1a_og", [D, T_SEQ], BF16)
        xhc = [p.sb(f"l1a_xhc{i}", [128, 256], BF16, stack=st) for i in range(DT)]
        _xst1 = p.sb("l1a_xst", [128, DT * TC], BF16, stack=st)
        xst = [_xst1, _xst1]
        q_d = p.dtmp("l1a_qd", [D, T_SEQ], BF16)
        k_d = p.dtmp("l1a_kd", [D, T_SEQ], BF16)
        v_d = p.dtmp("l1a_vd", [D, T_SEQ], BF16)
        qp = [p.sb(f"l1a_qp{i}", [128, T_SEQ], BF16, stack=st) for i in range(2)]
        qcb = p.sb("l1a_qcb", [128, DT * CC], BF16, stack=st)
        kcb = p.sb("l1a_kcb", [128, DT * CC], BF16, stack=st)
        vcb = p.sb("l1a_vcb", [128, DT * CC], BF16, stack=st)
        rdf = [p.sb(f"l1a_rd{i}", [128, T_SEQ], BF16, stack=st) for i in range(DT)]
        ogc = [p.sb(f"l1a_ogc{i}", [128, TC], BF16, stack=st) for i in range(DT)]
        yb = [p.sb(f"l1a_yb{i}", [128, TC], BF16, stack=st) for i in range(DT)]
        xld = p.sb("l1a_xld", [128, T_SEQ], BF16, stack=st)
        xin_d = p.dtmp("l1a_xin", [128, T_SEQ + 4], BF16)
        sqb = p.sb("l1a_sqb", [128, TC], BF16, stack=st)
        ogs = sqb
        rsf = p.sb("l1a_rsf", [128, TC], F32, stack=st)
        rsbb = p.sb("l1a_rsbb", [128, TC], BF16, stack=st)
        cums_t = p.sb("l1a_cumt", [128, H * NCH], F32, stack=st)
        beta_t = p.sb("l1a_bett", [128, H * NCH], F32, stack=st)
        abrow = p.sb("l1a_abrow", [32, TC], F32, stack=st)
        abrow2 = p.sb("l1a_abrow2", [32, TC], F32, stack=st)
        zrow = core.const_sb["zeros64"]
        rowc = p.sb("l1a_rowc", [32, CC], F32, stack=st)
        rowb = p.sb("l1a_rowb", [32, CC], F32, stack=st)
        shbuf = p.sb("l1a_shbuf", [32, CC], F32, stack=st)
        Rf = p.sb("l1a_Rf", [128, CC], F32, stack=st)
        RbN = p.sb("l1a_RbN", [128, CC], F32, stack=st)
        Rgb = p.sb("l1a_Rgb", [128, CC], BF16, stack=st)
        Dm = p.sb("l1a_Dm", [128, CC], F32, stack=st)
        Ef = p.sb("l1a_Ef", [128, CC], F32, stack=st)
        Gsb = p.sb("l1a_Gsb", [128, CC], F32, stack=st)
        Xa = p.sb("l1a_Xa", [128, CC], F32, stack=st)
        XaT = p.sb("l1a_XaT", [128, CC], F32, stack=st)
        PTf = p.sb("l1a_PTf", [128, CC], F32, stack=st)
        Wsb = p.sb("l1a_Wsb", [128, HD], F32, stack=st)
        Vf = p.sb("l1a_Vf", [128, HD], F32, stack=st)
        Ktf = p.sb("l1a_Ktf", [128, HD], F32, stack=st)
        qtf = p.sb("l1a_qtf", [128, HD], F32, stack=st)
        Rsb = p.sb("l1a_Rsb", [128, HD], F32, stack=st)
        rdt = p.sb("l1a_rdt", [128, HD], BF16, stack=st)
        colA = p.sb("l1a_colA", [128, 1], F32, stack=st)
        colB = p.sb("l1a_colB", [128, 1], F32, stack=st)
        colG = p.sb("l1a_colG", [128, 1], F32, stack=st)
        Z = [p.sb(f"l1a_Z{i}", [128, HD], F32, stack=st) for i in range(2 * H)]

        psA = p.ps("l1a_pA", [128, TC], F32, stack=st)
        psB = p.ps("l1a_pB", [128, TC], F32, stack=st)
        psW = p.ps("l1a_pW", [128, TC], F32, stack=st)
        psR = p.ps("l1a_pR", [128, TC], F32, stack=st)
        psZ = p.ps("l1a_pZ", [128, TC], F32, stack=st)
        psQ = p.ps("l1a_pQ", [128, TC], F32, stack=st)

        p.dma("gpsimd", xin_d[:, 0:3], core.const_sb["zeros128b"][:, 0:3],
              waits=as_waits(core.res_ready))
        zm = 0
        stream_war = {0: {}, 1: {}}
        ogc_war = {}
        res_r = as_waits(core.res_ready)
        done = {}
        yb_war = {}
        xh_war = {}
        fA = {}
        fB = {}
        fW = {}
        fR = {}
        fZ = {}
        fQ = {}
        ct_war = {}
        conv_war = {}
        for b in range(B_PER_CORE):
            for tc in range(T_SEQ // 256):
                t0 = b * T_SEQ + tc * 256
                d1_ = emit_ln(p, core, lnp, core.res, t0, 256, dst_tiles=xhc,
                              dst_t0=0,
                              src_ready=(res_r + vec_d) if (b == 0 and tc == 0) else (),
                              dst_war=as_waits(xh_war))
                xst_c = None
                for i in range(DT):
                    xst_c = p.dma("scalar",
                                  xhat_d[128 * i:128 * (i + 1), tc * 256:tc * 256 + 256],
                                  xhc[i][:, 0:256], sem="LD2",
                                  waits=as_waits(d1_) if i == 0 else ())
                xh_war = {"LD2": xst_c}
            xw = vec_d + [("LD2", p.snap("LD2"))]

            def stream_xh(cc2, par, extra=()):
                buf = xst[par]
                c = p.dma("gpsimd", buf[:].rearrange("p (i t) -> p i t", t=TC),
                          xhat_d[:, TC * cc2:TC * (cc2 + 1)].rearrange(
                              "(i p) t -> p i t", p=128),
                          waits=xw + list(extra))
                return buf, [("LDG", c)]
            slabs = ((q_d, 0, "w_q"), (k_d, 1024, "w_k"), (v_d, 2048, "w_v"))
            if "2" in L1A_STAGES:
                slabs = ()
            elif "V" in L1A_STAGES:
                slabs = ((v_d, 2048, "w_v"),)
            elif "Q" in L1A_STAGES:
                slabs = ((q_d, 0, "w_q"),)
            qk_d = {}
            pr_d = {}
            for (dram, wcol, cw) in slabs:
                for i in range(DT):
                    p.dma("gpsimd", wbuf[:, 1024 * i:1024 * (i + 1)],
                          win_ap[128 * i:128 * (i + 1), wcol:wcol + 1024],
                          waits=[("T", p.cnt["T"])] if i == 0 else ())
                wld = [("LDG", p.snap("LDG"))]
                for ot in range(DT):
                    dst = qp
                    bkk = ot % 2
                    ps_ = psA if bkk == 0 else psB
                    f_ = fA if bkk == 0 else fB
                    if "N" in L1A_STAGES:
                        for cc2 in range(T_SEQ // TC):
                            lc = cc2 * TC
                            sbuf_x, sxw = stream_xh(cc2, 0,
                                                    extra=as_waits(stream_war[0]))
                            t1 = mm_group(p, ps_[:],
                                          [(wbuf[:, 1024 * i + 128 * ot:1024 * i + 128 * (ot + 1)],
                                            sbuf_x[:, TC * i:TC * (i + 1)]) for i in range(DT)],
                                          waits_first=xw + wld + sxw + as_waits(f_))
                            stream_war[0] = {"T": t1}
                            s1 = p.op("scalar", "activation",
                                      dst[ot % 2][:, lc:lc + TC], ps_[:], AF.Silu,
                                      waits=[("T", t1)] + (as_waits(conv_war) if cc2 == 0 else []),
                                      inc="S")
                            f_.clear()
                            f_.update({"S": s1})
                        if ot % 2 == 1:
                            wv = [("S", p.cnt["S"])]
                            for tt in range(2):
                                c2 = p.dma("scalar",
                                           dram[128 * (ot - 1 + tt):128 * (ot + tt), :],
                                           dst[tt][:], sem="LD2",
                                           waits=wv if tt == 0 else ())
                            conv_war = {"LD2": c2, "S": p.cnt["S"]}
                        continue
                    for cc2 in range(T_SEQ // TC):
                        lc = cc2 * TC
                        sbuf_x, sxw = stream_xh(cc2, 0,
                                                extra=as_waits(stream_war[0]))
                        t1 = mm_group(p, ps_[:],
                                      [(wbuf[:, 1024 * i + 128 * ot:1024 * i + 128 * (ot + 1)],
                                        sbuf_x[:, TC * i:TC * (i + 1)]) for i in range(DT)],
                                      waits_first=xw + wld + sxw + as_waits(f_))
                        stream_war[0] = {"T": t1}
                        s1 = p.op("scalar", "activation", xld[:, lc:lc + TC],
                                  ps_[:], AF.Copy,
                                  waits=[("T", t1)] + (as_waits(conv_war) if cc2 == 0 else []),
                                  inc="S")
                        f_.clear()
                        f_.update({"S": s1})
                    cwt = cvw[cw]
                    dstt = dst[ot % 2]
                    sst = p.dma("gpsimd", xin_d[:, 3:3 + T_SEQ], xld[:],
                                waits=[("S", p.cnt["S"])] + as_waits(conv_war),
                                sem="LDG")
                    vy = None
                    for j in range(KW):
                        lj = p.dma("gpsimd", xld[:], xin_d[:, j:j + T_SEQ],
                                   sem="LDG",
                                   waits=[("LDG", sst)] if j == 0 else
                                         [("V", vy)])
                        if j == 0:
                            vy = p.op("vector", "tensor_scalar_mul", dstt[:],
                                      xld[:], cwt[:, 4 * ot:4 * ot + 1],
                                      waits=[("LDG", lj)], inc="V")
                        else:
                            vy = p.op("vector", "scalar_tensor_tensor", dstt[:],
                                      xld[:], cwt[:, 4 * ot + j:4 * ot + j + 1],
                                      dstt[:], ALU.mult, ALU.add,
                                      waits=[("LDG", lj)], inc="V")
                    sv = p.op("scalar", "activation", xld[:], dstt[:],
                              AF.Silu, waits=[("V", vy)], inc="S")
                    vcp = p.op("vector", "tensor_copy", dstt[:], xld[:],
                               waits=[("S", sv)], inc="V")
                    conv_war = {"S": sv, "V": vcp, "G": p.cnt["G"]}
                    if ot % 2 == 1:
                        h = ot // 2
                        pr_d = {"S": p.cnt["S"]}
                        if wcol != 2048:
                            for cc2 in range(T_SEQ // TC):
                                lc = cc2 * TC
                                v1 = p.op("vector", "tensor_mul", sqb[:, 0:TC],
                                          dst[0][:, lc:lc + TC],
                                          dst[0][:, lc:lc + TC],
                                          waits=as_waits(pr_d) if cc2 == 0 else (),
                                          inc="V")
                                p.op("tensor", "matmul", psA[:], onesb[:],
                                     sqb[:, 0:TC], start=True, stop=False,
                                     waits=[("V", v1)] + as_waits(fA))
                                v2 = p.op("vector", "tensor_mul", sqb[:, 0:TC],
                                          dst[1][:, lc:lc + TC],
                                          dst[1][:, lc:lc + TC],
                                          waits=[("V", v1)], inc="V")
                                t2 = p.op("tensor", "matmul", psA[:], onesb[:],
                                          sqb[:, 0:TC], start=False, stop=True,
                                          waits=[("V", v2)], inc="T")
                                s2 = p.op("scalar", "activation", rsf[:, 0:TC],
                                          psA[:], AF.Sqrt,
                                          waits=[("T", t2)], inc="S")
                                fA.clear()
                                fA.update({"S": s2})
                                p.op("vector", "tensor_scalar_max", rsf[:, 0:TC],
                                     rsf[:, 0:TC], 1e-6, waits=[("S", s2)])
                                p.op("vector", "reciprocal", rsf[:, 0:TC],
                                     rsf[:, 0:TC])
                                v3 = p.op("vector", "tensor_copy", rsbb[:, 0:TC],
                                          rsf[:, 0:TC], inc="V")
                                for tt in range(2):
                                    p.op("vector", "tensor_mul",
                                         dst[tt][:, lc:lc + TC],
                                         dst[tt][:, lc:lc + TC],
                                         rsbb[:, 0:TC],
                                         waits=[("V", v3)] if tt == 0 else ())
                        vfin = p.op("vector", "engine_nop", inc="V") if False else None
                        wv = [("V", p.cnt["V"]), ("S", p.cnt["S"]), ("G", p.cnt["G"])]
                        for tt in range(2):
                            c2 = p.dma("scalar",
                                       dram[128 * (2 * h + tt):128 * (2 * h + tt + 1), :],
                                       dst[tt][:], sem="LD2",
                                       waits=wv if tt == 0 else ())
                        conv_war = merge_counts(conv_war, {"LD2": c2})
                qk_d = {"V": p.cnt["V"], "S": p.cnt["S"], "G": p.cnt["G"],
                        "LD2": p.snap("LD2")}
            if "3" in L1A_STAGES:
                qk_d = {}
                ctl_d = {}
                continue
            for i in range(DT):
                p.dma("gpsimd", wbuf[:, 1024 * i:1024 * (i + 1)],
                      wctrl_ap[128 * i:128 * (i + 1), 8:1032],
                      waits=[("T", p.cnt["T"])] if i == 0 else ())
            p.dma("gpsimd", wab[:].rearrange("p (i c) -> p i c", c=8),
                  wctrl_ap[:, 0:8].rearrange("(i p) c -> p i c", p=128))
            wld2 = [("LDG", p.snap("LDG"))]
            og_war = {}
            for ot in range(DT):
                bkk = ot % 2
                ps_ = psA if bkk == 0 else psB
                f_ = fA if bkk == 0 else fB
                for cc2 in range(T_SEQ // TC):
                    lc = cc2 * TC
                    sbuf_x, sxw = stream_xh(cc2, 0,
                                            extra=as_waits(stream_war[0]))
                    t1 = mm_group(p, ps_[:],
                                  [(wbuf[:, 1024 * i + 128 * ot:1024 * i + 128 * (ot + 1)],
                                    sbuf_x[:, TC * i:TC * (i + 1)]) for i in range(DT)],
                                  waits_first=xw + wld2 + sxw + as_waits(f_))
                    stream_war[0] = {"T": t1}
                    s1 = p.op("scalar", "activation", ogs[:, 0:TC], ps_[:],
                              AF.Sigmoid, bias=bogc[:, ot:ot + 1],
                              waits=[("T", t1)] + as_waits(og_war), inc="S")
                    if b == 0 and ot == 0 and cc2 == 0:
                        tap(p, "l1a_ogs0", ogs[:, 0:TC], [("S", s1)],
                            block_eng="scalar")
                        tap(p, "l1a_ogx0", sbuf_x[:, 0:TC], [("S", s1)],
                            block_eng=None)
                        tap(p, "l1a_ogw0", wbuf[:, 0:1024], [("S", s1)],
                            block_eng=None)
                        tap(p, "l1a_ogb0", bogc[:], [("S", s1)], block_eng=None)
                    c1 = p.dma("scalar", og_d[128 * ot:128 * (ot + 1), lc:lc + TC],
                               ogs[:, 0:TC], sem="LD2", waits=[("S", s1)])
                    og_war = {"LD2": c1}
                    f_.clear()
                    f_.update({"S": s1})
            ab_d = {}
            for cc2 in range(T_SEQ // TC):
                lc = cc2 * TC
                sbuf_x, sxw = stream_xh(cc2, 0,
                                        extra=as_waits(stream_war[0]))
                t1 = mm_group(p, psQ[0:8, :],
                              [(wab[:, 8 * i:8 * (i + 1)],
                                sbuf_x[:, TC * i:TC * (i + 1)])
                               for i in range(DT)],
                              waits_first=xw + wld2 + sxw + as_waits(fQ))
                stream_war[0] = {"T": t1}
                s1 = p.op("scalar", "activation", rowc[0:8, 0:TC] if False else
                          abrow[0:8, 0:TC], psQ[0:8, :], AF.Sigmoid,
                          bias=bab[0:8, 0:1], waits=[("T", t1)], inc="S")
                fQ = {"S": s1}
                v1 = p.op("vector", "tensor_scalar", abrow2[0:4, 0:TC],
                          abrow[0:4, 0:TC], 0.98, 0.01, ALU.mult, ALU.add,
                          waits=[("S", s1)])
                if b == 0 and cc2 == 0:
                    tap(p, "l1a_ab", abrow[0:8, 0:TC], [("S", s1)])
                    tap(p, "l1a_ap", abrow2[0:4, 0:TC], [("V", p.cnt["V"])])
                s2 = p.op("scalar", "activation", abrow2[0:4, 0:TC],
                          abrow2[0:4, 0:TC], AF.Ln, waits=[("V", p.cnt["V"])], inc="S")
                if b == 0 and cc2 == 0:
                    tap(p, "l1a_la", abrow2[0:4, 0:TC], [("S", s2)], block_eng="scalar")
                for q2 in range(TC // CC):
                    ci = cc2 * 4 + q2
                    l2 = q2 * CC
                    v2 = p.op("vector", "tensor_tensor_scan", shbuf[0:4, 0:CC],
                              abrow2[0:4, l2:l2 + CC], zrow[0:4, 0:CC], 0.0,
                              ALU.add, ALU.add,
                              waits=[("S", s2)] + as_waits(fW), inc="V")
                    if b == 0 and ci == 0:
                        tap(p, "l1a_sc0", shbuf[0:4, 0:CC], [("V", v2)])
                    t3 = p.op("tensor", "matmul", psW[0:128, 0:4],
                              shbuf[0:32, 0:CC], identf[0:32, 0:4],
                              start=True, stop=True,
                              waits=[("V", v2)] + as_waits(fW), inc="T")
                    s3 = p.op("scalar", "activation",
                              cums_t[:, 4 * ci:4 * ci + 4], psW[0:128, 0:4],
                              AF.Copy, waits=[("T", t3)] + as_waits(ct_war), inc="S")
                    fW = {"S": s3}
                    t4 = p.op("tensor", "matmul", psW[0:128, 4:8],
                              abrow[0:32, l2:l2 + CC], identf[0:32, 4:8],
                              start=True, stop=True, waits=[("S", s3)], inc="T")
                    s4 = p.op("scalar", "activation",
                              beta_t[:, 4 * ci:4 * ci + 4], psW[0:128, 4:8],
                              AF.Copy, waits=[("T", t4)], inc="S")
                    fW = {"S": s4}
            ctl_d = {"S": p.cnt["S"], "V": p.cnt["V"]}
            if b == 0:
                tap(p, "l1a_qd", q_d[:], as_waits(qk_d))
                tap(p, "l1a_kd", k_d[:], as_waits(qk_d))
                tap(p, "l1a_vd", v_d[:], as_waits(qk_d))
                tap(p, "l1a_xh", xhat_d[:], [("LD2", p.snap("LD2"))])
                tap(p, "l1a_ct", cums_t[:], as_waits(ctl_d))
                tap(p, "l1a_bt", beta_t[:], as_waits(ctl_d))
            aw = as_waits(merge_counts(qk_d, ctl_d))
            qkv_war = {}
            fQt = {}
            for ci in range(NCH if "B" in L1A_STAGES else 0):
                lc = ci * CC
                cl = None
                for (dram, buf) in ((q_d, qcb), (k_d, kcb), (v_d, vcb)):
                    cl = p.dma("gpsimd", buf[:].rearrange("p (i t) -> p i t", t=CC),
                               dram[:, lc:lc + CC].rearrange("(i p) t -> p i t", p=128),
                               waits=(aw + as_waits(qkv_war)) if dram is q_d else ())
                qkv_war = {}
                cw2 = [("LDG", cl)]

                def khs(tt):
                    return kcb[:, CC * (2 * h + tt):CC * (2 * h + tt) + CC]

                def qhs(tt):
                    return qcb[:, CC * (2 * h + tt):CC * (2 * h + tt) + CC]

                def vhs(tt):
                    return vcb[:, CC * (2 * h + tt):CC * (2 * h + tt) + CC]
                t5 = p.op("tensor", "matmul", psQ[0:4, 0:CC],
                          cums_t[:, 4 * ci:4 * ci + 4], identf[:],
                          start=True, stop=True, waits=aw + as_waits(fQ), inc="T")
                s5 = p.op("scalar", "activation", rowc[0:4, 0:CC], psQ[0:4, 0:CC],
                          AF.Copy, waits=[("T", t5)], inc="S")
                t6 = p.op("tensor", "matmul", psQ[0:4, 0:CC],
                          beta_t[:, 4 * ci:4 * ci + 4], identf[:],
                          start=True, stop=True, waits=[("S", s5)], inc="T")
                s6 = p.op("scalar", "activation", rowb[0:4, 0:CC], psQ[0:4, 0:CC],
                          AF.Copy, waits=[("T", t6)], inc="S")
                fQ = {"S": s6}
                row_d = [("S", s6)]
                for h in range(H):
                    ccol = cums_t[:, 4 * ci + h:4 * ci + h + 1]
                    bcol = beta_t[:, 4 * ci + h:4 * ci + h + 1]
                    v7 = p.op("vector", "stream_shuffle", shbuf[:, 0:CC],
                              rowc[0:32, 0:CC], [h] * 32, waits=row_d, inc="V")
                    t7 = p.op("tensor", "matmul", psQ[:, 0:CC], ones32f[:],
                              shbuf[:, 0:CC], start=True, stop=True,
                              waits=[("V", v7)] + as_waits(fQ), inc="T")
                    v8 = p.op("vector", "tensor_copy", Rf[:, 0:CC], psQ[:, 0:CC],
                              waits=[("T", t7)], inc="V")
                    v9 = p.op("vector", "stream_shuffle", shbuf[:, 0:CC],
                              rowb[0:32, 0:CC], [h] * 32, inc="V")
                    t8 = p.op("tensor", "matmul", psQ[:, 0:CC], ones32f[:],
                              shbuf[:, 0:CC], start=True, stop=True,
                              waits=[("V", v9), ("V", v8)], inc="T")
                    s8 = p.op("scalar", "activation", RbN[:, 0:CC], psQ[:, 0:CC],
                              AF.Copy, scale=core.fscal(-1.0),
                              waits=[("T", t8)], inc="S")
                    fQ = {"S": s8, "V": v8}
                    tG = mm_group(p, psA[:, 0:CC],
                                  [(khs(tt), khs(tt)) for tt in range(2)],
                                  waits_first=cw2 + as_waits(fA))
                    vG = p.op("vector", "tensor_copy", Gsb[:, 0:CC], psA[:, 0:CC],
                              waits=[("T", tG)], inc="V")
                    fA = {"V": vG}
                    p.op("vector", "tensor_scalar", Dm[:, 0:CC], Rf[:, 0:CC],
                         ccol, -1.0, ALU.subtract, ALU.mult, waits=[("V", v8)])
                    p.op("vector", "tensor_add", Dm[:, 0:CC], Dm[:, 0:CC], mlt[:, 0:CC])
                    sE = p.op("scalar", "activation", Ef[:, 0:CC], Dm[:, 0:CC],
                              AF.Exp, waits=[("V", p.cnt["V"])], inc="S")
                    p.op("vector", "tensor_mul", Ef[:, 0:CC], Ef[:, 0:CC],
                         Gsb[:, 0:CC], waits=[("S", sE), ("V", vG)])
                    vX = p.op("vector", "tensor_scalar", Xa[:, 0:CC], Ef[:, 0:CC],
                              bcol, -1.0, ALU.mult, ALU.mult, inc="V")
                    p.op("vector", "tensor_scalar", Dm[:, 0:CC], Rf[:, 0:CC],
                         ccol, None, ALU.subtract)
                    p.op("vector", "tensor_add", Dm[:, 0:CC], Dm[:, 0:CC], mgt[:, 0:CC])
                    sE2 = p.op("scalar", "activation", Ef[:, 0:CC], Dm[:, 0:CC],
                               AF.Exp, waits=[("V", p.cnt["V"])], inc="S")
                    p.op("vector", "tensor_mul", Ef[:, 0:CC], Ef[:, 0:CC],
                         Gsb[:, 0:CC], waits=[("S", sE2)])
                    vXT = p.op("vector", "tensor_mul", XaT[:, 0:CC], Ef[:, 0:CC],
                               RbN[:, 0:CC], waits=[("S", s8)], inc="V")
                    tKQ = mm_group(p, psB[:, 0:CC],
                                   [(khs(tt), qhs(tt)) for tt in range(2)],
                                   waits_first=cw2 + as_waits(fB))
                    p.op("vector", "tensor_scalar", Dm[:, 0:CC], Rf[:, 0:CC],
                         ccol, None, ALU.subtract)
                    p.op("vector", "tensor_add", Dm[:, 0:CC], Dm[:, 0:CC], mge[:, 0:CC])
                    sE3 = p.op("scalar", "activation", PTf[:, 0:CC], Dm[:, 0:CC],
                               AF.Exp, waits=[("V", p.cnt["V"])], inc="S")
                    vPT = p.op("vector", "tensor_tensor", PTf[:, 0:CC], PTf[:, 0:CC],
                               psB[:, 0:CC], ALU.mult,
                               waits=[("S", sE3), ("T", tKQ)], inc="V")
                    fB = {"V": vPT}
                    for (srcf, dstt) in ((vhs, Vf), (khs, Ktf)):
                        for tt in range(2):
                            tT = p.op("tensor", "matmul", psZ[:, 128 * tt:128 * (tt + 1)],
                                      srcf(tt), identb[:],
                                      start=True, stop=True,
                                      waits=(cw2 + as_waits(fZ)) if tt == 0 else (),
                                      inc="T")
                        vT = p.op("vector", "tensor_copy", dstt[:, 0:HD],
                                  psZ[:, 0:HD], waits=[("T", p.cnt["T"])], inc="V")
                        fZ = {"V": vT}
                    sRg = p.op("scalar", "activation", Rgb[:, 0:CC], Rf[:, 0:CC],
                               AF.Exp, waits=[("V", v8)], inc="S")
                    for tt in range(2):
                        p.op("vector", "tensor_tensor", sqb[:, 0:CC],
                             qhs(tt), Rgb[:, 0:CC], ALU.mult,
                             waits=[("S", sRg)] + cw2 + as_waits(fQt))
                        p.op("vector", "tensor_copy",
                             qtf[:, 128 * tt:128 * (tt + 1)], sqb[:, 0:CC],
                             inc="V")
                        fQt = {}
                    if ci > 0:
                        vkc = None
                        for tt in range(2):
                            vkc = p.op("vector", "tensor_copy",
                                       Rsb[:, 128 * tt:128 * (tt + 1)], khs(tt),
                                       waits=cw2 + as_waits(fR) if tt == 0 else (),
                                       inc="V")
                        tKZ = mm_group(p, psW[:, 0:HD],
                                       [(Rsb[:, 128 * tt:128 * (tt + 1)],
                                         Z[2 * h + tt][:])
                                        for tt in range(2)],
                                       waits_first=[("V", vkc)] + as_waits(fW))
                        sG = p.op("scalar", "activation", colG[:], ccol, AF.Exp,
                                  waits=[("V", p.cnt["V"])], inc="S")
                        vbg = p.op("vector", "tensor_mul", colA[:], colG[:], bcol,
                                   waits=[("S", sG)], inc="V")
                        vW = p.op("vector", "tensor_scalar", Wsb[:, 0:HD],
                                  psW[:, 0:HD], colA[:, 0:1], -1.0,
                                  ALU.mult, ALU.mult,
                                  waits=[("T", tKZ), ("V", vbg)], inc="V")
                        vW = p.op("vector", "scalar_tensor_tensor", Wsb[:, 0:HD],
                                  Vf[:, 0:HD], bcol, Wsb[:, 0:HD],
                                  ALU.mult, ALU.add, inc="V")
                        fW = {"V": vW}
                    else:
                        vW = p.op("vector", "tensor_scalar_mul", Wsb[:, 0:HD],
                                  Vf[:, 0:HD], bcol, waits=as_waits(fW), inc="V")
                        fW = {"V": vW}
                    Xc, XcT, Xn, XnT = Xa, XaT, Dm, Ef
                    for lvl in range(7):
                        ta = p.op("tensor", "matmul", psW[:, 0:HD], XcT[:, 0:CC],
                                  Wsb[:, 0:HD], start=True, stop=True,
                                  waits=[("V", p.cnt["V"])] +
                                        ([("V", vX), ("V", vXT)] if lvl == 0 else []),
                                  inc="T")
                        vW = p.op("vector", "tensor_add", Wsb[:, 0:HD], Wsb[:, 0:HD],
                                  psW[:, 0:HD], waits=[("T", ta)], inc="V")
                        if lvl < 6:
                            tsq = p.op("tensor", "matmul", psR[:, 0:CC], XcT[:, 0:CC],
                                       Xc[:, 0:CC], start=True, stop=True,
                                       waits=as_waits(fR), inc="T")
                            vsq = p.op("vector", "tensor_copy", Xn[:, 0:CC],
                                       psR[:, 0:CC], waits=[("T", tsq)], inc="V")
                            tsqT = p.op("tensor", "matmul", psR[:, 0:CC], Xc[:, 0:CC],
                                        XcT[:, 0:CC], start=True, stop=True,
                                        waits=[("V", vsq)], inc="T")
                            vsqT = p.op("vector", "tensor_copy", XnT[:, 0:CC],
                                        psR[:, 0:CC], waits=[("T", tsqT)], inc="V")
                            fR = {"V": vsqT}
                            Xc, XcT, Xn, XnT = Xn, XnT, Xc, XcT
                    fW = {"V": vW}
                    nR = 0
                    if ci > 0:
                        for tt in range(2):
                            p.op("tensor", "matmul", psR[:, 0:HD],
                                 qtf[:, 128 * tt:128 * (tt + 1)], Z[2 * h + tt][:],
                                 start=(tt == 0), stop=False,
                                 waits=as_waits(fR) if tt == 0 else ())
                        nR = 2
                    tR = p.op("tensor", "matmul", psR[:, 0:HD], PTf[:, 0:CC],
                              Wsb[:, 0:HD], start=(nR == 0), stop=True,
                              waits=(as_waits(fR) + [("V", vPT), ("V", vW)])
                                    if nR == 0 else [("V", vPT), ("V", vW)],
                              inc="T")
                    fQt = {"T": tR}
                    vcl = p.op("vector", "tensor_scalar", colB[:],
                               Rf[:, CC - 1:CC], ccol, None, ALU.subtract,
                               waits=[("V", v8)])
                    se2 = p.op("scalar", "activation", colB[:], colB[:], AF.Exp,
                               waits=[("V", p.cnt["V"])], inc="S")
                    sgc = p.op("scalar", "activation", colG[:], Rf[:, CC - 1:CC],
                               AF.Exp, waits=[("V", p.cnt["V"])], inc="S")
                    vwt = p.op("vector", "tensor_scalar_mul", Wsb[:, 0:HD],
                               Wsb[:, 0:HD], colB[:, 0:1],
                               waits=[("S", se2), ("V", vW), ("T", tR)], inc="V")
                    for tt in range(2):
                        tz = p.op("tensor", "matmul", psZ[:, 0:HD],
                                  Ktf[:, 128 * tt:128 * (tt + 1)], Wsb[:, 0:HD],
                                  start=True, stop=True,
                                  waits=[("V", vwt)] + as_waits(fZ), inc="T")
                        if ci > 0:
                            vz2 = p.op("vector", "scalar_tensor_tensor",
                                       Z[2 * h + tt][:], Z[2 * h + tt][:],
                                       colG[:, 0:1], psZ[:, 0:HD],
                                       ALU.mult, ALU.add,
                                       waits=[("T", tz), ("S", sgc)], inc="V")
                        else:
                            vz2 = p.op("vector", "tensor_copy", Z[2 * h + tt][:],
                                       psZ[:, 0:HD], waits=[("T", tz)], inc="V")
                        fZ = {"V": vz2}
                    vR = p.op("vector", "tensor_copy", Rsb[:, 0:HD], psR[:, 0:HD],
                              waits=[("T", tR)], inc="V")
                    fR = {"V": vR}
                    vsq2 = p.op("vector", "tensor_mul", Vf[:, 0:HD], Rsb[:, 0:HD],
                                Rsb[:, 0:HD])
                    vrs = p.op("vector", "reduce_sum", colA[:], Vf[:, 0:HD], AX.X)
                    srs = p.op("scalar", "activation", colA[:], colA[:], AF.Sqrt,
                               scale=core.fscal(1.0 / HD), bias=core.fscal(1e-6),
                               waits=[("V", p.cnt["V"])], inc="S")
                    vrc = p.op("vector", "reciprocal", colB[:], colA[:],
                               waits=[("S", srs)])
                    vrd = p.op("vector", "tensor_scalar_mul", rdt[:, 0:HD],
                               Rsb[:, 0:HD], colB[:, 0:1], inc="V")
                    for tt in range(2):
                        tt2 = p.op("tensor", "matmul", psQ[:, 0:CC],
                                   rdt[:, 128 * tt:128 * (tt + 1)], identb[:],
                                   start=True, stop=True,
                                   waits=[("V", vrd)] + as_waits(fQ), inc="T")
                        sfm = p.op("scalar", "activation",
                                   rdf[2 * h + tt][:, lc:lc + CC], psQ[:, 0:CC],
                                   AF.Copy, waits=[("T", tt2)], inc="S")
                        fQ = {"S": sfm}
                    qkv_war = merge_counts(qkv_war, {"T": p.cnt["T"],
                                                     "V": p.cnt["V"]})
            rd_done = {"S": p.cnt["S"], "V": p.cnt["V"]}
            if b == 0:
                for i in range(DT):
                    tap(p, f"l1a_rdf{i}", rdf[i][:], as_waits(rd_done))
            if "C" not in L1A_STAGES:
                continue
            for i in range(DT):
                p.dma("gpsimd", wbuf[:, 1024 * i:1024 * (i + 1)],
                      wout_ap[128 * i:128 * (i + 1), :],
                      waits=[("T", p.cnt["T"])] if i == 0 else ())
            wld3 = [("LDG", p.snap("LDG"))]
            yw = as_waits(rd_done)
            for tc in range(T_SEQ // TC):
                t0 = b * T_SEQ + tc * TC
                lc = tc * TC
                ydone = {}
                ogl = None
                for i in range(DT):
                    ogl = p.dma("gpsimd", ogc[i][:, 0:TC],
                                og_d[128 * i:128 * (i + 1), lc:lc + TC],
                                waits=(as_waits(ogc_war) + [("LD2", p.snap("LD2"))])
                                      if i == 0 else ())
                for ot in range(DT):
                    bkk = ot % 2
                    ps_ = psA if bkk == 0 else psB
                    f_ = fA if bkk == 0 else fB
                    tmm = mm_group(p, ps_[:],
                                   [(wbuf[:, 1024 * i + 128 * ot:1024 * (i) + 128 * ot + 128],
                                     rdf[i][:, lc:lc + TC]) for i in range(DT)],
                                   waits_first=yw + wld3 + as_waits(f_))
                    sy = p.op("scalar", "activation", yb[ot][:, 0:TC], ps_[:],
                              AF.Identity, bias=boutc[:, ot:ot + 1],
                              waits=[("T", tmm)] + as_waits(yb_war), inc="S")
                    if b == 0 and tc == 0 and ot == 0:
                        tap(p, "l1a_cy0", yb[0][:, 0:TC], [("S", sy)],
                            block_eng="scalar")
                    vy1 = p.op("vector", "tensor_mul", yb[ot][:, 0:TC],
                               yb[ot][:, 0:TC], ogc[ot][:, 0:TC],
                               waits=[("S", sy), ("LDG", ogl)])
                    if b == 0 and tc == 0 and ot == 0:
                        tap(p, "l1a_cy1", yb[0][:, 0:TC], [("V", p.cnt["V"])])
                        tap(p, "l1a_cog0", ogc[0][:, 0:TC], [("LDG", ogl)],
                            block_eng=None)
                    vy2 = p.op("vector", "tensor_scalar_mul", yb[ot][:, 0:TC],
                               yb[ot][:, 0:TC], rampc[:, ot:ot + 1], inc="V")
                    if b == 0 and tc == 0 and ot == 0:
                        tap(p, "l1a_cy2", yb[0][:, 0:TC], [("V", vy2)])
                        tap(p, "l1a_cramp", rampc[:], [("V", vy2)])
                        tap(p, "l1a_cbout", boutc[:], [("V", vy2)])
                        tap(p, "l1a_cwb", wbuf[:, 0:1024], [("V", vy2)])
                    f_.clear()
                    f_.update({"V": vy2})
                    ydone = {"V": vy2}
                ogc_war = {"V": p.cnt["V"]}
                for _h2 in range(2):
                    d2_ = emit_ln(p, core, lnp, yb, 256 * _h2, 256, res_add=True,
                                  res_t0=t0 + 256 * _h2,
                                  src_ready=as_waits(ydone) if _h2 == 0 else ())
                yb_war = {"T": lnp.prev["T"], "Gap": d2_["Gap"], "V": d2_["V"]}
                done = d2_
            xh_war = merge_counts({"T": p.cnt["T"]}, yb_war)
            ct_war = {"T": p.cnt["T"]}
            conv_war = merge_counts(conv_war, {"T": p.cnt["T"]})
        core.res_ready = merge_counts(core.res_ready, done)

BLOCK_CONSTS = {
    "l0f": ["ones_f32"],
    "l2f": ["ident"],
    "l3a": [],
    "l0a": ["bones64_bf"],
    "l2a": ["ident", "zeros64"],
    "l1a": ["ident", "zeros64", "zeros128b"],
}

BLOCK_WEIGHTS = {
    "l3f": ["L3_ffn_weight"],
    "l1f": ["L1_ffn_w_gate", "L1_ffn_w_up", "L1_ffn_w_down"],
    "l0f": ["L0_ffn_w_gate", "L0_ffn_w1_e", "L0_ffn_w3_e", "L0_ffn_w2_e",
            "L0_ffn_w1_s", "L0_ffn_w3_s", "L0_ffn_w2_s"],
    "l3a": ["L3_attn_W_DQ", "L3_attn_W_UQ", "L3_attn_W_QR", "L3_attn_W_DKV",
            "L3_attn_W_UK", "L3_attn_W_UV", "L3_attn_W_KR", "L3_attn_W_O"],
    "l0a": ["L0_attn_Wq", "L0_attn_Wk", "L0_attn_Wv", "L0_attn_Wo",
            "L0_attn_q_norm_w", "L0_attn_k_norm_w"],
    "l2a": ["L2_attn_qkv_w", "L2_attn_qkv_b", "L2_attn_out_w", "L2_attn_out_b"],
    "l1a": ["L1_attn_W_in", "L1_attn_W_ctrl", "L1_attn_b_ctrl", "L1_attn_w_q",
            "L1_attn_w_k", "L1_attn_w_v", "L1_attn_W_out", "L1_attn_b_out",
            "L1_attn_ramp_logit"],
    "l2f": ["L2_ffn_w1_w", "L2_ffn_w1_s", "L2_ffn_w2_w", "L2_ffn_w2_s",
            "L2_ffn_h1_w", "L2_ffn_h1_s", "L2_ffn_h2_w", "L2_ffn_h2_s",
            "L2_ffn_w3_w", "L2_ffn_w3_s"],
}

WEIGHT_SHAPES = {
    "L3_ffn_weight": (1024, 1024),
    "L0_ffn_w_gate": (16, 1024),
    "L0_ffn_w1_e": (16, 512, 1024),
    "L0_ffn_w3_e": (16, 512, 1024),
    "L0_ffn_w2_e": (16, 1024, 512),
    "L0_ffn_w1_s": (1024, 1024),
    "L0_ffn_w3_s": (1024, 1024),
    "L0_ffn_w2_s": (1024, 1024),
    "L3_attn_W_DQ": (1024, 256),
    "L3_attn_W_UQ": (256, 1024),
    "L3_attn_W_QR": (256, 512),
    "L3_attn_W_DKV": (1024, 256),
    "L3_attn_W_UK": (256, 1024),
    "L3_attn_W_UV": (256, 1024),
    "L3_attn_W_KR": (1024, 32),
    "L3_attn_W_O": (1024, 1024),
    "L0_attn_Wq": (1024, 1024),
    "L0_attn_Wk": (1024, 1024),
    "L0_attn_Wv": (1024, 1024),
    "L0_attn_Wo": (1024, 1024),
    "L0_attn_q_norm_w": (64,),
    "L0_attn_k_norm_w": (64,),
    "L2_attn_qkv_w": (3072, 1024),
    "L2_attn_qkv_b": (3072,),
    "L2_attn_out_w": (1024, 1024),
    "L2_attn_out_b": (1024,),
    "L1_attn_W_in": (1024, 3072),
    "L1_attn_W_ctrl": (1024, 1032),
    "L1_attn_b_ctrl": (1032,),
    "L1_attn_w_q": (1024, 1, 4),
    "L1_attn_w_k": (1024, 1, 4),
    "L1_attn_w_v": (1024, 1, 4),
    "L1_attn_W_out": (1024, 1024),
    "L1_attn_b_out": (1024,),
    "L1_attn_ramp_logit": (1024,),
    "L1_ffn_w_gate": (3072, 1024),
    "L1_ffn_w_up": (3072, 1024),
    "L1_ffn_w_down": (1024, 3072),
    "L2_ffn_w1_w": (4096, 1024),
    "L2_ffn_w1_s": (4096,),
    "L2_ffn_w2_w": (4096, 1024),
    "L2_ffn_w2_s": (4096,),
    "L2_ffn_h1_w": (4096, 4096),
    "L2_ffn_h1_s": (4096,),
    "L2_ffn_h2_w": (4096, 4096),
    "L2_ffn_h2_s": (4096,),
    "L2_ffn_w3_w": (1024, 4096),
    "L2_ffn_w3_s": (1024,),
}

WEIGHT_DTYPES = {
    "L1_ffn_w_gate": I32,
    "L1_ffn_w_up": I32,
    "L1_ffn_w_down": I32,
}

BLOCK_EMITTERS = {}


def _register():
    BLOCK_EMITTERS["l3f"] = lambda p, core, aps: emit_l3f(p, core, aps["L3_ffn_weight"])
    BLOCK_EMITTERS["l1f"] = emit_l1f
    BLOCK_EMITTERS["l2f"] = emit_l2f
    BLOCK_EMITTERS["l0f"] = emit_l0f
    BLOCK_EMITTERS["l3a"] = emit_l3a
    BLOCK_EMITTERS["l0a"] = emit_l0a
    BLOCK_EMITTERS["l2a"] = emit_l2a
    BLOCK_EMITTERS["l1a"] = emit_l1a


_register()


def build_chain_prog(blocks, first, last):
    p = Prog("_".join(blocks))
    cnames = ["ident_bf", "ones_bf", "ones32_bf", "ident", "zeros64"]
    for b in blocks:
        for cn in BLOCK_CONSTS.get(b, []):
            if cn not in cnames:
                cnames.append(cn)
    core = Core(p, const_names=tuple(cnames))
    xin = p.din("xin", [D, NT], F32 if first else BF16)
    emit_load_res(p, core, xin, fp32_src=first)
    if "res_in" in TAPS:
        ri = p.dout("tap_res_in", [D, 512], F32 if first else BF16)
        for i in range(DT):
            p.dma("gpsimd", ri[128 * i:128 * (i + 1), :],
                  core.res[i][:, 0:512], sem="STG",
                  waits=as_waits(core.res_ready) if i == 0 else ())
    cld = core.load_consts()
    core.res_ready = merge_counts(core.res_ready, cld)
    weight_aps = {}
    for b in blocks:
        for wname in BLOCK_WEIGHTS[b]:
            weight_aps[wname] = p.din(wname, WEIGHT_SHAPES[wname],
                                      WEIGHT_DTYPES.get(wname, F32))
    for bi, b in enumerate(blocks):
        if bi > 0:
            p.barrier()
        BLOCK_EMITTERS[b](p, core, weight_aps)
    xout = p.dout("xout", [D, NT], F32 if last else BF16)
    stc = emit_store_res(p, core, xout, fp32_dst=last)
    if _WTD3_ECHO:
        wt = _WTD3_ECHO.pop()
        w3e = p.dout("tap_wtd3end", [wt.I, wt.O], BF16)
        p.dma("gpsimd", w3e[:], wt.wtd[:], sem="STG", waits=as_waits(stc))
        stc = {"STG": p.snap("STG")}
    if DEBUG_ECHO:
        xe = p.dout("xin_echo", [D, 64], F32 if first else BF16)
        p.dma("gpsimd", xe[:], xin[:, 0:64], sem="STG", waits=as_waits(stc))
        for wname in DEBUG_ECHO:
            if wname in weight_aps:
                wa = weight_aps[wname]
                we = p.dout(wname + "_echo", [128, 128],
                            WEIGHT_DTYPES.get(wname, F32))
                p.dma("gpsimd", we[:], wa[0:128, 0:128], sem="STG",
                      waits=as_waits(stc))
        stc = {"STG": p.snap("STG")}
    p.wait("sync", *as_waits(stc))
    nc = p.finish()
    in_names = ["xin"] + list(weight_aps.keys()) + list(core.const_aps.keys())
    return p, nc, core


BLOCK_ORDER = ["l0a", "l0f", "l1a", "l1f", "l2a", "l2f", "l3a", "l3f"]


FUSED = True


def kernel(**inputs):
    n_cores = 8
    bpc = B_FULL // n_cores
    x = np.asarray(inputs["L0_attn_x"], dtype=np.float32)
    acts = []
    for c in range(n_cores):
        xc = x[c * bpc:(c + 1) * bpc]
        acts.append(np.ascontiguousarray(xc.reshape(bpc * T_SEQ, D).T))
    groups = [BLOCK_ORDER] if FUSED else [[b] for b in BLOCK_ORDER]
    for i, blks in enumerate(groups):
        first = (i == 0)
        last = (i == len(groups) - 1)
        p, nc, core = build_chain_prog(blks, first=first, last=last)
        cdata = core.get_const_data()
        in_maps = []
        for c in range(n_cores):
            m = {"xin": acts[c]}
            for blk in blks:
                for w in BLOCK_WEIGHTS[blk]:
                    m[w] = np.asarray(inputs[w])
            m.update(cdata)
            in_maps.append(m)
        res = run_bass_kernel_spmd(nc, in_maps, core_ids=list(range(n_cores)))
        acts = [res.results[c]["xout"] for c in range(n_cores)]
    out = np.empty((B_FULL, T_SEQ, D), np.float32)
    for c in range(n_cores):
        out[c * bpc:(c + 1) * bpc] = np.asarray(acts[c], np.float32).T.reshape(
            bpc, T_SEQ, D)
    return out
